# version 12
# baseline (speedup 1.0000x reference)
import numpy as np
from contextlib import ExitStack

import concourse.bass as bass
import concourse.mybir as mybir
from concourse.bass_utils import run_bass_kernel_spmd

import ml_dtypes
BFD = ml_dtypes.bfloat16

F32 = mybir.dt.float32
BF16 = mybir.dt.bfloat16
F16 = mybir.dt.float16
AF = mybir.ActivationFunctionType
ALU = mybir.AluOpType
AX = mybir.AxisListType

B, L, D = 32, 2048, 1024
SEQ = 4
TOK = SEQ * L
DI = 2048
DS = 64
H = 16
CHD = 2176
F = 4240
CHUNK = 256
TILE = 512
NT = TOK // TILE
LN_EPS = 1e-5
RMS_EPS = 1e-5
NZT = DI // 128
NCT = CHD // 128
NFT = 34
KD = D // 128
FFI = 2816
KFI = FFI // 128
ISLS = [(0, 512), (512, 1024), (1024, 1536), (1536, 2048), (2048, 2560),
        (2560, 2816)]

QTHR = [0.25, 0.75, 1.25, 1.75, 2.5, 3.5, 5.0]
QWTS = [1.0, 1.0, 1.0, 1.0, 2.0, 2.0, 4.0]


def build_nc(dev_dump=(), maxphase=4, ntl=NT, p2stage=99):
    nc = bass.Bass("TRN2", target_bir_lowering=False)

    u_h = nc.dram_tensor("u", [TOK, D], F32, kind="ExternalInput").ap()
    win_h = nc.dram_tensor("in_proj_w", [F, D], F32, kind="ExternalInput").ap()
    convw_h = nc.dram_tensor("conv_w", [CHD, 4], F32, kind="ExternalInput").ap()
    convb_h = nc.dram_tensor("conv_b", [CHD], F32, kind="ExternalInput").ap()
    dtb_h = nc.dram_tensor("dt_bias", [H], F32, kind="ExternalInput").ap()
    alog_h = nc.dram_tensor("A_log", [H], F32, kind="ExternalInput").ap()
    dd_h = nc.dram_tensor("D_h", [H], F32, kind="ExternalInput").ap()
    xw_h = nc.dram_tensor("x_norm_w", [D], F32, kind="ExternalInput").ap()
    xb_h = nc.dram_tensor("x_norm_b", [D], F32, kind="ExternalInput").ap()
    rmsw_h = nc.dram_tensor("rms_w", [DI], F32, kind="ExternalInput").ap()
    wout_h = nc.dram_tensor("out_proj_w", [D, DI], F32, kind="ExternalInput").ap()
    wg_h = nc.dram_tensor("w_gate", [FFI, D], F32, kind="ExternalInput").ap()
    wu_h = nc.dram_tensor("w_up", [FFI, D], F32, kind="ExternalInput").ap()
    wd_h = nc.dram_tensor("w_down", [D, FFI], F32, kind="ExternalInput").ap()
    identb_h = nc.dram_tensor("ident_b", [128, 128], BF16, kind="ExternalInput").ap()
    identf_h = nc.dram_tensor("ident_f", [128, 128], F32, kind="ExternalInput").ap()
    mask0_h = nc.dram_tensor("mask_j0", [128, 256], BF16, kind="ExternalInput").ap()
    mask1_h = nc.dram_tensor("mask_j1", [128, 256], BF16, kind="ExternalInput").ap()

    out_h = nc.dram_tensor("out", [TOK, D], F32, kind="ExternalOutput").ap()

    def dscratch(name, shape, dt):
        kind = "ExternalOutput" if name in dev_dump else "Internal"
        return nc.dram_tensor(name, shape, dt, kind=kind).ap()

    xnqT_s = dscratch("s_xnqT", [NT * KD * 128, TILE], BF16)
    siluz_s = dscratch("s_siluz", [NT * NZT * 128, TILE], BF16)
    xbc_s = dscratch("s_xbc", [SEQ * NCT * 128, L + 3], BF16)
    dt_s = dscratch("s_dt", [NT * H, TILE], F32)
    part_s = dscratch("s_part", [TOK, D], F32)
    hqT_s = dscratch("s_hqT", [NT * KFI * 128, TILE], BF16)

    ctx = ExitStack()
    sbt = lambda name, shape, dt=F32: ctx.enter_context(nc.sbuf_tensor(name, shape, dt))

    SEMS = ["sy", "po", "pe", "ve", "ac", "gp"]
    sems = {s: ctx.enter_context(nc.semaphore(s)) for s in SEMS}
    cnt = {s: 0 for s in SEMS}

    def inc(instr, s, n=1):
        instr.then_inc(sems[s], n)
        cnt[s] += n
        return cnt[s]

    def w(eng, s, val=None):
        v = cnt[s] if val is None else val
        if v > 0:
            eng.wait_ge(sems[s], v)

    block = ctx.enter_context(nc.Block())

    from concourse.hw_specs import get_activation_tables
    _tabs = list(get_activation_tables(nc.m.arch).keys())
    SET_NLE = _tabs.index("natural_log_exp_and_others")
    SET_SILU = _tabs.index("silu_and_others")

    def load_act_set(sid):
        nc.scalar.add_instruction(mybir.InstLoadActFuncSet(
            name=nc.get_next_instruction_name(), act_func_set_id=sid,
            ins=[], outs=[]))

    def barrier():
        snap = dict(cnt)
        for name in ["sync", "gpsimd", "tensor", "vector", "scalar"]:
            dec = getattr(block, name)
            @dec
            def _(eng):
                for s in SEMS:
                    if snap[s]:
                        eng.wait_ge(sems[s], snap[s])

    identb = sbt("identb", [128, 128], BF16)
    identf = sbt("identf", [128, 128], F32)
    masks = [sbt("mask0", [128, 256], BF16), sbt("mask1", [128, 256], BF16)]
    ones_r128b = sbt("ones_r128b", [1, 128], BF16)
    ones_r128f = sbt("ones_r128f", [1, 128], F32)
    ones_r512b = sbt("ones_r512b", [1, TILE], BF16)
    ones_c128b = sbt("ones_c128b", [128, 1], BF16)
    convw = sbt("convw", [128, NCT * 4], F32)
    convb = sbt("convb", [128, NCT], F32)
    dtbcol = sbt("dtbcol", [16, 1], F32)
    acol = sbt("acol", [16, 1], F32)
    rmscol = sbt("rmscol", [128, H], F32)
    i_d = sbt("i_d", [128, H * 128], BF16)
    xwcol = sbt("xwcol", [128, KD], F32)
    xbcolT = sbt("xbcolT", [128, KD], F32)
    xbcolT_b = sbt("xbcolT_b", [128, KD], BF16)
    carryA = [sbt(f"carryA{s}", [128, 8 * 128], F32) for s in range(SEQ)]
    carry_b = sbt("carry_b", [128, 8 * 128], BF16)
    drow = sbt("drow", [1, H], F32)
    zero3 = sbt("zero3", [128, 4], BF16)
    epscol = sbt("epscol", [128, 1], F32)

    def cslc(h):
        hb = 0 if h < 8 else 64
        hc = (h % 8) * 128
        return hb, hc

    @block.sync
    def _(sync):
        for dst, src, slow in [
            (identb, identb_h, False), (identf, identf_h, False),
            (masks[0], mask0_h, False), (masks[1], mask1_h, False),
            (convw[:].rearrange("p (c k) -> p c k", k=4),
             convw_h.rearrange("(c p) k -> p c k", p=128), False),
            (convb, convb_h.rearrange("(c p) -> p c", p=128), True),
            (dtbcol, dtb_h.unsqueeze(1), False),
            (acol, alog_h.unsqueeze(1), False),
            (rmscol, rmsw_h.rearrange("(h p) -> p h", p=128), True),
            (xwcol, xw_h.rearrange("(k p) -> p k", p=128), True),
            (xbcolT, xb_h.rearrange("(k p) -> p k", p=128), True),
            (drow, dd_h.unsqueeze(0), False),
        ]:
            i = sync.dma_start(out=dst[:], in_=src, allow_slow_non_contiguous=slow)
            inc(i, "sy", 16)
        sync.wait_ge(sems["sy"], cnt["sy"])
    SMALL_LD = cnt["sy"]

    @block.vector
    def _(vector):
        nc.vector.memset(ones_r128b[:], 1.0)
        nc.vector.memset(ones_r128f[:], 1.0)
        nc.vector.memset(ones_r512b[:], 1.0)
        nc.vector.memset(ones_c128b[:], 1.0)
        nc.vector.memset(zero3[:], 0.0)
        nc.vector.memset(epscol[:], LN_EPS)
        for s in range(SEQ):
            nc.vector.memset(carryA[s][:], 0.0)
        nc.vector.memset(carry_b[:], 0.0)
        w(vector, "sy", SMALL_LD)
        i = nc.vector.tensor_copy(xbcolT_b[:], xbcolT[:])
        inc(i, "ve")
    CONSTS_VE = cnt["ve"]

    @block.scalar
    def _(scalar):
        w(scalar, "sy", SMALL_LD)
        load_act_set(SET_NLE)
        i = nc.scalar.activation(acol[:], acol[:], AF.Exp)
        inc(i, "ac")
    ACOL_AC = cnt["ac"]

    with nc.psum_tensor("dcols_p", [128, H], F32) as dcols_p:
        dcols = sbt("dcols", [128, H], F32)

        @block.tensor
        def _(tensor):
            w(tensor, "sy", SMALL_LD)
            w(tensor, "ve", CONSTS_VE)
            i = nc.tensor.matmul(dcols_p[:], ones_r128f[0:1, :], drow[:],
                                 start=True, stop=True)
            inc(i, "pe")

        @block.vector
        def _(vector):
            w(vector, "pe")
            w(vector, "ac", ACOL_AC)
            nc.vector.tensor_scalar_mul(acol[:], acol[:], -1.0)
            nc.vector.tensor_copy(dcols[:], dcols_p[:])
            nc.vector.drain()
            for h in range(H):
                i = nc.vector.tensor_scalar_mul(i_d[:, h * 128:(h + 1) * 128],
                                                identb[:], dcols[:, h:h + 1])
            inc(i, "ve")

    @block.gpsimd
    def _(gpsimd):
        w(gpsimd, "ve", CONSTS_VE)
        xbv = xbc_s.rearrange("(s c p) l -> s c p l", s=SEQ, c=NCT)
        for s in range(SEQ):
            for c in range(NCT):
                i = gpsimd.dma_start(out=xbv[s, c, :, 0:3], in_=zero3[:, 0:3])
                inc(i, "po", 16)
        gpsimd.wait_ge(sems["po"], cnt["po"])

    def quant_scratch(stk, tag, n, f16=False):
        sb = lambda nm, shape, dt=F32: stk.enter_context(nc.sbuf_tensor(nm, shape, dt))
        nb = n // 16
        ddt = F16 if f16 else F32
        d = {
            "amax": sb(f"{tag}_amax", [128, nb]), "scl": sb(f"{tag}_scl", [128, nb]),
            "absx": sb(f"{tag}_absx", [128, n], ddt),
            "sg": sb(f"{tag}_sg", [128, n], F16 if f16 else BF16),
            "aa": sb(f"{tag}_aa", [128, n], ddt),
            "s1": sb(f"{tag}_s1", [128, n], F16 if f16 else BF16),
            "s2": sb(f"{tag}_s2", [128, n], F16 if f16 else BF16),
            "cc": sb(f"{tag}_cc", [128, n], F16 if f16 else BF16),
            "cd": sb(f"{tag}_cd", [128, n], F16 if f16 else BF16),
            "c5": sb(f"{tag}_c5", [128, n], F16 if f16 else BF16),
            "c6": sb(f"{tag}_c6", [128, n], F16 if f16 else BF16),
            "ss": sb(f"{tag}_ss", [128, n], F16 if f16 else BF16),
            "rcp16": sb(f"{tag}_rcp16", [128, nb], F16 if f16 else F32),
            "sclh16": sb(f"{tag}_sclh16", [128, nb], F16 if f16 else F32),
        }
        return d

    def emit_quant_full(src, dst, n, q, first_wait=None):
        nb = n // 16
        assert n % 32 == 0
        hn, hb = n // 2, nb // 2
        CS = [slice(0, hn), slice(hn, n)]
        GS = [slice(0, hb), slice(hb, nb)]
        @block.scalar
        def _(scalar):
            if first_wait is not None:
                w(scalar, first_wait[0], first_wait[1])
            if q.get("prev") is not None:
                w(scalar, "ve", q["prev_absx"])
                w(scalar, "gp", q["prev_sg"])
            else:
                w(scalar, "ve")
                w(scalar, "gp")
            nc.scalar.activation(q["absx"][:, 0:n], src, AF.Abs)
            i = nc.scalar.activation(q["sg"][:, 0:n], src, AF.Sign)
            inc(i, "ac")
        AC = cnt["ac"]
        gp0 = cnt["gp"]
        gp_ss = [gp0 + 1, gp0 + 2]
        gp_ad = [gp0 + 5, gp0 + 6]
        mk = {}

        @block.vector
        def _(vector):
            def vw(v):
                if v:
                    nc.vector.wait_ge(sems["ve"], v)
            def vop(key, i, instr):
                inc(instr, "ve")
                mk[(key, i)] = cnt["ve"]
            nc.vector.wait_ge(sems["ac"], AC)
            if q.get("prev") is not None:
                vw(q["prev"])
                nc.vector.wait_ge(sems["gp"], q["prev_gp"])
            for i in (0, 1):
                vop("mx", i, nc.vector.reduce_max(
                    q["amax"][:, GS[i]],
                    src[:, CS[i]].rearrange("p (a b) -> p a b", b=16),
                    axis=AX.X, apply_absolute_value=True))
            for i in (0, 1):
                vw(mk[("mx", i)])
                vop("rc", i, nc.vector.reciprocal(q["scl"][:, GS[i]],
                                                  q["amax"][:, GS[i]]))
            for i in (0, 1):
                vw(mk[("rc", i)])
                vop("r6", i, nc.vector.tensor_scalar(
                    q["rcp16"][:, GS[i]], q["scl"][:, GS[i]], 6.0, 1e12,
                    ALU.mult, ALU.min))
                vop("sh", i, nc.vector.tensor_scalar(
                    q["sclh16"][:, GS[i]], q["amax"][:, GS[i]], 1.0 / 12.0,
                    5e-13, ALU.mult, ALU.max))
            for i in (0, 1):
                vw(mk[("r6", i)])
                vop("aa", i, nc.vector.tensor_mul(
                    q["aa"][:, CS[i]], q["absx"][:, CS[i]],
                    q["rcp16"][:, GS[i]].unsqueeze(2)
                    .broadcast_to([128, hb, 16])))
            for i in (0, 1):
                vw(mk[("aa", i)])
                vop("t0", i, nc.vector.tensor_scalar(
                    q["s1"][:, CS[i]], q["aa"][:, CS[i]], QTHR[0], 1.0,
                    ALU.is_gt, ALU.mult))
                vop("c0", i, nc.vector.tensor_scalar(
                    q["cc"][:, CS[i]], q["aa"][:, CS[i]], QTHR[1], 1.0,
                    ALU.is_gt, ALU.mult))
            for i in (0, 1):
                vw(mk[("t0", i)])
                vw(mk[("c0", i)])
                vop("t1", i, nc.vector.tensor_add(
                    q["s1"][:, CS[i]], q["s1"][:, CS[i]], q["cc"][:, CS[i]]))
                vop("c1", i, nc.vector.tensor_scalar(
                    q["cd"][:, CS[i]], q["aa"][:, CS[i]], QTHR[2], 1.0,
                    ALU.is_gt, ALU.mult))
            for i in (0, 1):
                vw(mk[("t1", i)])
                vw(mk[("c1", i)])
                vop("t2", i, nc.vector.tensor_add(
                    q["s1"][:, CS[i]], q["s1"][:, CS[i]], q["cd"][:, CS[i]]))
                vop("c2", i, nc.vector.tensor_scalar(
                    q["cc"][:, CS[i]], q["aa"][:, CS[i]], QTHR[3], 1.0,
                    ALU.is_gt, ALU.mult))
            for i in (0, 1):
                vw(mk[("t2", i)])
                vw(mk[("c2", i)])
                vop("t3", i, nc.vector.tensor_add(
                    q["s1"][:, CS[i]], q["s1"][:, CS[i]], q["cc"][:, CS[i]]))
                vop("u0", i, nc.vector.tensor_scalar(
                    q["s2"][:, CS[i]], q["aa"][:, CS[i]], QTHR[4], 2.0,
                    ALU.is_gt, ALU.mult))
                vop("u1", i, nc.vector.tensor_scalar(
                    q["c5"][:, CS[i]], q["aa"][:, CS[i]], QTHR[5], 2.0,
                    ALU.is_gt, ALU.mult))
                vop("u2", i, nc.vector.tensor_scalar(
                    q["c6"][:, CS[i]], q["aa"][:, CS[i]], QTHR[6], 4.0,
                    ALU.is_gt, ALU.mult))
            for i in (0, 1):
                vw(mk[("t3", i)])
                nc.vector.wait_ge(sems["gp"], gp_ad[i])
                vop("fin", i, nc.vector.tensor_add(
                    q["s1"][:, CS[i]], q["s1"][:, CS[i]], q["s2"][:, CS[i]]))
            for i in (0, 1):
                vw(mk[("fin", i)])
                vop("ds", i, nc.vector.tensor_mul(
                    dst[:, CS[i]], q["s1"][:, CS[i]], q["ss"][:, CS[i]]))

        @block.gpsimd
        def _(gpsimd):
            w(gpsimd, "ac", AC)
            if q.get("prev") is not None:
                w(gpsimd, "ve", q["prev"])
            for i in (0, 1):
                gpsimd.wait_ge(sems["ve"], mk[("sh", i)])
                ii = nc.gpsimd.tensor_mul(
                    q["ss"][:, CS[i]], q["sg"][:, CS[i]],
                    q["sclh16"][:, GS[i]].unsqueeze(2)
                    .broadcast_to([128, hb, 16]))
                inc(ii, "gp")
            for i in (0, 1):
                gpsimd.wait_ge(sems["ve"], mk[("u2", i)])
                ii = nc.gpsimd.tensor_add(
                    q["s2"][:, CS[i]], q["s2"][:, CS[i]], q["c5"][:, CS[i]])
                inc(ii, "gp")
            for i in (0, 1):
                gpsimd.wait_ge(sems["gp"], gp0 + 3 + i)
                ii = nc.gpsimd.tensor_add(
                    q["s2"][:, CS[i]], q["s2"][:, CS[i]], q["c6"][:, CS[i]])
                inc(ii, "gp")
        assert cnt["gp"] == gp0 + 6
        q["prev"] = cnt["ve"]
        q["prev_gp"] = cnt["gp"]
        q["prev_absx"] = mk[("aa", 1)]
        q["prev_sg"] = gp_ss[1]
        return cnt["ve"]

    def emit_transpose(blocks, bankA, bankB, ident_t, rows_in=128, pe_waits=()):
        same = bankA is bankB
        ngr = (len(blocks) + 3) // 4
        ev_done = [None, None]
        for g in range(ngr):
            bank = (bankA, bankB)[g % 2]
            cur = blocks[g * 4:(g + 1) * 4]
            @block.tensor
            def _(tensor):
                if g == 0:
                    for s_, v_ in pe_waits:
                        w(tensor, s_, v_)
                    w(tensor, "ve")
                guard = ev_done[(g - 1) % 2] if same else ev_done[g % 2]
                if guard is not None:
                    w(tensor, "ve", guard)
                for j, (srcap, dstap) in enumerate(cur):
                    i = nc.tensor.transpose(
                        bank[:, j * rows_in:(j + 1) * rows_in], srcap,
                        ident_t[0:rows_in, 0:rows_in])
                inc(i, "pe")
            PE_V = cnt["pe"]
            @block.vector
            def _(vector):
                w(vector, "pe", PE_V)
                for j, (srcap, dstap) in enumerate(cur):
                    i = nc.vector.tensor_copy(
                        dstap, bank[0:dstap.shape[0],
                                    j * rows_in: j * rows_in + dstap.shape[1]])
                inc(i, "ve")
            ev_done[g % 2] = cnt["ve"]
        return cnt["ve"]

    def emit_l2norm_prep(w_hbm, n_rows, ncols, wld, wsq, wssq, wrn, wbf,
                         bankA, bankB, dstT, scale_const):
        kk = ncols // 128
        n_tiles = (n_rows + 127) // 128
        buf_free = [None, None]
        for wt in range(n_tiles):
            rows = min(128, n_rows - wt * 128)
            buf = wld[wt % 2]
            @block.sync
            def _(sync):
                if wt < 2:
                    w(sync, "ve")
                    w(sync, "ac")
                if buf_free[wt % 2] is not None:
                    w(sync, "ve", buf_free[wt % 2])
                i = sync.dma_start(out=buf[0:rows, 0:ncols],
                                   in_=w_hbm[wt * 128: wt * 128 + rows, :])
                inc(i, "sy", 16)
                sync.wait_ge(sems["sy"], cnt["sy"])
            @block.scalar
            def _(scalar):
                w(scalar, "sy")
                w(scalar, "ve")
                nc.scalar.activation(wsq[0:rows, 0:ncols], buf[0:rows, 0:ncols],
                                     AF.Square, accum_out=wssq[0:rows, :])
                nc.scalar.drain()
                nc.scalar.activation(wrn[0:rows, :], wssq[0:rows, :], AF.Ln)
                nc.scalar.drain()
                i = nc.scalar.activation(wrn[0:rows, :], wrn[0:rows, :], AF.Exp,
                                         scale=-0.5)
                inc(i, "ac")
            @block.vector
            def _(vector):
                w(vector, "ac")
                w(vector, "pe")
                nc.vector.tensor_scalar(wrn[0:rows, :], wrn[0:rows, :], 1e6,
                                        scale_const, ALU.min, ALU.mult)
                nc.vector.drain()
                i = nc.vector.tensor_scalar_mul(wbf[0:rows, 0:ncols],
                                                buf[0:rows, 0:ncols],
                                                wrn[0:rows, 0:1])
                inc(i, "ve")
            buf_free[wt % 2] = cnt["ve"]
            blocks = [(wbf[0:rows, k * 128:(k + 1) * 128],
                       dstT[:, k * n_rows + wt * 128: k * n_rows + wt * 128 + rows])
                      for k in range(kk)]
            emit_transpose(blocks, bankA, bankB, identb, rows_in=rows,
                           pe_waits=[("ve", buf_free[wt % 2])])

    def emit_wquant_prep(w_hbm, n_rows, ncols, wld, qsc, wbf, bankA, bankB,
                         dstT, qn):
        kk = ncols // 128
        n_tiles = (n_rows + 127) // 128
        buf_free = [None, None]
        for wt in range(n_tiles):
            rows = min(128, n_rows - wt * 128)
            buf = wld[wt % 2]
            @block.sync
            def _(sync):
                if wt < 2:
                    w(sync, "ve")
                    w(sync, "ac")
                if buf_free[wt % 2] is not None:
                    w(sync, "ve", buf_free[wt % 2])
                i = sync.dma_start(out=buf[0:rows, 0:ncols],
                                   in_=w_hbm[wt * 128: wt * 128 + rows, :])
                inc(i, "sy", 16)
                sync.wait_ge(sems["sy"], cnt["sy"])
            ld_done = cnt["sy"]
            @block.vector
            def _(vector):
                w(vector, "pe")
                i = nc.vector.engine_nop()
                inc(i, "ve")
            for off in range(0, ncols, qn):
                nn = min(qn, ncols - off)
                emit_quant_full(buf[0:128, off:off + nn],
                                wbf[0:128, off:off + nn], nn, qsc,
                                first_wait=("sy", ld_done))
            buf_free[wt % 2] = cnt["ve"]
            blocks = [(wbf[0:rows, k * 128:(k + 1) * 128],
                       dstT[:, k * n_rows + wt * 128: k * n_rows + wt * 128 + rows])
                      for k in range(kk)]
            emit_transpose(blocks, bankA, bankB, identb, rows_in=rows,
                           pe_waits=[("ve", buf_free[wt % 2])])

    u_view = u_h.rearrange("(n p) d -> n p d", p=128)

    def trow(t):
        return (t % 4) * 16 + (t // 4) * 4

    with ExitStack() as stk, \
         nc.psum_tensor("ptrA1", [128, 512], BF16) as ptrA, \
         nc.psum_tensor("ptrB1", [128, 512], BF16) as ptrB, \
         nc.psum_tensor("pmm0", [128, TILE], F32) as pmm0, \
         nc.psum_tensor("pmm1", [128, TILE], F32) as pmm1, \
         nc.psum_tensor("pcr", [1, 512], F32) as pcr:
        sb1 = lambda name, shape, dt=F32: stk.enter_context(nc.sbuf_tensor(name, shape, dt))
        wnt = sb1("wnt", [128, KD * F], BF16)
        constrow = sb1("constrow", [1, F], F32)
        constrow_b = sb1("constrow_b", [1, F], BF16)
        pstk = ExitStack()
        sb1p = lambda name, shape, dt=F32: pstk.enter_context(
            nc.sbuf_tensor(name, shape, dt))
        wld = [sb1p(f"wld{i}", [128, D]) for i in range(2)]
        wsq = sb1p("wsq", [128, D])
        wssq = sb1p("wssq", [128, 1])
        wrn = sb1p("wrn", [128, 1])
        wbf = sb1p("wbf", [128, D], BF16)

        emit_l2norm_prep(win_h, F, D, wld, wsq, wssq, wrn, wbf, ptrA, ptrB,
                         wnt, D ** -0.5)

        for nsl in range(9):
            n0, n1 = nsl * 512, min(F, nsl * 512 + 512)
            @block.tensor
            def _(tensor):
                w(tensor, "ve")
                for k in range(KD):
                    i = nc.tensor.matmul(pcr[0:1, 0:n1 - n0], xbcolT_b[:, k:k + 1],
                                         wnt[:, k * F + n0: k * F + n1],
                                         start=(k == 0), stop=(k == KD - 1))
                inc(i, "pe")
            CR_PE = cnt["pe"]
            @block.vector
            def _(vector):
                w(vector, "pe", CR_PE)
                i = nc.vector.tensor_copy(constrow[0:1, n0:n1], pcr[0:1, 0:n1 - n0])
                inc(i, "ve")
        @block.vector
        def _(vector):
            w(vector, "ve")
            nc.vector.tensor_copy(constrow_b[:], constrow[:])
            for k in range(KD):
                i = nc.vector.tensor_scalar_mul(wnt[:, k * F:(k + 1) * F],
                                                wnt[:, k * F:(k + 1) * F],
                                                xwcol[:, k:k + 1])
            inc(i, "ve")
        WNT_VE = cnt["ve"]
        pstk.close()

        ubuf = sb1("ubuf", [128, 4 * D])
        xn = sb1("xn", [128, 4 * D])
        xh = sb1("xh", [128, 4 * D], BF16)
        xhT = sb1("xhT", [128, KD * TILE], BF16)
        qd = sb1("qd", [128, 1024], BF16)
        xnqT = sb1("xnqT", [128, KD * TILE], BF16)
        su = sb1("su", [128, 4])
        ssq4 = sb1("ssq4", [128, 4])
        mean4 = sb1("mean4", [128, 4])
        msq4 = sb1("msq4", [128, 4])
        var4 = sb1("var4", [128, 4])
        rstd4 = sb1("rstd4", [128, 4])
        bia4 = sb1("bia4", [128, 4])
        q1 = quant_scratch(stk, "q1", 1024, f16=True)
        stg_ev = [sb1(f"stg_ev{i}", [128, TILE], BF16) for i in range(2)]
        dtstg = sb1("dtstg", [16, TILE])

        u_done = [None] * (NT + 1)
        @block.sync
        def _(sync):
            w(sync, "ve", WNT_VE)
            i = sync.dma_start(
                out=ubuf[:].rearrange("p (n d) -> p n d", n=4),
                in_=u_view[trow(0):trow(0) + 4, :, :].rearrange("n p d -> p n d"))
            u_done[0] = inc(i, "sy", 16)
            sync.wait_ge(sems["sy"], cnt["sy"])

        def emit_ln(src_buf, dst, waits, dum1, dum2):
            @block.scalar
            def _(scalar):
                for s_, v_ in waits:
                    w(scalar, s_, v_)
                w(scalar, "ac")
                w(scalar, "ve")
                for mt in range(4):
                    sl = src_buf[:, mt * D:(mt + 1) * D]
                    nc.scalar.activation(dum1[:, mt * D:(mt + 1) * D], sl,
                                         AF.Copy, accum_out=su[:, mt:mt + 1])
                    i = nc.scalar.activation(dum2[:, mt * D:(mt + 1) * D], sl,
                                             AF.Square,
                                             accum_out=ssq4[:, mt:mt + 1])
                inc(i, "ac")
            ACC_AC = cnt["ac"]
            @block.vector
            def _(vector):
                w(vector, "ac", ACC_AC)
                w(vector, "ve")
                nc.vector.tensor_scalar_mul(mean4[:], su[:], 1.0 / D)
                nc.vector.drain()
                nc.vector.tensor_mul(msq4[:], mean4[:], mean4[:])
                nc.vector.drain()
                i = nc.vector.scalar_tensor_tensor(var4[:], ssq4[:], 1.0 / D,
                                                   msq4[:], ALU.mult, ALU.subtract)
                inc(i, "ve")
            @block.scalar
            def _(scalar):
                w(scalar, "ve")
                nc.scalar.activation(rstd4[:], var4[:], AF.Ln, bias=epscol[:, 0:1])
                nc.scalar.drain()
                i = nc.scalar.activation(rstd4[:], rstd4[:], AF.Exp, scale=-0.5)
                inc(i, "ac")
            @block.vector
            def _(vector):
                w(vector, "ac")
                w(vector, "ve")
                nc.vector.scalar_tensor_tensor(bia4[:], mean4[:], -1.0,
                                               rstd4[:], ALU.mult, ALU.mult)
                nc.vector.drain()
                for mt in range(4):
                    i = nc.vector.tensor_scalar(dst[:, mt * D:(mt + 1) * D],
                                                src_buf[:, mt * D:(mt + 1) * D],
                                                rstd4[:, mt:mt + 1],
                                                bia4[:, mt:mt + 1],
                                                ALU.mult, ALU.add)
                inc(i, "ve")
            return ("ve", cnt["ve"])

        pm_free = [None, None]
        pm_free_sem = [None, None]
        stg_free = [None] * 2
        prev_qsg_ac = None
        prev_xht_pe = None
        prev_xnqt_po = None

        for t in range(ntl):
            s_seq, cpair = t % 4, t // 4

            lnw = [("sy", u_done[t]), ("ve", WNT_VE)]
            if t > 0:
                lnw += [("ac", prev_qsg_ac), ("pe", prev_xht_pe),
                        ("po", prev_xnqt_po)]
            XN_M = emit_ln(ubuf, xn, lnw, xn, xh)

            if t + 1 < NT:
                @block.sync
                def _(sync):
                    w(sync, XN_M[0], XN_M[1])
                    i = sync.dma_start(
                        out=ubuf[:].rearrange("p (n d) -> p n d", n=4),
                        in_=u_view[trow(t + 1):trow(t + 1) + 4, :, :]
                        .rearrange("n p d -> p n d"))
                    u_done[t + 1] = inc(i, "sy", 16)
                    sync.wait_ge(sems["sy"], cnt["sy"])

            XH_M = emit_ln(xn, xh, [], xh, xnqT)

            blocks = [(xh[:, mt * D + k * 128: mt * D + (k + 1) * 128],
                       xhT[:, k * TILE + mt * 128: k * TILE + (mt + 1) * 128])
                      for mt in range(4) for k in range(KD)]
            XHT_VE = emit_transpose(blocks, ptrA, ptrB, identb,
                                    pe_waits=[XH_M])
            prev_xht_pe = cnt["pe"]
            for sl4 in range(4):
                QS_VE = emit_quant_full(xn[:, sl4 * 1024:(sl4 + 1) * 1024],
                                        qd[:], 1024, q1,
                                        first_wait=XN_M)
                if sl4 == 3:
                    prev_qsg_ac = cnt["ac"]
                mt = sl4
                blocks = [(qd[:, j * 128:(j + 1) * 128],
                           xnqT[:, j * TILE + mt * 128:
                                j * TILE + (mt + 1) * 128])
                          for j in range(8)]
                emit_transpose(blocks, ptrA, ptrB, identb,
                               pe_waits=[("ve", QS_VE)])
            XNQT_VE = cnt["ve"]

            @block.gpsimd
            def _(gpsimd):
                w(gpsimd, "ve", XNQT_VE)
                i = gpsimd.dma_start(
                    out=xnqT_s.rearrange("(t k p) n -> t p k n", t=NT, k=KD)[t],
                    in_=xnqT[:].rearrange("p (k n) -> p k n", k=KD))
                inc(i, "po", 16)
                gpsimd.wait_ge(sems["po"], cnt["po"])
            prev_xnqt_po = cnt["po"]

            xbv = xbc_s.rearrange("(s c p) l -> s c p l", s=SEQ, c=NCT)
            szv = siluz_s.rearrange("(t z p) n -> t z p n", t=NT, z=NZT)
            for ft in range(NFT):
                rows = 128 if ft < 33 else 16
                pm = (pmm0, pmm1)[ft % 2]
                @block.tensor
                def _(tensor):
                    if ft == 0:
                        w(tensor, "ve", XHT_VE)
                    if pm_free[ft % 2] is not None:
                        w(tensor, pm_free_sem[ft % 2], pm_free[ft % 2])
                    for k in range(KD):
                        nc.tensor.matmul(
                            pm[0:rows, :],
                            wnt[:, k * F + ft * 128: k * F + ft * 128 + rows],
                            xhT[:, k * TILE:(k + 1) * TILE],
                            start=(k == 0), stop=False)
                    i = nc.tensor.matmul(pm[0:rows, :],
                                         constrow_b[0:1, ft * 128: ft * 128 + rows],
                                         ones_r512b[:], start=False, stop=True)
                    inc(i, "pe")
                MM_PE = cnt["pe"]
                if ft < 33:
                    sbuf_ev = stg_ev[ft % 2]
                    @block.scalar
                    def _(scalar):
                        w(scalar, "pe", MM_PE)
                        if stg_free[ft % 2] is not None:
                            w(scalar, stg_free[ft % 2][0], stg_free[ft % 2][1])
                        i = nc.scalar.activation(sbuf_ev[:], pm[:], AF.Copy)
                        inc(i, "ac")
                    EV_AC = cnt["ac"]
                    pm_free[ft % 2], pm_free_sem[ft % 2] = EV_AC, "ac"
                    if ft < 16:
                        @block.gpsimd
                        def _(gpsimd):
                            w(gpsimd, "ac", EV_AC)
                            i = gpsimd.dma_start(out=szv[t, ft], in_=sbuf_ev[:])
                            inc(i, "po", 16)
                            gpsimd.wait_ge(sems["po"], cnt["po"])
                        stg_free[ft % 2] = ("po", cnt["po"])
                    else:
                        ct = ft - 16
                        @block.sync
                        def _(sync):
                            w(sync, "ac", EV_AC)
                            i = sync.dma_start(
                                out=xbv[s_seq, ct, :,
                                        3 + cpair * 512: 3 + (cpair + 1) * 512],
                                in_=sbuf_ev[:])
                            inc(i, "sy", 16)
                            sync.wait_ge(sems["sy"], cnt["sy"])
                        stg_free[ft % 2] = ("sy", cnt["sy"])
                else:
                    @block.vector
                    def _(vector):
                        w(vector, "pe", MM_PE)
                        w(vector, "po")
                        i = nc.vector.tensor_copy(dtstg[:], pm[0:16, :])
                        inc(i, "ve")
                    DT_VE = cnt["ve"]
                    pm_free[ft % 2], pm_free_sem[ft % 2] = DT_VE, "ve"
                    @block.gpsimd
                    def _(gpsimd):
                        w(gpsimd, "ve", DT_VE)
                        i = gpsimd.dma_start(out=dt_s[t * 16:(t + 1) * 16, :],
                                             in_=dtstg[:])
                        inc(i, "po", 16)
                        gpsimd.wait_ge(sems["po"], cnt["po"])

        @block.gpsimd
        def _(gpsimd):
            w(gpsimd, "po")

    barrier()
    if maxphase <= 1:
        ctx.close()
        return nc

    with ExitStack() as stk, \
         nc.psum_tensor("p_sc", [128, 512], F32) as p_sc, \
         nc.psum_tensor("p_seg", [128, 512], F32) as p_seg, \
         nc.psum_tensor("p_y0", [128, 512], F32) as p_y0, \
         nc.psum_tensor("p_y1", [128, 512], F32) as p_y1, \
         nc.psum_tensor("p_st", [128, 512], F32) as p_st, \
         nc.psum_tensor("p_ssq", [128, 512], F32) as p_ssq, \
         nc.psum_tensor("p_rep", [128, 512], F32) as p_rep, \
         nc.psum_tensor("ptr2", [128, 512], BF16) as ptr2:
        sb2 = lambda name, shape, dt=F32: stk.enter_context(nc.sbuf_tensor(name, shape, dt))
        wot = sb2("wot", [128, 16 * 1024], BF16)
        accA = sb2("accA", [128, 4 * TILE])
        accB = sb2("accB", [128, 4 * TILE])
        diagb = sb2("diagb", [128, 2 * 512], BF16)
        xbcw = sb2("xbcw", [128, NCT * 515], BF16)
        x_j = sb2("x_j", [128, 4 * DI], BF16)
        zld = sb2("zld", [128, NZT * TILE], BF16)
        bdec = sb2("bdec", [128, 4 * 1024], BF16)
        wssq2 = sb2("wssq2", [128, 1])
        wrn2 = sb2("wrn2", [128, 1])

        ubp = sb2("ubp", [128, DI])
        emit_l2norm_prep(wout_h, D, DI, [accA, accB], ubp, wssq2, wrn2,
                         x_j, ptr2, ptr2, wot, 1.0)
        WOT_VE = cnt["ve"]

        dtr = accA[0:16, 0:512]
        sp_t1 = accA[0:16, 512:1024]
        sp_t2 = accA[0:16, 1024:1536]
        dt_ht = accA[0:16, 1536:2048]
        ldt_t = accB[0:16, 0:512]
        dA_t = accB[0:16, 512:1024]
        cum = accB[0:16, 1024:1536]
        r1 = accB[0:16, 1536:2048]
        dd2 = sb2("dd2", [16, 1024])
        ddec = dd2[0:16, 0:512]
        dtdec = dd2[0:16, 512:1024]
        chi = sb2("chi", [16, TILE], BF16)
        cmid = sb2("cmid", [16, TILE], BF16)
        clo = sb2("clo", [16, TILE], BF16)
        ec_b = sb2("ec_b", [16, TILE], BF16)
        cdf = sb2("cdf", [16, 2])
        ldtT = sb2("ldtT", [128, 4 * 16])
        dtdecT = sb2("dtdecT", [128, 4 * 16])
        bjn = sb2("bjn", [128, 4 * 64], BF16)
        segL = sb2("segL", [6, 4096], BF16)
        segR = sb2("segR", [6, 4096], BF16)
        ecrow = sb2("ecrow", [1, 4096], BF16)
        cdrow = sb2("cdrow", [1, 16])
        cdrepS = sb2("cdrepS", [128, 16])
        cs2 = sb2("cs2", [128, 256], BF16)
        scoresM = sb2("scoresM", [128, 512], BF16)
        relb = [sb2(f"relb{i}", [128, 512], BF16) for i in range(3)]
        eb = [sb2(f"eb{i}", [128, 512], BF16) for i in range(3)]
        attb = [sb2(f"attb{i}", [128, 512], BF16) for i in range(3)]
        cecb = sb2("cecb", [128, 8 * 256], BF16)
        g2scr = [sb2(f"g2scr{i}", [128, 256], BF16) for i in range(2)]
        gbuf = xbcw[:, 0:H * TILE]
        rstd_rf = sb2("rstd_rf", [1, 512])
        rstd_rb = sb2("rstd_rb", [1, 512], BF16)
        rep_rstd = sb2("rep_rstd", [128, 512], BF16)
        ub2 = [ubp[:, 0:D], ubp[:, D:2 * D]]
        pstg = [sb2(f"pstg{i}", [128, D]) for i in range(2)]

        @block.vector
        def _(vector):
            w(vector, "ve", WOT_VE)
            nc.vector.memset(segL[:], 1.0)
            i = nc.vector.memset(segR[:], -1.0)
            inc(i, "ve")
        SEGI_VE = cnt["ve"]

        tile_res_ve = None
        tile_res_po = None
        pst_done = [None, None]
        for t in range(ntl):
            s_seq, cpair = t % 4, t // 4

            @block.sync
            def _(sync):
                if tile_res_ve is not None:
                    w(sync, "ve", tile_res_ve)
                xbv = xbc_s.rearrange("(s c p) l -> s c p l", s=SEQ, c=NCT)
                for c in range(NCT):
                    i = sync.dma_start(
                        out=xbcw[:, c * 515:(c + 1) * 515],
                        in_=xbv[s_seq, c, :, cpair * 512: cpair * 512 + 515])
                    inc(i, "sy", 16)
                i = sync.dma_start(
                    out=zld[:].rearrange("p (z n) -> p z n", z=NZT),
                    in_=siluz_s.rearrange("(t z p) n -> t p z n", t=NT, z=NZT)[t])
                inc(i, "sy", 16)
                sync.wait_ge(sems["sy"], cnt["sy"])
            LD_SY = cnt["sy"]

            conv_free = [None, None]
            diag_free = [None, None]
            for c in range(NCT):
                slot = c % 2
                pcv = (p_sc, p_seg)[slot]
                @block.vector
                def _(vector):
                    w(vector, "sy", LD_SY)
                    if c < 2:
                        w(vector, "ve")
                        w(vector, "pe")
                    if diag_free[slot] is not None:
                        w(vector, "pe", diag_free[slot])
                    for j in range(4):
                        i = nc.vector.tensor_scalar_mul(
                            diagb[:, slot * 512 + j * 128:
                                  slot * 512 + (j + 1) * 128],
                            identb[:], convw[:, c * 4 + j: c * 4 + j + 1])
                    inc(i, "ve")
                DG_VE = cnt["ve"]
                @block.tensor
                def _(tensor):
                    w(tensor, "ve", DG_VE)
                    if conv_free[slot] is not None:
                        w(tensor, "ac", conv_free[slot])
                    elif c < 2:
                        w(tensor, "ac")
                        w(tensor, "ve")
                    for j in range(4):
                        i = nc.tensor.matmul(
                            pcv[:, 0:512],
                            diagb[:, slot * 512 + j * 128:
                                  slot * 512 + (j + 1) * 128],
                            xbcw[:, c * 515 + j: c * 515 + j + 512],
                            start=(j == 0), stop=(j == 3))
                    inc(i, "pe")
                CV_PE = cnt["pe"]
                diag_free[slot] = CV_PE
                @block.scalar
                def _(scalar):
                    w(scalar, "pe", CV_PE)
                    if c == 0:
                        load_act_set(SET_SILU)
                    i = nc.scalar.activation(
                        xbcw[:, c * 515 + 3: c * 515 + 515], pcv[:, 0:512],
                        AF.Silu, bias=convb[:, c:c + 1])
                    inc(i, "ac")
                conv_free[slot] = cnt["ac"]
            @block.scalar
            def _(scalar):
                w(scalar, "sy", LD_SY)
                i = nc.scalar.activation(zld[:], zld[:], AF.Silu)
                inc(i, "ac")
            ZSIL_AC = cnt["ac"]
            CONV_AC = cnt["ac"]
            if p2stage <= 1:
                continue

            def xs(ct, tok0, nt_, rows=slice(0, 128)):
                return xbcw[rows, ct * 515 + 3 + tok0: ct * 515 + 3 + tok0 + nt_]

            @block.sync
            def _(sync):
                w(sync, "ve")
                w(sync, "ac")
                i = sync.dma_start(out=dtr, in_=dt_s[t * 16:(t + 1) * 16, :])
                inc(i, "sy", 16)
                sync.wait_ge(sems["sy"], cnt["sy"])
            DTL_SY = cnt["sy"]
            @block.scalar
            def _(scalar):
                w(scalar, "sy", DTL_SY)
                load_act_set(SET_NLE)
                nc.scalar.activation(sp_t1, dtr, AF.Abs, bias=dtbcol[:, 0:1])
                nc.scalar.drain()
                nc.scalar.activation(sp_t1, sp_t1, AF.Exp, scale=-1.0)
                nc.scalar.drain()
                nc.scalar.activation(sp_t2, sp_t1, AF.Ln, bias=1.0)
                nc.scalar.drain()
                i = nc.scalar.activation(sp_t1, dtr, AF.Relu, bias=dtbcol[:, 0:1])
                inc(i, "ac")
            @block.vector
            def _(vector):
                w(vector, "ac")
                nc.vector.tensor_add(dt_ht, sp_t1, sp_t2)
                nc.vector.drain()
                nc.vector.tensor_scalar_mul(dA_t, dt_ht, acol[:, 0:1])
                nc.vector.drain()
                for c in range(2):
                    sl = slice(c * CHUNK, (c + 1) * CHUNK)
                    nc.vector.tensor_tensor_scan(cum[:, sl], dA_t[:, sl],
                                                 dA_t[:, sl], 0.0, ALU.add,
                                                 ALU.bypass)
                nc.vector.drain()
                nc.vector.tensor_copy(chi[:], cum)
                nc.vector.drain()
                nc.vector.scalar_tensor_tensor(r1, cum, 1.0, chi[:],
                                               ALU.mult, ALU.subtract)
                nc.vector.drain()
                nc.vector.tensor_copy(cmid[:], r1)
                nc.vector.drain()
                nc.vector.scalar_tensor_tensor(r1, r1, 1.0, cmid[:],
                                               ALU.mult, ALU.subtract)
                nc.vector.drain()
                i = nc.vector.tensor_copy(clo[:], r1)
                inc(i, "ve")
            CUM_VE = cnt["ve"]
            @block.scalar
            def _(scalar):
                w(scalar, "ve", CUM_VE)
                for c in range(2):
                    sl = slice(c * CHUNK, (c + 1) * CHUNK)
                    last = slice(c * CHUNK + 255, c * CHUNK + 256)
                    nc.scalar.activation(ddec[:, c * 256:(c + 1) * 256],
                                         cum[:, sl], AF.Exp,
                                         scale=-1.0, bias=cum[:, last])
                    nc.scalar.activation(cdf[:, c:c + 1], cum[:, last], AF.Exp)
                i = nc.scalar.activation(ec_b[:], cum, AF.Exp)
                inc(i, "ac")
            DT_AC = cnt["ac"]
            @block.vector
            def _(vector):
                w(vector, "ac", DT_AC)
                i = nc.vector.tensor_mul(dtdec, dt_ht, ddec)
                inc(i, "ve")
            DTD_VE = cnt["ve"]

            blocks = [(dt_ht[0:16, tb * 128:(tb + 1) * 128],
                       ldtT[:, tb * 16:(tb + 1) * 16]) for tb in range(4)]
            emit_transpose(blocks, p_seg, p_seg, identf, rows_in=16,
                           pe_waits=[("ac", DT_AC)])
            blocks = [(dtdec[0:16, tb * 128:(tb + 1) * 128],
                       dtdecT[:, tb * 16:(tb + 1) * 16]) for tb in range(4)]
            emit_transpose(blocks, p_seg, p_seg, identf, rows_in=16,
                           pe_waits=[("ve", DTD_VE)])
            blocks = [(xs(zt, tb * 128, 128),
                       x_j[:, tb * DI + zt * 128: tb * DI + (zt + 1) * 128])
                      for tb in range(4) for zt in range(16)]
            emit_transpose(blocks, ptr2, ptr2, identb, rows_in=128,
                           pe_waits=[("ac", CONV_AC)])
            blocks = [(xs(16, tb * 128, 128, rows=slice(0, 64)),
                       bjn[:, tb * 64:(tb + 1) * 64]) for tb in range(4)]
            emit_transpose(blocks, ptr2, ptr2, identb, rows_in=64)
            TR_VE = cnt["ve"]

            @block.vector
            def _(vector):
                w(vector, "ve", TR_VE)
                for tb in range(4):
                    i = nc.vector.tensor_mul(
                        bdec[:, tb * 1024:(tb + 1) * 1024],
                        bjn[:, tb * 64:(tb + 1) * 64].unsqueeze(1)
                        .broadcast_to([128, 16, 64]),
                        dtdecT[:, tb * 16:(tb + 1) * 16].unsqueeze(2)
                        .broadcast_to([128, 16, 64]))
                inc(i, "ve")
            BDEC_VE = cnt["ve"]
            if p2stage <= 2:
                continue

            py_free = {0: None, 1: None}
            ebwar = {0: None, 1: None, 2: None}
            attbwar = {0: None, 1: None, 2: None}
            stbank_free = {0: None, 1: None}
            relseg_free = {0: None, 1: None}
            ssq_pe_prev = {0: None, 1: None}
            last_seg_pe = None
            last_cec_ve = None
            psc_free = None
            rstd_prev_ac = None
            for c in range(2):
                csl = slice(c * CHUNK, (c + 1) * CHUNK)
                @block.sync
                def _(sync):
                    w(sync, "ve", CUM_VE)
                    w(sync, "ac", DT_AC)
                    w(sync, "pe")
                    for row, srct in ((3, chi), (4, cmid), (5, clo)):
                        i = sync.dma_start(
                            out=segL[row:row + 1, :].rearrange(
                                "o (h i) -> o h i", h=16),
                            in_=srct[:, csl])
                        inc(i, "sy", 16)
                        i = sync.dma_start(
                            out=segR[row - 3:row - 2, :].rearrange(
                                "o (h i) -> o h i", h=16),
                            in_=srct[:, csl])
                        inc(i, "sy", 16)
                    i = sync.dma_start(
                        out=ecrow[:].rearrange("o (h i) -> o h i", h=16),
                        in_=ec_b[:, csl])
                    inc(i, "sy", 16)
                    i = sync.dma_start(out=cdrow[:].unsqueeze(2),
                                       in_=cdf[:, c:c + 1])
                    inc(i, "sy", 16)
                    i = sync.dma_start(out=cs2[0:64, :],
                                       in_=xs(16, c * CHUNK, CHUNK,
                                              rows=slice(64, 128)))
                    inc(i, "sy", 16)
                    i = sync.dma_start(out=cs2[64:128, :],
                                       in_=xs(16, c * CHUNK, CHUNK,
                                              rows=slice(64, 128)))
                    inc(i, "sy", 16)
                    sync.wait_ge(sems["sy"], cnt["sy"])
                STG_SY = cnt["sy"]

                @block.vector
                def _(vector):
                    w(vector, "ve")
                    i = nc.vector.tensor_copy(carry_b[:], carryA[s_seq][:])
                    inc(i, "ve")
                CB_VE = cnt["ve"]

                @block.tensor
                def _(tensor):
                    w(tensor, "ac", CONV_AC)
                    w(tensor, "sy", STG_SY)
                    if psc_free is not None:
                        w(tensor, "ve", psc_free)
                    for jt in range(2):
                        i = nc.tensor.matmul(
                            p_sc[:, jt * 256:(jt + 1) * 256],
                            xs(16, c * CHUNK + jt * 128, 128, slice(0, 64)),
                            cs2[0:64, :], start=True, stop=True)
                    inc(i, "pe")
                SCO_PE = cnt["pe"]
                @block.vector
                def _(vector):
                    w(vector, "pe", SCO_PE)
                    for jt in range(2):
                        i = nc.vector.tensor_mul(
                            scoresM[:, jt * 256:(jt + 1) * 256],
                            p_sc[:, jt * 256:(jt + 1) * 256], masks[jt][:])
                    inc(i, "ve")
                SCOM_VE = cnt["ve"]
                psc_free = SCOM_VE

                @block.tensor
                def _(tensor):
                    w(tensor, "sy", STG_SY)
                    w(tensor, "ve")
                    i = nc.tensor.matmul(p_rep[:, 0:16],
                                         ones_r128f[0:1, :], cdrow[:],
                                         start=True, stop=True)
                    inc(i, "pe")
                CDR_PE = cnt["pe"]
                @block.vector
                def _(vector):
                    w(vector, "pe", CDR_PE)
                    i = nc.vector.tensor_copy(cdrepS[:], p_rep[:, 0:16])
                    inc(i, "ve")
                CDREP_VE = cnt["ve"]

                for b4 in range(8):
                    rb = 0 if b4 < 4 else 64
                    @block.tensor
                    def _(tensor):
                        w(tensor, "ve")
                        for j in range(2):
                            hh = b4 * 2 + j
                            i = nc.tensor.matmul(
                                p_rep[rb:rb + 64, j * 256:(j + 1) * 256],
                                ones_r128b[0:1, 0:64],
                                ecrow[0:1, hh * 256:(hh + 1) * 256],
                                start=True, stop=True)
                        inc(i, "pe")
                    REP_PE = cnt["pe"]
                    @block.vector
                    def _(vector):
                        w(vector, "pe", REP_PE)
                        i = nc.vector.tensor_mul(
                            cecb[rb:rb + 64, (b4 % 4) * 512:(b4 % 4 + 1) * 512],
                            p_rep[rb:rb + 64, 0:512],
                            cs2[rb:rb + 64, :].unsqueeze(1)
                            .broadcast_to([64, 2, 256]))
                        inc(i, "ve")
                CEC_VE = cnt["ve"]
                if p2stage <= 3:
                    continue

                seg_pe = {}
                g2_ac = {}
                ssq_pe = {}

                def emit_seg(hh):
                    pcs_ = (p_seg, p_sc)[hh % 2]
                    @block.tensor
                    def _(tensor):
                        if relseg_free[hh % 2] is not None:
                            w(tensor, "ac", relseg_free[hh % 2])
                        elif hh % 2 == 0:
                            w(tensor, "ve", BDEC_VE)
                        else:
                            w(tensor, "ve", psc_free if psc_free else SCOM_VE)
                        w(tensor, "sy", STG_SY)
                        for jt in range(2):
                            i = nc.tensor.matmul(
                                pcs_[:, jt * 256:(jt + 1) * 256],
                                segL[0:6, hh * 256 + jt * 128:
                                     hh * 256 + (jt + 1) * 128],
                                segR[0:6, hh * 256:(hh + 1) * 256],
                                start=True, stop=True)
                        inc(i, "pe")
                    seg_pe[hh] = cnt["pe"]

                def emit_ssq(hh):
                    @block.tensor
                    def _(tensor):
                        w(tensor, "gp", g2_ac[hh])
                        if c == 1 and hh == 0:
                            w(tensor, "ac", rstd_prev_ac)
                        i = nc.tensor.matmul(
                            p_ssq[0:1, c * 256:(c + 1) * 256],
                            ones_c128b[:], g2scr[hh % 2][:],
                            start=(hh == 0), stop=(hh == H - 1))
                        inc(i, "pe")
                    ssq_pe[hh] = cnt["pe"]

                att_ve = {}

                def emit_actatt(hh):
                    pcs_ = (p_seg, p_sc)[hh % 2]
                    SEG_PE_ = seg_pe[hh]
                    EBW = ebwar[hh % 3]
                    @block.scalar
                    def _(scalar):
                        w(scalar, "pe", SEG_PE_)
                        if EBW is not None:
                            w(scalar, "ve", EBW)
                        i = nc.scalar.activation(
                            relb[hh % 3][:], pcs_[:], AF.Relu, scale=-1.0)
                        inc(i, "ac")
                        scalar.drain()
                        i = nc.scalar.activation(
                            eb[hh % 3][:], relb[hh % 3][:], AF.Exp, scale=-1.0)
                        inc(i, "ac")
                    E_AC_ = cnt["ac"]
                    relseg_free[hh % 2] = E_AC_ - 1
                    ATTBW = attbwar[hh % 3]
                    @block.vector
                    def _(vector):
                        w(vector, "ac", E_AC_)
                        w(vector, "ve", SCOM_VE)
                        if ATTBW is not None:
                            w(vector, "pe", ATTBW)
                        for jt in range(2):
                            tb = c * 2 + jt
                            i = nc.vector.scalar_tensor_tensor(
                                attb[hh % 3][:, jt * 256:(jt + 1) * 256],
                                eb[hh % 3][:, jt * 256:(jt + 1) * 256],
                                ldtT[:, tb * 16 + hh: tb * 16 + hh + 1],
                                scoresM[:, jt * 256:(jt + 1) * 256],
                                ALU.mult, ALU.mult)
                        inc(i, "ve")
                    ebwar[hh % 3] = cnt["ve"]
                    att_ve[hh] = cnt["ve"]

                emit_seg(0)
                emit_seg(1)
                emit_actatt(0)
                for h in range(H):
                    tb0, tb1 = c * 2, c * 2 + 1
                    hb, hc = cslc(h)
                    pyb = (p_y0, p_y1)[h % 2]
                    sbank = (p_st, p_rep)[h % 2]
                    ATT_VE = att_ve[h]

                    @block.tensor
                    def _(tensor):
                        w(tensor, "ve", ATT_VE)
                        w(tensor, "ve", CEC_VE)
                        w(tensor, "ve", CB_VE)
                        w(tensor, "ve", BDEC_VE)
                        if py_free[h % 2] is not None:
                            w(tensor, "ve", py_free[h % 2])
                        nc.tensor.matmul(
                            pyb[:, 0:256],
                            x_j[:, tb0 * DI + h * 128: tb0 * DI + (h + 1) * 128],
                            attb[h % 3][:, 0:256], start=True, stop=False)
                        nc.tensor.matmul(
                            pyb[:, 0:256],
                            x_j[:, tb1 * DI + h * 128: tb1 * DI + (h + 1) * 128],
                            attb[h % 3][:, 256:512], start=False, stop=False)
                        nc.tensor.matmul(
                            pyb[:, 0:128],
                            x_j[:, tb0 * DI + h * 128: tb0 * DI + (h + 1) * 128],
                            i_d[:, h * 128:(h + 1) * 128], start=False, stop=False)
                        nc.tensor.matmul(
                            pyb[:, 128:256],
                            x_j[:, tb1 * DI + h * 128: tb1 * DI + (h + 1) * 128],
                            i_d[:, h * 128:(h + 1) * 128], start=False, stop=False)
                        i = nc.tensor.matmul(
                            pyb[:, 0:256],
                            carry_b[hb:hb + 64, hc:hc + 128],
                            cecb[hb:hb + 64, (h % 8) * 256:(h % 8 + 1) * 256],
                            start=False, stop=True)
                        inc(i, "pe")
                        if stbank_free[h % 2] is not None:
                            w(tensor, "ve", stbank_free[h % 2])
                        nc.tensor.matmul(
                            p_st[hb:hb + 64, 0:128] if h % 2 == 0 else
                            p_rep[hb:hb + 64, 0:128],
                            bdec[:, tb0 * 1024 + h * 64: tb0 * 1024 + (h + 1) * 64],
                            x_j[:, tb0 * DI + h * 128: tb0 * DI + (h + 1) * 128],
                            start=True, stop=False)
                        i = nc.tensor.matmul(
                            p_st[hb:hb + 64, 0:128] if h % 2 == 0 else
                            p_rep[hb:hb + 64, 0:128],
                            bdec[:, tb1 * 1024 + h * 64: tb1 * 1024 + (h + 1) * 64],
                            x_j[:, tb1 * DI + h * 128: tb1 * DI + (h + 1) * 128],
                            start=False, stop=True)
                        inc(i, "pe")
                    ST_PE = cnt["pe"]
                    attbwar[h % 3] = ST_PE - 1

                    if h + 2 < H:
                        emit_seg(h + 2)
                    if h + 1 < H:
                        emit_actatt(h + 1)
                    if h >= 1:
                        emit_ssq(h - 1)

                    @block.vector
                    def _(vector):
                        w(vector, "pe", ST_PE)
                        w(vector, "ve", CDREP_VE)
                        nc.vector.scalar_tensor_tensor(
                            carryA[s_seq][hb:hb + 64, hc:hc + 128],
                            carryA[s_seq][hb:hb + 64, hc:hc + 128],
                            cdrepS[hb:hb + 64, h:h + 1],
                            sbank[hb:hb + 64, 0:128],
                            ALU.mult, ALU.add)
                        if h == 0 and c == 0:
                            w(vector, "ac", ZSIL_AC)
                        i = nc.vector.tensor_mul(
                            gbuf[:, h * TILE + c * 256: h * TILE + (c + 1) * 256],
                            pyb[:, 0:256],
                            zld[:, h * TILE + c * 256: h * TILE + (c + 1) * 256])
                        inc(i, "ve")
                    GE_VE = cnt["ve"]
                    stbank_free[h % 2] = GE_VE
                    py_free[h % 2] = GE_VE

                    @block.gpsimd
                    def _(gpsimd):
                        w(gpsimd, "ve", GE_VE)
                        if ssq_pe.get(h - 2) is not None:
                            w(gpsimd, "pe", ssq_pe[h - 2])
                        i = nc.gpsimd.tensor_mul(
                            g2scr[h % 2][:],
                            gbuf[:, h * TILE + c * 256: h * TILE + (c + 1) * 256],
                            gbuf[:, h * TILE + c * 256: h * TILE + (c + 1) * 256])
                        inc(i, "gp")
                    g2_ac[h] = cnt["gp"]
                emit_ssq(H - 1)
                SSQ_PE = cnt["pe"]
                if p2stage <= 4:
                    continue

                @block.scalar
                def _(scalar):
                    w(scalar, "pe", SSQ_PE)
                    nc.scalar.activation(rstd_rf[0:1, csl], p_ssq[0:1, csl],
                                         AF.Ln, scale=1.0 / DI, bias=epscol[0:1, 0:1])
                    nc.scalar.drain()
                    i = nc.scalar.activation(rstd_rf[0:1, csl], rstd_rf[0:1, csl],
                                             AF.Exp, scale=-0.5)
                    inc(i, "ac")
                RSTD_AC = cnt["ac"]
                rstd_prev_ac = RSTD_AC
                @block.vector
                def _(vector):
                    w(vector, "ac", RSTD_AC)
                    i = nc.vector.tensor_copy(rstd_rb[0:1, csl], rstd_rf[0:1, csl])
                    inc(i, "ve")
                RSTDB_VE = cnt["ve"]
                @block.tensor
                def _(tensor):
                    w(tensor, "ve", RSTDB_VE)
                    w(tensor, "ve", psc_free)
                    i = nc.tensor.matmul(p_sc[:, 0:256], ones_r128b[0:1, :],
                                         rstd_rb[0:1, csl], start=True, stop=True)
                    inc(i, "pe")
                RR_PE = cnt["pe"]
                @block.vector
                def _(vector):
                    w(vector, "pe", RR_PE)
                    i = nc.vector.tensor_copy(rep_rstd[:, csl], p_sc[:, 0:256])
                    inc(i, "ve")
                psc_free = cnt["ve"]

            if p2stage <= 5:
                continue
            @block.vector
            def _(vector):
                w(vector, "ve")
                for h in range(H):
                    i = nc.vector.scalar_tensor_tensor(
                        gbuf[:, h * TILE:(h + 1) * TILE],
                        gbuf[:, h * TILE:(h + 1) * TILE],
                        rmscol[:, h:h + 1], rep_rstd[:],
                        ALU.mult, ALU.mult)
                inc(i, "ve")
            GQ_VE = cnt["ve"]

            pop_free = {0: psc_free, 1: None}
            pop_sem = {0: "ve", 1: "ac"}
            for mt in range(4):
                @block.sync
                def _(sync):
                    if pst_done[mt % 2] is not None:
                        w(sync, "po", pst_done[mt % 2][1])
                    i = sync.dma_start(out=ub2[mt % 2],
                                       in_=u_view[trow(t) + mt])
                    inc(i, "sy", 16)
                    sync.wait_ge(sems["sy"], cnt["sy"])
                UB_SY = cnt["sy"]
                for ns in range(2):
                    pb = (p_sc, p_seg)[ns]
                    @block.tensor
                    def _(tensor):
                        w(tensor, "ve", GQ_VE)
                        if pop_free[ns] is not None:
                            if pop_sem[ns] == "ac":
                                w(tensor, "ac", pop_free[ns])
                            else:
                                w(tensor, "ve", pop_free[ns])
                        elif ns == 1:
                            w(tensor, "ac")
                        for h in range(H):
                            i = nc.tensor.matmul(
                                pb[:, 0:512],
                                gbuf[:, h * TILE + mt * 128:
                                     h * TILE + (mt + 1) * 128],
                                wot[:, h * 1024 + ns * 512:
                                    h * 1024 + (ns + 1) * 512],
                                start=(h == 0), stop=(h == H - 1))
                        inc(i, "pe")
                    OP_PE = cnt["pe"]
                    @block.vector
                    def _(vector):
                        w(vector, "pe", OP_PE)
                        w(vector, "sy", UB_SY)
                        if ns == 0 and pst_done[mt % 2] is not None:
                            w(vector, "po", pst_done[mt % 2][1])
                        i = nc.vector.tensor_add(
                            pstg[mt % 2][:, ns * 512:(ns + 1) * 512],
                            pb[:, 0:512],
                            ub2[mt % 2][:, ns * 512:(ns + 1) * 512])
                        inc(i, "ve")
                    pop_free[ns] = cnt["ve"]
                    pop_sem[ns] = "ve"
                PS_VE = cnt["ve"]
                @block.gpsimd
                def _(gpsimd):
                    w(gpsimd, "ve", PS_VE)
                    i = gpsimd.dma_start(
                        out=part_s.rearrange("(n p) d -> n p d", p=128)[trow(t) + mt],
                        in_=pstg[mt % 2][:])
                    inc(i, "po", 16)
                    gpsimd.wait_ge(sems["po"], cnt["po"])
                pst_done[mt % 2] = (PS_VE, cnt["po"])
            tile_res_ve = cnt["ve"]

        @block.gpsimd
        def _(gpsimd):
            w(gpsimd, "po")

    barrier()
    if maxphase <= 2:
        ctx.close()
        return nc

    with ExitStack() as stk, \
         nc.psum_tensor("p_g", [128, 512], F32) as p_g, \
         nc.psum_tensor("p_u", [128, 512], F32) as p_u, \
         nc.psum_tensor("ptrA3", [128, 512], BF16) as ptrA3, \
         nc.psum_tensor("ptrB3", [128, 512], BF16) as ptrB3:
        sb3 = lambda name, shape, dt=F32: stk.enter_context(nc.sbuf_tensor(name, shape, dt))
        wgt = sb3("wgt", [128, KD * FFI], BF16)
        wut = sb3("wut", [128, KD * FFI], BF16)
        pstk3 = ExitStack()
        sb3p = lambda name, shape, dt=F32: pstk3.enter_context(
            nc.sbuf_tensor(name, shape, dt))
        wld3 = [sb3p(f"wld3{i}", [128, D]) for i in range(2)]
        wbf3 = sb3p("wbf3", [128, D], BF16)
        qw3 = quant_scratch(pstk3, "qw3", 1024)

        @block.scalar
        def _(scalar):
            load_act_set(SET_SILU)
        emit_wquant_prep(wg_h, FFI, D, wld3, qw3, wbf3, ptrA3, ptrB3, wgt, 1024)
        emit_wquant_prep(wu_h, FFI, D, wld3, qw3, wbf3, ptrA3, ptrB3, wut, 1024)
        WQ_VE = cnt["ve"]
        pstk3.close()

        q3 = quant_scratch(stk, "q3", 1024, f16=True)
        xq = [sb3(f"xq{i}", [128, KD * TILE], BF16) for i in range(2)]
        gs = sb3("gs", [128, 512], BF16)
        hb = sb3("hb", [128, FFI])
        hqb = sb3("hqb", [128, FFI], BF16)
        hqTb = sb3("hqTb", [128, KFI * TILE], BF16)

        xq_done = [None, None]
        for t0 in range(2):
            @block.sync
            def _(sync):
                w(sync, "ve", WQ_VE)
                i = sync.dma_start(
                    out=xq[t0][:].rearrange("p (k n) -> p k n", k=KD),
                    in_=xnqT_s.rearrange("(t k p) n -> t p k n", t=NT, k=KD)[t0])
                xq_done[t0] = inc(i, "sy", 16)
                sync.wait_ge(sems["sy"], cnt["sy"])

        hqT_done = None
        pg_free = None
        pu_free = None
        for t in range(min(ntl, NT)):
            xqb = xq[t % 2]
            for mt in range(4):
                for ii, (i0, i1) in enumerate(ISLS):
                    nn = i1 - i0
                    @block.tensor
                    def _(tensor):
                        if t == 0 and mt == 0 and ii == 0:
                            w(tensor, "ve", WQ_VE)
                            w(tensor, "ac")
                        w(tensor, "sy", xq_done[t % 2])
                        if pg_free is not None:
                            w(tensor, "ac", pg_free)
                        for k in range(KD):
                            i = nc.tensor.matmul(
                                p_g[:, 0:nn],
                                xqb[:, k * TILE + mt * 128: k * TILE + (mt + 1) * 128],
                                wgt[:, k * FFI + i0: k * FFI + i1],
                                start=(k == 0), stop=(k == KD - 1))
                        inc(i, "pe")
                        if pu_free is not None:
                            w(tensor, "ve", pu_free)
                        for k in range(KD):
                            i = nc.tensor.matmul(
                                p_u[:, 0:nn],
                                xqb[:, k * TILE + mt * 128: k * TILE + (mt + 1) * 128],
                                wut[:, k * FFI + i0: k * FFI + i1],
                                start=(k == 0), stop=(k == KD - 1))
                        inc(i, "pe")
                    MM_PE = cnt["pe"]
                    @block.scalar
                    def _(scalar):
                        w(scalar, "pe", MM_PE - 1)
                        if pu_free is not None:
                            w(scalar, "ve", pu_free)
                        i = nc.scalar.activation(gs[:, 0:nn], p_g[:, 0:nn], AF.Silu)
                        inc(i, "ac")
                    GS_AC = cnt["ac"]
                    pg_free = GS_AC
                    @block.vector
                    def _(vector):
                        w(vector, "pe", MM_PE)
                        w(vector, "ac", GS_AC)
                        i = nc.vector.tensor_mul(hb[:, i0:i1], p_u[:, 0:nn],
                                                 gs[:, 0:nn])
                        inc(i, "ve")
                    HB_VE = cnt["ve"]
                    pu_free = HB_VE
                    if ii % 2 == 1:
                        lo = 1024 * (ii // 2)
                        emit_quant_full(hb[:, lo:i1], hqb[:, lo:i1], i1 - lo,
                                        q3, first_wait=("ve", HB_VE))
                HQ_VE = cnt["ve"]

                pe_extra = [("ve", HQ_VE)]
                if hqT_done is not None and mt == 0:
                    pe_extra.append(("po", hqT_done))
                blocks = [(hqb[:, k * 128:(k + 1) * 128],
                           hqTb[:, k * TILE + mt * 128: k * TILE + (mt + 1) * 128])
                          for k in range(KFI)]
                emit_transpose(blocks, ptrA3, ptrB3, identb, pe_waits=pe_extra)
            HQT_VE = cnt["ve"]

            if t + 2 < NT:
                @block.sync
                def _(sync):
                    w(sync, "pe")
                    i = sync.dma_start(
                        out=xq[t % 2][:].rearrange("p (k n) -> p k n", k=KD),
                        in_=xnqT_s.rearrange("(t k p) n -> t p k n",
                                             t=NT, k=KD)[t + 2])
                    xq_done[t % 2] = inc(i, "sy", 16)
                    sync.wait_ge(sems["sy"], cnt["sy"])

            @block.gpsimd
            def _(gpsimd):
                w(gpsimd, "ve", HQT_VE)
                i = gpsimd.dma_start(
                    out=hqT_s.rearrange("(t k p) n -> t p k n", t=NT, k=KFI)[t],
                    in_=hqTb[:].rearrange("p (k n) -> p k n", k=KFI))
                inc(i, "po", 16)
                gpsimd.wait_ge(sems["po"], cnt["po"])
            hqT_done = cnt["po"]

    barrier()
    if maxphase <= 3:
        ctx.close()
        return nc

    with ExitStack() as stk, \
         nc.psum_tensor("p_d0", [128, 512], F32) as p_d0, \
         nc.psum_tensor("p_d1", [128, 512], F32) as p_d1, \
         nc.psum_tensor("ptrA4", [128, 512], BF16) as ptrA4, \
         nc.psum_tensor("ptrB4", [128, 512], BF16) as ptrB4:
        sb4 = lambda name, shape, dt=F32: stk.enter_context(nc.sbuf_tensor(name, shape, dt))
        wdt = sb4("wdt", [128, KFI * D], BF16)
        wld4 = [sb4(f"wld4{i}", [128, FFI]) for i in range(2)]
        wbf4 = sb4("wbf4", [128, FFI], BF16)
        q4 = quant_scratch(stk, "q4", 1024)

        @block.scalar
        def _(scalar):
            load_act_set(SET_SILU)
        emit_wquant_prep(wd_h, D, FFI, wld4, q4, wbf4, ptrA4, ptrB4, wdt, 1024)
        WD_VE = cnt["ve"]

        hqt = [sb4(f"hqt{i}", [128, KFI * TILE], BF16) for i in range(2)]
        pld = sb4("pld", [128, 4 * D])
        ostg = sb4("ostg", [128, 4 * D])

        hq_done = [None, None]
        for t0 in range(2):
            @block.sync
            def _(sync):
                i = sync.dma_start(
                    out=hqt[t0][:].rearrange("p (k n) -> p k n", k=KFI),
                    in_=hqT_s.rearrange("(t k p) n -> t p k n", t=NT, k=KFI)[t0])
                hq_done[t0] = inc(i, "sy", 16)
                sync.wait_ge(sems["sy"], cnt["sy"])

        ost_po = None
        pd_free = [None, None]
        for t in range(min(ntl, NT)):
            hq_b = hqt[t % 2]
            @block.sync
            def _(sync):
                w(sync, "ve")
                i = sync.dma_start(
                    out=pld[:].rearrange("p (n d) -> p n d", n=4),
                    in_=part_s.rearrange("(n p) d -> n p d", p=128)
                    [trow(t):trow(t) + 4].rearrange("n p d -> p n d"))
                inc(i, "sy", 16)
                sync.wait_ge(sems["sy"], cnt["sy"])
            PLD_SY = cnt["sy"]

            for mt in range(4):
                for ns in range(2):
                    pb = (p_d0, p_d1)[ns]
                    @block.tensor
                    def _(tensor):
                        if t == 0 and mt == 0 and ns == 0:
                            w(tensor, "ve", WD_VE)
                        w(tensor, "sy", hq_done[t % 2])
                        if pd_free[ns] is not None:
                            w(tensor, "ve", pd_free[ns])
                        for k in range(KFI):
                            i = nc.tensor.matmul(
                                pb[:, 0:512],
                                hq_b[:, k * TILE + mt * 128:
                                     k * TILE + (mt + 1) * 128],
                                wdt[:, k * D + ns * 512: k * D + (ns + 1) * 512],
                                start=(k == 0), stop=(k == KFI - 1))
                        inc(i, "pe")
                    DN_PE = cnt["pe"]
                    @block.vector
                    def _(vector):
                        w(vector, "pe", DN_PE)
                        w(vector, "sy", PLD_SY)
                        if ost_po is not None and mt == 0 and ns == 0:
                            w(vector, "po", ost_po)
                        i = nc.vector.tensor_add(
                            ostg[:, mt * D + ns * 512: mt * D + (ns + 1) * 512],
                            pb[:, 0:512],
                            pld[:, mt * D + ns * 512: mt * D + (ns + 1) * 512])
                        inc(i, "ve")
                    pd_free[ns] = cnt["ve"]
            OST_VE = cnt["ve"]

            if t + 2 < NT:
                @block.sync
                def _(sync):
                    w(sync, "pe")
                    i = sync.dma_start(
                        out=hqt[t % 2][:].rearrange("p (k n) -> p k n", k=KFI),
                        in_=hqT_s.rearrange("(t k p) n -> t p k n",
                                            t=NT, k=KFI)[t + 2])
                    hq_done[t % 2] = inc(i, "sy", 16)
                    sync.wait_ge(sems["sy"], cnt["sy"])

            @block.gpsimd
            def _(gpsimd):
                w(gpsimd, "ve", OST_VE)
                i = gpsimd.dma_start(
                    out=out_h.rearrange("(n p) d -> n p d", p=128)
                    [trow(t):trow(t) + 4].rearrange("n p d -> p n d"),
                    in_=ostg[:].rearrange("p (n d) -> p n d", n=4))
                inc(i, "po", 16)
                gpsimd.wait_ge(sems["po"], cnt["po"])
            ost_po = cnt["po"]

        @block.gpsimd
        def _(gpsimd):
            w(gpsimd, "po")

    ctx.close()
    return nc


_HOST_CONSTS = None


def _host_consts():
    global _HOST_CONSTS
    if _HOST_CONSTS is None:
        identb = np.eye(128, dtype=BFD)
        identf = np.eye(128, dtype=np.float32)
        jj = np.arange(128)[:, None]
        ii = np.arange(256)[None, :]
        m0 = (ii >= jj).astype(BFD)
        m1 = (ii >= jj + 128).astype(BFD)
        _HOST_CONSTS = (identb, identf, m0, m1)
    return _HOST_CONSTS


def make_in_maps(inputs):
    identb, identf, m0, m1 = _host_consts()
    base = {
        "in_proj_w": np.ascontiguousarray(inputs["L0_attn_in_proj_w"]),
        "conv_w": np.ascontiguousarray(inputs["L0_attn_conv_w"]),
        "conv_b": np.ascontiguousarray(inputs["L0_attn_conv_b"]),
        "dt_bias": np.ascontiguousarray(inputs["L0_attn_dt_bias"]),
        "A_log": np.ascontiguousarray(inputs["L0_attn_A_log"]),
        "D_h": np.ascontiguousarray(inputs["L0_attn_D"]),
        "x_norm_w": np.ascontiguousarray(inputs["L0_attn_x_norm_w"]),
        "x_norm_b": np.ascontiguousarray(inputs["L0_attn_x_norm_b"]),
        "rms_w": np.ascontiguousarray(inputs["L0_attn_rms_w"]),
        "out_proj_w": np.ascontiguousarray(inputs["L0_attn_out_proj_w"]),
        "w_gate": np.ascontiguousarray(inputs["L0_ffn_w_gate"]),
        "w_up": np.ascontiguousarray(inputs["L0_ffn_w_up"]),
        "w_down": np.ascontiguousarray(inputs["L0_ffn_w_down"]),
        "ident_b": identb, "ident_f": identf,
        "mask_j0": m0, "mask_j1": m1,
    }
    in_maps = []
    u = inputs["L0_attn_u"]
    for c in range(8):
        m = dict(base)
        m["u"] = np.ascontiguousarray(u[c * SEQ:(c + 1) * SEQ].reshape(TOK, D))
        in_maps.append(m)
    return in_maps


def kernel(**inputs):
    nc = build_nc()
    in_maps = make_in_maps(inputs)
    res = run_bass_kernel_spmd(nc, in_maps, core_ids=list(range(8)))
    out = np.empty((B, L, D), dtype=np.float32)
    for c in range(8):
        out[c * SEQ:(c + 1) * SEQ] = res.results[c]["out"].reshape(SEQ, L, D)
    return out


# version 13
# speedup vs baseline: 1.0765x; 1.0765x over previous
import numpy as np
from contextlib import ExitStack

import concourse.bass as bass
import concourse.mybir as mybir
from concourse.bass_utils import run_bass_kernel_spmd

import ml_dtypes
BFD = ml_dtypes.bfloat16

F32 = mybir.dt.float32
BF16 = mybir.dt.bfloat16
F16 = mybir.dt.float16
AF = mybir.ActivationFunctionType
ALU = mybir.AluOpType
AX = mybir.AxisListType

B, L, D = 32, 2048, 1024
SEQ = 4
TOK = SEQ * L
DI = 2048
DS = 64
H = 16
CHD = 2176
F = 4240
CHUNK = 256
TILE = 512
NT = TOK // TILE
LN_EPS = 1e-5
RMS_EPS = 1e-5
NZT = DI // 128
NCT = CHD // 128
NFT = 34
KD = D // 128
FFI = 2816
KFI = FFI // 128
ISLS = [(0, 512), (512, 1024), (1024, 1536), (1536, 2048), (2048, 2560),
        (2560, 2816)]

QTHR = [0.25, 0.75, 1.25, 1.75, 2.5, 3.5, 5.0]
QWTS = [1.0, 1.0, 1.0, 1.0, 2.0, 2.0, 4.0]


def build_nc(dev_dump=(), maxphase=4, ntl=NT, p2stage=99):
    nc = bass.Bass("TRN2", target_bir_lowering=False)

    u_h = nc.dram_tensor("u", [TOK, D], F32, kind="ExternalInput").ap()
    win_h = nc.dram_tensor("in_proj_w", [F, D], F32, kind="ExternalInput").ap()
    convw_h = nc.dram_tensor("conv_w", [CHD, 4], F32, kind="ExternalInput").ap()
    convb_h = nc.dram_tensor("conv_b", [CHD], F32, kind="ExternalInput").ap()
    dtb_h = nc.dram_tensor("dt_bias", [H], F32, kind="ExternalInput").ap()
    alog_h = nc.dram_tensor("A_log", [H], F32, kind="ExternalInput").ap()
    dd_h = nc.dram_tensor("D_h", [H], F32, kind="ExternalInput").ap()
    xw_h = nc.dram_tensor("x_norm_w", [D], F32, kind="ExternalInput").ap()
    xb_h = nc.dram_tensor("x_norm_b", [D], F32, kind="ExternalInput").ap()
    rmsw_h = nc.dram_tensor("rms_w", [DI], F32, kind="ExternalInput").ap()
    wout_h = nc.dram_tensor("out_proj_w", [D, DI], F32, kind="ExternalInput").ap()
    wg_h = nc.dram_tensor("w_gate", [FFI, D], F32, kind="ExternalInput").ap()
    wu_h = nc.dram_tensor("w_up", [FFI, D], F32, kind="ExternalInput").ap()
    wd_h = nc.dram_tensor("w_down", [D, FFI], F32, kind="ExternalInput").ap()
    identb_h = nc.dram_tensor("ident_b", [128, 128], BF16, kind="ExternalInput").ap()
    identf_h = nc.dram_tensor("ident_f", [128, 128], F32, kind="ExternalInput").ap()
    mask0_h = nc.dram_tensor("mask_j0", [128, 256], BF16, kind="ExternalInput").ap()
    mask1_h = nc.dram_tensor("mask_j1", [128, 256], BF16, kind="ExternalInput").ap()

    out_h = nc.dram_tensor("out", [TOK, D], F32, kind="ExternalOutput").ap()

    def dscratch(name, shape, dt):
        kind = "ExternalOutput" if name in dev_dump else "Internal"
        return nc.dram_tensor(name, shape, dt, kind=kind).ap()

    xnqT_s = dscratch("s_xnqT", [NT * KD * 128, TILE], BF16)
    siluz_s = dscratch("s_siluz", [NT * NZT * 128, TILE], BF16)
    xbc_s = dscratch("s_xbc", [SEQ * NCT * 128, L + 3], BF16)
    dt_s = dscratch("s_dt", [NT * H, TILE], F32)
    part_s = dscratch("s_part", [TOK, D], F32)
    hqT_s = dscratch("s_hqT", [NT * KFI * 128, TILE], BF16)

    ctx = ExitStack()
    sbt = lambda name, shape, dt=F32: ctx.enter_context(nc.sbuf_tensor(name, shape, dt))

    SEMS = ["sy", "po", "pe", "ve", "ac", "gp"]
    sems = {s: ctx.enter_context(nc.semaphore(s)) for s in SEMS}
    cnt = {s: 0 for s in SEMS}

    def inc(instr, s, n=1):
        instr.then_inc(sems[s], n)
        cnt[s] += n
        return cnt[s]

    def w(eng, s, val=None):
        v = cnt[s] if val is None else val
        if v > 0:
            eng.wait_ge(sems[s], v)

    block = ctx.enter_context(nc.Block())

    from concourse.hw_specs import get_activation_tables
    _tabs = list(get_activation_tables(nc.m.arch).keys())
    SET_NLE = _tabs.index("natural_log_exp_and_others")
    SET_SILU = _tabs.index("silu_and_others")

    def load_act_set(sid):
        nc.scalar.add_instruction(mybir.InstLoadActFuncSet(
            name=nc.get_next_instruction_name(), act_func_set_id=sid,
            ins=[], outs=[]))

    def barrier():
        snap = dict(cnt)
        for name in ["sync", "gpsimd", "tensor", "vector", "scalar"]:
            dec = getattr(block, name)
            @dec
            def _(eng):
                for s in SEMS:
                    if snap[s]:
                        eng.wait_ge(sems[s], snap[s])

    identb = sbt("identb", [128, 128], BF16)
    identf = sbt("identf", [128, 128], F32)
    masks = [sbt("mask0", [128, 256], BF16), sbt("mask1", [128, 256], BF16)]
    ones_r128b = sbt("ones_r128b", [1, 128], BF16)
    ones_r128f = sbt("ones_r128f", [1, 128], F32)
    ones_r512b = sbt("ones_r512b", [1, TILE], BF16)
    ones_c128b = sbt("ones_c128b", [128, 1], BF16)
    convw = sbt("convw", [128, NCT * 4], F32)
    convb = sbt("convb", [128, NCT], F32)
    dtbcol = sbt("dtbcol", [16, 1], F32)
    acol = sbt("acol", [16, 1], F32)
    rmscol = sbt("rmscol", [128, H], F32)
    i_d = sbt("i_d", [128, H * 128], BF16)
    xwcol = sbt("xwcol", [128, KD], F32)
    xbcolT = sbt("xbcolT", [128, KD], F32)
    xbcolT_b = sbt("xbcolT_b", [128, KD], BF16)
    carryA = [sbt(f"carryA{s}", [128, 8 * 128], F32) for s in range(SEQ)]
    carry_b = sbt("carry_b", [128, 8 * 128], BF16)
    drow = sbt("drow", [1, H], F32)
    zero3 = sbt("zero3", [128, 4], BF16)
    epscol = sbt("epscol", [128, 1], F32)

    def cslc(h):
        hb = 0 if h < 8 else 64
        hc = (h % 8) * 128
        return hb, hc

    @block.sync
    def _(sync):
        for dst, src, slow in [
            (identb, identb_h, False), (identf, identf_h, False),
            (masks[0], mask0_h, False), (masks[1], mask1_h, False),
            (convw[:].rearrange("p (c k) -> p c k", k=4),
             convw_h.rearrange("(c p) k -> p c k", p=128), False),
            (convb, convb_h.rearrange("(c p) -> p c", p=128), True),
            (dtbcol, dtb_h.unsqueeze(1), False),
            (acol, alog_h.unsqueeze(1), False),
            (rmscol, rmsw_h.rearrange("(h p) -> p h", p=128), True),
            (xwcol, xw_h.rearrange("(k p) -> p k", p=128), True),
            (xbcolT, xb_h.rearrange("(k p) -> p k", p=128), True),
            (drow, dd_h.unsqueeze(0), False),
        ]:
            i = sync.dma_start(out=dst[:], in_=src, allow_slow_non_contiguous=slow)
            inc(i, "sy", 16)
        sync.wait_ge(sems["sy"], cnt["sy"])
    SMALL_LD = cnt["sy"]

    @block.vector
    def _(vector):
        nc.vector.memset(ones_r128b[:], 1.0)
        nc.vector.memset(ones_r128f[:], 1.0)
        nc.vector.memset(ones_r512b[:], 1.0)
        nc.vector.memset(ones_c128b[:], 1.0)
        nc.vector.memset(zero3[:], 0.0)
        nc.vector.memset(epscol[:], LN_EPS)
        for s in range(SEQ):
            nc.vector.memset(carryA[s][:], 0.0)
        nc.vector.memset(carry_b[:], 0.0)
        w(vector, "sy", SMALL_LD)
        i = nc.vector.tensor_copy(xbcolT_b[:], xbcolT[:])
        inc(i, "ve")
    CONSTS_VE = cnt["ve"]

    @block.scalar
    def _(scalar):
        w(scalar, "sy", SMALL_LD)
        load_act_set(SET_NLE)
        i = nc.scalar.activation(acol[:], acol[:], AF.Exp)
        inc(i, "ac")
    ACOL_AC = cnt["ac"]

    with nc.psum_tensor("dcols_p", [128, H], F32) as dcols_p:
        dcols = sbt("dcols", [128, H], F32)

        @block.tensor
        def _(tensor):
            w(tensor, "sy", SMALL_LD)
            w(tensor, "ve", CONSTS_VE)
            i = nc.tensor.matmul(dcols_p[:], ones_r128f[0:1, :], drow[:],
                                 start=True, stop=True)
            inc(i, "pe")

        @block.vector
        def _(vector):
            w(vector, "pe")
            w(vector, "ac", ACOL_AC)
            nc.vector.tensor_scalar_mul(acol[:], acol[:], -1.0)
            nc.vector.tensor_copy(dcols[:], dcols_p[:])
            nc.vector.drain()
            for h in range(H):
                i = nc.vector.tensor_scalar_mul(i_d[:, h * 128:(h + 1) * 128],
                                                identb[:], dcols[:, h:h + 1])
            inc(i, "ve")

    @block.gpsimd
    def _(gpsimd):
        w(gpsimd, "ve", CONSTS_VE)
        xbv = xbc_s.rearrange("(s c p) l -> s c p l", s=SEQ, c=NCT)
        for s in range(SEQ):
            for c in range(NCT):
                i = gpsimd.dma_start(out=xbv[s, c, :, 0:3], in_=zero3[:, 0:3])
                inc(i, "po", 16)
        gpsimd.wait_ge(sems["po"], cnt["po"])

    def quant_scratch(stk, tag, n, f16=False):
        sb = lambda nm, shape, dt=F32: stk.enter_context(nc.sbuf_tensor(nm, shape, dt))
        nb = n // 16
        ddt = F16 if f16 else F32
        d = {
            "amax": sb(f"{tag}_amax", [128, nb]), "scl": sb(f"{tag}_scl", [128, nb]),
            "absx": sb(f"{tag}_absx", [128, n], ddt),
            "sg": sb(f"{tag}_sg", [128, n], F16 if f16 else BF16),
            "aa": sb(f"{tag}_aa", [128, n], ddt),
            "s1": sb(f"{tag}_s1", [128, n], F16 if f16 else BF16),
            "s2": sb(f"{tag}_s2", [128, n], F16 if f16 else BF16),
            "cc": sb(f"{tag}_cc", [128, n], F16 if f16 else BF16),
            "cd": sb(f"{tag}_cd", [128, n], F16 if f16 else BF16),
            "c5": sb(f"{tag}_c5", [128, n], F16 if f16 else BF16),
            "c6": sb(f"{tag}_c6", [128, n], F16 if f16 else BF16),
            "ss": sb(f"{tag}_ss", [128, n], F16 if f16 else BF16),
            "rcp16": sb(f"{tag}_rcp16", [128, nb], F16 if f16 else F32),
            "sclh16": sb(f"{tag}_sclh16", [128, nb], F16 if f16 else F32),
        }
        return d

    def emit_quant_full(src, dst, n, q, first_wait=None):
        nb = n // 16
        assert n % 32 == 0
        hn, hb = n // 2, nb // 2
        CS = [slice(0, hn), slice(hn, n)]
        GS = [slice(0, hb), slice(hb, nb)]
        @block.scalar
        def _(scalar):
            if first_wait is not None:
                w(scalar, first_wait[0], first_wait[1])
            if q.get("prev") is not None:
                w(scalar, "ve", q["prev_absx"])
                w(scalar, "gp", q["prev_sg"])
            else:
                w(scalar, "ve")
                w(scalar, "gp")
            nc.scalar.activation(q["absx"][:, 0:n], src, AF.Abs)
            i = nc.scalar.activation(q["sg"][:, 0:n], src, AF.Sign)
            inc(i, "ac")
        AC = cnt["ac"]
        gp0 = cnt["gp"]
        gp_ss = [gp0 + 1, gp0 + 2]
        gp_ad = [gp0 + 5, gp0 + 6]
        mk = {}

        @block.vector
        def _(vector):
            def vw(v):
                if v:
                    nc.vector.wait_ge(sems["ve"], v)
            def vop(key, i, instr):
                inc(instr, "ve")
                mk[(key, i)] = cnt["ve"]
            nc.vector.wait_ge(sems["ac"], AC)
            if q.get("prev") is not None:
                vw(q["prev"])
                nc.vector.wait_ge(sems["gp"], q["prev_gp"])
            for i in (0, 1):
                vop("mx", i, nc.vector.reduce_max(
                    q["amax"][:, GS[i]],
                    src[:, CS[i]].rearrange("p (a b) -> p a b", b=16),
                    axis=AX.X, apply_absolute_value=True))
            for i in (0, 1):
                vw(mk[("mx", i)])
                vop("rc", i, nc.vector.reciprocal(q["scl"][:, GS[i]],
                                                  q["amax"][:, GS[i]]))
            for i in (0, 1):
                vw(mk[("rc", i)])
                vop("r6", i, nc.vector.tensor_scalar(
                    q["rcp16"][:, GS[i]], q["scl"][:, GS[i]], 6.0, 1e12,
                    ALU.mult, ALU.min))
                vop("sh", i, nc.vector.tensor_scalar(
                    q["sclh16"][:, GS[i]], q["amax"][:, GS[i]], 1.0 / 12.0,
                    5e-13, ALU.mult, ALU.max))
            for i in (0, 1):
                vw(mk[("r6", i)])
                vop("aa", i, nc.vector.tensor_mul(
                    q["aa"][:, CS[i]], q["absx"][:, CS[i]],
                    q["rcp16"][:, GS[i]].unsqueeze(2)
                    .broadcast_to([128, hb, 16])))
            for i in (0, 1):
                vw(mk[("aa", i)])
                vop("t0", i, nc.vector.tensor_scalar(
                    q["s1"][:, CS[i]], q["aa"][:, CS[i]], QTHR[0], 1.0,
                    ALU.is_gt, ALU.mult))
                vop("c0", i, nc.vector.tensor_scalar(
                    q["cc"][:, CS[i]], q["aa"][:, CS[i]], QTHR[1], 1.0,
                    ALU.is_gt, ALU.mult))
            for i in (0, 1):
                vw(mk[("t0", i)])
                vw(mk[("c0", i)])
                vop("t1", i, nc.vector.tensor_add(
                    q["s1"][:, CS[i]], q["s1"][:, CS[i]], q["cc"][:, CS[i]]))
                vop("c1", i, nc.vector.tensor_scalar(
                    q["cd"][:, CS[i]], q["aa"][:, CS[i]], QTHR[2], 1.0,
                    ALU.is_gt, ALU.mult))
            for i in (0, 1):
                vw(mk[("t1", i)])
                vw(mk[("c1", i)])
                vop("t2", i, nc.vector.tensor_add(
                    q["s1"][:, CS[i]], q["s1"][:, CS[i]], q["cd"][:, CS[i]]))
                vop("c2", i, nc.vector.tensor_scalar(
                    q["cc"][:, CS[i]], q["aa"][:, CS[i]], QTHR[3], 1.0,
                    ALU.is_gt, ALU.mult))
            for i in (0, 1):
                vw(mk[("t2", i)])
                vw(mk[("c2", i)])
                vop("t3", i, nc.vector.tensor_add(
                    q["s1"][:, CS[i]], q["s1"][:, CS[i]], q["cc"][:, CS[i]]))
                vop("u0", i, nc.vector.tensor_scalar(
                    q["s2"][:, CS[i]], q["aa"][:, CS[i]], QTHR[4], 2.0,
                    ALU.is_gt, ALU.mult))
                vop("u1", i, nc.vector.tensor_scalar(
                    q["c5"][:, CS[i]], q["aa"][:, CS[i]], QTHR[5], 2.0,
                    ALU.is_gt, ALU.mult))
                vop("u2", i, nc.vector.tensor_scalar(
                    q["c6"][:, CS[i]], q["aa"][:, CS[i]], QTHR[6], 4.0,
                    ALU.is_gt, ALU.mult))
            for i in (0, 1):
                vw(mk[("u1", i)])
                vop("v1", i, nc.vector.tensor_add(
                    q["s2"][:, CS[i]], q["s2"][:, CS[i]], q["c5"][:, CS[i]]))
            for i in (0, 1):
                vw(mk[("t3", i)])
                vw(mk[("v1", i)])
                vop("v2", i, nc.vector.tensor_add(
                    q["s1"][:, CS[i]], q["s1"][:, CS[i]], q["s2"][:, CS[i]]))
            for i in (0, 1):
                vw(mk[("u2", i)])
                vw(mk[("v2", i)])
                vop("fin", i, nc.vector.tensor_add(
                    q["s1"][:, CS[i]], q["s1"][:, CS[i]], q["c6"][:, CS[i]]))
            for i in (0, 1):
                vw(mk[("fin", i)])
                nc.vector.wait_ge(sems["gp"], gp_ss[i])
                vop("ds", i, nc.vector.tensor_mul(
                    dst[:, CS[i]], q["s1"][:, CS[i]], q["ss"][:, CS[i]]))

        @block.gpsimd
        def _(gpsimd):
            w(gpsimd, "ac", AC)
            if q.get("prev") is not None:
                w(gpsimd, "ve", q["prev"])
            for i in (0, 1):
                gpsimd.wait_ge(sems["ve"], mk[("sh", i)])
                ii = nc.gpsimd.tensor_mul(
                    q["ss"][:, CS[i]], q["sg"][:, CS[i]],
                    q["sclh16"][:, GS[i]].unsqueeze(2)
                    .broadcast_to([128, hb, 16]))
                inc(ii, "gp")
        assert cnt["gp"] == gp_ss[1]
        q["prev"] = cnt["ve"]
        q["prev_gp"] = cnt["gp"]
        q["prev_absx"] = mk[("aa", 1)]
        q["prev_sg"] = gp_ss[1]
        return cnt["ve"]

    def emit_transpose(blocks, bankA, bankB, ident_t, rows_in=128, pe_waits=()):
        same = bankA is bankB
        ngr = (len(blocks) + 3) // 4
        ev_done = [None, None]
        for g in range(ngr):
            bank = (bankA, bankB)[g % 2]
            cur = blocks[g * 4:(g + 1) * 4]
            @block.tensor
            def _(tensor):
                if g == 0:
                    for s_, v_ in pe_waits:
                        w(tensor, s_, v_)
                    w(tensor, "ve")
                guard = ev_done[(g - 1) % 2] if same else ev_done[g % 2]
                if guard is not None:
                    w(tensor, "ve", guard)
                for j, (srcap, dstap) in enumerate(cur):
                    i = nc.tensor.transpose(
                        bank[:, j * rows_in:(j + 1) * rows_in], srcap,
                        ident_t[0:rows_in, 0:rows_in])
                inc(i, "pe")
            PE_V = cnt["pe"]
            @block.vector
            def _(vector):
                w(vector, "pe", PE_V)
                for j, (srcap, dstap) in enumerate(cur):
                    i = nc.vector.tensor_copy(
                        dstap, bank[0:dstap.shape[0],
                                    j * rows_in: j * rows_in + dstap.shape[1]])
                inc(i, "ve")
            ev_done[g % 2] = cnt["ve"]
        return cnt["ve"]

    def emit_l2norm_prep(w_hbm, n_rows, ncols, wld, wsq, wssq, wrn, wbf,
                         bankA, bankB, dstT, scale_const):
        kk = ncols // 128
        n_tiles = (n_rows + 127) // 128
        buf_free = [None, None]
        for wt in range(n_tiles):
            rows = min(128, n_rows - wt * 128)
            buf = wld[wt % 2]
            @block.sync
            def _(sync):
                if wt < 2:
                    w(sync, "ve")
                    w(sync, "ac")
                if buf_free[wt % 2] is not None:
                    w(sync, "ve", buf_free[wt % 2])
                i = sync.dma_start(out=buf[0:rows, 0:ncols],
                                   in_=w_hbm[wt * 128: wt * 128 + rows, :])
                inc(i, "sy", 16)
                sync.wait_ge(sems["sy"], cnt["sy"])
            @block.scalar
            def _(scalar):
                w(scalar, "sy")
                w(scalar, "ve")
                nc.scalar.activation(wsq[0:rows, 0:ncols], buf[0:rows, 0:ncols],
                                     AF.Square, accum_out=wssq[0:rows, :])
                nc.scalar.drain()
                nc.scalar.activation(wrn[0:rows, :], wssq[0:rows, :], AF.Ln)
                nc.scalar.drain()
                i = nc.scalar.activation(wrn[0:rows, :], wrn[0:rows, :], AF.Exp,
                                         scale=-0.5)
                inc(i, "ac")
            @block.vector
            def _(vector):
                w(vector, "ac")
                w(vector, "pe")
                nc.vector.tensor_scalar(wrn[0:rows, :], wrn[0:rows, :], 1e6,
                                        scale_const, ALU.min, ALU.mult)
                nc.vector.drain()
                i = nc.vector.tensor_scalar_mul(wbf[0:rows, 0:ncols],
                                                buf[0:rows, 0:ncols],
                                                wrn[0:rows, 0:1])
                inc(i, "ve")
            buf_free[wt % 2] = cnt["ve"]
            blocks = [(wbf[0:rows, k * 128:(k + 1) * 128],
                       dstT[:, k * n_rows + wt * 128: k * n_rows + wt * 128 + rows])
                      for k in range(kk)]
            emit_transpose(blocks, bankA, bankB, identb, rows_in=rows,
                           pe_waits=[("ve", buf_free[wt % 2])])

    def emit_wquant_prep(w_hbm, n_rows, ncols, wld, qsc, wbf, bankA, bankB,
                         dstT, qn):
        kk = ncols // 128
        n_tiles = (n_rows + 127) // 128
        buf_free = [None, None]
        for wt in range(n_tiles):
            rows = min(128, n_rows - wt * 128)
            buf = wld[wt % 2]
            @block.sync
            def _(sync):
                if wt < 2:
                    w(sync, "ve")
                    w(sync, "ac")
                if buf_free[wt % 2] is not None:
                    w(sync, "ve", buf_free[wt % 2])
                i = sync.dma_start(out=buf[0:rows, 0:ncols],
                                   in_=w_hbm[wt * 128: wt * 128 + rows, :])
                inc(i, "sy", 16)
                sync.wait_ge(sems["sy"], cnt["sy"])
            ld_done = cnt["sy"]
            @block.vector
            def _(vector):
                w(vector, "pe")
                i = nc.vector.engine_nop()
                inc(i, "ve")
            for off in range(0, ncols, qn):
                nn = min(qn, ncols - off)
                emit_quant_full(buf[0:128, off:off + nn],
                                wbf[0:128, off:off + nn], nn, qsc,
                                first_wait=("sy", ld_done))
            buf_free[wt % 2] = cnt["ve"]
            blocks = [(wbf[0:rows, k * 128:(k + 1) * 128],
                       dstT[:, k * n_rows + wt * 128: k * n_rows + wt * 128 + rows])
                      for k in range(kk)]
            emit_transpose(blocks, bankA, bankB, identb, rows_in=rows,
                           pe_waits=[("ve", buf_free[wt % 2])])

    u_view = u_h.rearrange("(n p) d -> n p d", p=128)

    def trow(t):
        return (t % 4) * 16 + (t // 4) * 4

    with ExitStack() as stk, \
         nc.psum_tensor("ptrA1", [128, 512], BF16) as ptrA, \
         nc.psum_tensor("ptrB1", [128, 512], BF16) as ptrB, \
         nc.psum_tensor("pmm0", [128, TILE], F32) as pmm0, \
         nc.psum_tensor("pmm1", [128, TILE], F32) as pmm1, \
         nc.psum_tensor("pcr", [1, 512], F32) as pcr:
        sb1 = lambda name, shape, dt=F32: stk.enter_context(nc.sbuf_tensor(name, shape, dt))
        wnt = sb1("wnt", [128, KD * F], BF16)
        constrow = sb1("constrow", [1, F], F32)
        constrow_b = sb1("constrow_b", [1, F], BF16)
        pstk = ExitStack()
        sb1p = lambda name, shape, dt=F32: pstk.enter_context(
            nc.sbuf_tensor(name, shape, dt))
        wld = [sb1p(f"wld{i}", [128, D]) for i in range(2)]
        wsq = sb1p("wsq", [128, D])
        wssq = sb1p("wssq", [128, 1])
        wrn = sb1p("wrn", [128, 1])
        wbf = sb1p("wbf", [128, D], BF16)

        emit_l2norm_prep(win_h, F, D, wld, wsq, wssq, wrn, wbf, ptrA, ptrB,
                         wnt, D ** -0.5)

        for nsl in range(9):
            n0, n1 = nsl * 512, min(F, nsl * 512 + 512)
            @block.tensor
            def _(tensor):
                w(tensor, "ve")
                for k in range(KD):
                    i = nc.tensor.matmul(pcr[0:1, 0:n1 - n0], xbcolT_b[:, k:k + 1],
                                         wnt[:, k * F + n0: k * F + n1],
                                         start=(k == 0), stop=(k == KD - 1))
                inc(i, "pe")
            CR_PE = cnt["pe"]
            @block.vector
            def _(vector):
                w(vector, "pe", CR_PE)
                i = nc.vector.tensor_copy(constrow[0:1, n0:n1], pcr[0:1, 0:n1 - n0])
                inc(i, "ve")
        @block.vector
        def _(vector):
            w(vector, "ve")
            nc.vector.tensor_copy(constrow_b[:], constrow[:])
            for k in range(KD):
                i = nc.vector.tensor_scalar_mul(wnt[:, k * F:(k + 1) * F],
                                                wnt[:, k * F:(k + 1) * F],
                                                xwcol[:, k:k + 1])
            inc(i, "ve")
        WNT_VE = cnt["ve"]
        pstk.close()

        ubuf = sb1("ubuf", [128, 4 * D])
        xn = sb1("xn", [128, 4 * D])
        xh = sb1("xh", [128, 4 * D], BF16)
        xhT = sb1("xhT", [128, KD * TILE], BF16)
        qd = sb1("qd", [128, 1024], BF16)
        xnqT = sb1("xnqT", [128, KD * TILE], BF16)
        su = sb1("su", [128, 4])
        ssq4 = sb1("ssq4", [128, 4])
        mean4 = sb1("mean4", [128, 4])
        msq4 = sb1("msq4", [128, 4])
        var4 = sb1("var4", [128, 4])
        rstd4 = sb1("rstd4", [128, 4])
        bia4 = sb1("bia4", [128, 4])
        q1 = quant_scratch(stk, "q1", 1024, f16=True)
        stg_ev = [sb1(f"stg_ev{i}", [128, TILE], BF16) for i in range(2)]
        dtstg = sb1("dtstg", [16, TILE])

        u_done = [None] * (NT + 1)
        @block.sync
        def _(sync):
            w(sync, "ve", WNT_VE)
            i = sync.dma_start(
                out=ubuf[:].rearrange("p (n d) -> p n d", n=4),
                in_=u_view[trow(0):trow(0) + 4, :, :].rearrange("n p d -> p n d"))
            u_done[0] = inc(i, "sy", 16)
            sync.wait_ge(sems["sy"], cnt["sy"])

        def emit_ln_dbl(src_buf, xn_dst, xh_dst, waits):
            @block.scalar
            def _(scalar):
                for s_, v_ in waits:
                    w(scalar, s_, v_)
                w(scalar, "ac")
                w(scalar, "ve")
                for mt in range(4):
                    sl = src_buf[:, mt * D:(mt + 1) * D]
                    nc.scalar.activation(xn_dst[:, mt * D:(mt + 1) * D], sl,
                                         AF.Copy, accum_out=su[:, mt:mt + 1])
                    i = nc.scalar.activation(xh_dst[:, mt * D:(mt + 1) * D],
                                             sl, AF.Square,
                                             accum_out=ssq4[:, mt:mt + 1])
                inc(i, "ac")
            ACC_AC = cnt["ac"]
            @block.vector
            def _(vector):
                w(vector, "ac", ACC_AC)
                w(vector, "ve")
                nc.vector.tensor_scalar_mul(mean4[:], su[:], 1.0 / D)
                nc.vector.drain()
                nc.vector.tensor_mul(msq4[:], mean4[:], mean4[:])
                nc.vector.drain()
                i = nc.vector.scalar_tensor_tensor(var4[:], ssq4[:], 1.0 / D,
                                                   msq4[:], ALU.mult, ALU.subtract)
                inc(i, "ve")
            @block.scalar
            def _(scalar):
                w(scalar, "ve")
                nc.scalar.activation(rstd4[:], var4[:], AF.Ln, bias=epscol[:, 0:1])
                nc.scalar.drain()
                i = nc.scalar.activation(rstd4[:], rstd4[:], AF.Exp, scale=-0.5)
                inc(i, "ac")
            @block.vector
            def _(vector):
                w(vector, "ac")
                nc.vector.tensor_mul(msq4[:], var4[:], rstd4[:])
                nc.vector.drain()
                i = nc.vector.tensor_mul(msq4[:], msq4[:], rstd4[:])
                inc(i, "ve")
            @block.scalar
            def _(scalar):
                w(scalar, "ve")
                nc.scalar.activation(su[:], msq4[:], AF.Ln, bias=epscol[:, 0:1])
                nc.scalar.drain()
                i = nc.scalar.activation(su[:], su[:], AF.Exp, scale=-0.5)
                inc(i, "ac")
            @block.vector
            def _(vector):
                w(vector, "ac")
                w(vector, "ve")
                nc.vector.tensor_mul(var4[:], rstd4[:], su[:])
                nc.vector.scalar_tensor_tensor(bia4[:], mean4[:], -1.0,
                                               rstd4[:], ALU.mult, ALU.mult)
                nc.vector.drain()
                nc.vector.scalar_tensor_tensor(ssq4[:], mean4[:], -1.0,
                                               var4[:], ALU.mult, ALU.mult)
                nc.vector.drain()
                for mt in range(4):
                    nc.vector.tensor_scalar(xn_dst[:, mt * D:(mt + 1) * D],
                                            src_buf[:, mt * D:(mt + 1) * D],
                                            rstd4[:, mt:mt + 1],
                                            bia4[:, mt:mt + 1],
                                            ALU.mult, ALU.add)
                    i = nc.vector.tensor_scalar(xh_dst[:, mt * D:(mt + 1) * D],
                                                src_buf[:, mt * D:(mt + 1) * D],
                                                var4[:, mt:mt + 1],
                                                ssq4[:, mt:mt + 1],
                                                ALU.mult, ALU.add)
                inc(i, "ve")
            return ("ve", cnt["ve"])

        pm_free = [None, None]
        pm_free_sem = [None, None]
        stg_free = [None] * 2
        prev_qsg_ac = None
        prev_xht_pe = None
        prev_xnqt_po = None

        for t in range(ntl):
            s_seq, cpair = t % 4, t // 4

            lnw = [("sy", u_done[t]), ("ve", WNT_VE)]
            if t > 0:
                lnw += [("ac", prev_qsg_ac), ("pe", prev_xht_pe),
                        ("po", prev_xnqt_po)]
            XN_M = emit_ln_dbl(ubuf, xn, xh, lnw)
            XH_M = XN_M

            if t + 1 < NT:
                @block.sync
                def _(sync):
                    w(sync, XN_M[0], XN_M[1])
                    i = sync.dma_start(
                        out=ubuf[:].rearrange("p (n d) -> p n d", n=4),
                        in_=u_view[trow(t + 1):trow(t + 1) + 4, :, :]
                        .rearrange("n p d -> p n d"))
                    u_done[t + 1] = inc(i, "sy", 16)
                    sync.wait_ge(sems["sy"], cnt["sy"])

            blocks = [(xh[:, mt * D + k * 128: mt * D + (k + 1) * 128],
                       xhT[:, k * TILE + mt * 128: k * TILE + (mt + 1) * 128])
                      for mt in range(4) for k in range(KD)]
            XHT_VE = emit_transpose(blocks, ptrA, ptrB, identb,
                                    pe_waits=[XH_M])
            prev_xht_pe = cnt["pe"]
            for sl4 in range(4):
                QS_VE = emit_quant_full(xn[:, sl4 * 1024:(sl4 + 1) * 1024],
                                        qd[:], 1024, q1,
                                        first_wait=XN_M)
                if sl4 == 3:
                    prev_qsg_ac = cnt["ac"]
                mt = sl4
                blocks = [(qd[:, j * 128:(j + 1) * 128],
                           xnqT[:, j * TILE + mt * 128:
                                j * TILE + (mt + 1) * 128])
                          for j in range(8)]
                emit_transpose(blocks, ptrA, ptrB, identb,
                               pe_waits=[("ve", QS_VE)])
            XNQT_VE = cnt["ve"]

            @block.gpsimd
            def _(gpsimd):
                w(gpsimd, "ve", XNQT_VE)
                i = gpsimd.dma_start(
                    out=xnqT_s.rearrange("(t k p) n -> t p k n", t=NT, k=KD)[t],
                    in_=xnqT[:].rearrange("p (k n) -> p k n", k=KD))
                inc(i, "po", 16)
                gpsimd.wait_ge(sems["po"], cnt["po"])
            prev_xnqt_po = cnt["po"]

            xbv = xbc_s.rearrange("(s c p) l -> s c p l", s=SEQ, c=NCT)
            szv = siluz_s.rearrange("(t z p) n -> t z p n", t=NT, z=NZT)
            for ft in range(NFT):
                rows = 128 if ft < 33 else 16
                pm = (pmm0, pmm1)[ft % 2]
                @block.tensor
                def _(tensor):
                    if ft == 0:
                        w(tensor, "ve", XHT_VE)
                    if pm_free[ft % 2] is not None:
                        w(tensor, pm_free_sem[ft % 2], pm_free[ft % 2])
                    for k in range(KD):
                        nc.tensor.matmul(
                            pm[0:rows, :],
                            wnt[:, k * F + ft * 128: k * F + ft * 128 + rows],
                            xhT[:, k * TILE:(k + 1) * TILE],
                            start=(k == 0), stop=False)
                    i = nc.tensor.matmul(pm[0:rows, :],
                                         constrow_b[0:1, ft * 128: ft * 128 + rows],
                                         ones_r512b[:], start=False, stop=True)
                    inc(i, "pe")
                MM_PE = cnt["pe"]
                if ft < 33:
                    sbuf_ev = stg_ev[ft % 2]
                    @block.scalar
                    def _(scalar):
                        w(scalar, "pe", MM_PE)
                        if stg_free[ft % 2] is not None:
                            w(scalar, stg_free[ft % 2][0], stg_free[ft % 2][1])
                        i = nc.scalar.activation(sbuf_ev[:], pm[:], AF.Copy)
                        inc(i, "ac")
                    EV_AC = cnt["ac"]
                    pm_free[ft % 2], pm_free_sem[ft % 2] = EV_AC, "ac"
                    if ft < 16:
                        @block.gpsimd
                        def _(gpsimd):
                            w(gpsimd, "ac", EV_AC)
                            i = gpsimd.dma_start(out=szv[t, ft], in_=sbuf_ev[:])
                            inc(i, "po", 16)
                            gpsimd.wait_ge(sems["po"], cnt["po"])
                        stg_free[ft % 2] = ("po", cnt["po"])
                    else:
                        ct = ft - 16
                        @block.sync
                        def _(sync):
                            w(sync, "ac", EV_AC)
                            i = sync.dma_start(
                                out=xbv[s_seq, ct, :,
                                        3 + cpair * 512: 3 + (cpair + 1) * 512],
                                in_=sbuf_ev[:])
                            inc(i, "sy", 16)
                            sync.wait_ge(sems["sy"], cnt["sy"])
                        stg_free[ft % 2] = ("sy", cnt["sy"])
                else:
                    @block.vector
                    def _(vector):
                        w(vector, "pe", MM_PE)
                        w(vector, "po")
                        i = nc.vector.tensor_copy(dtstg[:], pm[0:16, :])
                        inc(i, "ve")
                    DT_VE = cnt["ve"]
                    pm_free[ft % 2], pm_free_sem[ft % 2] = DT_VE, "ve"
                    @block.gpsimd
                    def _(gpsimd):
                        w(gpsimd, "ve", DT_VE)
                        i = gpsimd.dma_start(out=dt_s[t * 16:(t + 1) * 16, :],
                                             in_=dtstg[:])
                        inc(i, "po", 16)
                        gpsimd.wait_ge(sems["po"], cnt["po"])

        @block.gpsimd
        def _(gpsimd):
            w(gpsimd, "po")

    barrier()
    if maxphase <= 1:
        ctx.close()
        return nc

    with ExitStack() as stk, \
         nc.psum_tensor("p_sc", [128, 512], F32) as p_sc, \
         nc.psum_tensor("p_seg", [128, 512], F32) as p_seg, \
         nc.psum_tensor("p_y0", [128, 512], F32) as p_y0, \
         nc.psum_tensor("p_y1", [128, 512], F32) as p_y1, \
         nc.psum_tensor("p_st", [128, 512], F32) as p_st, \
         nc.psum_tensor("p_ssq", [128, 512], F32) as p_ssq, \
         nc.psum_tensor("p_rep", [128, 512], F32) as p_rep, \
         nc.psum_tensor("ptr2", [128, 512], BF16) as ptr2:
        sb2 = lambda name, shape, dt=F32: stk.enter_context(nc.sbuf_tensor(name, shape, dt))
        wot = sb2("wot", [128, 16 * 1024], BF16)
        accA = sb2("accA", [128, 4 * TILE])
        accB = sb2("accB", [128, 4 * TILE])
        diagb = sb2("diagb", [128, 2 * 512], BF16)
        xbcw = sb2("xbcw", [128, NCT * 515], BF16)
        x_j = sb2("x_j", [128, 4 * DI], BF16)
        zld = sb2("zld", [128, NZT * TILE], BF16)
        bdec = sb2("bdec", [128, 4 * 1024], BF16)
        wssq2 = sb2("wssq2", [128, 1])
        wrn2 = sb2("wrn2", [128, 1])

        ubp = sb2("ubp", [128, DI])
        emit_l2norm_prep(wout_h, D, DI, [accA, accB], ubp, wssq2, wrn2,
                         x_j, ptr2, ptr2, wot, 1.0)
        WOT_VE = cnt["ve"]

        dtr = accA[0:16, 0:512]
        sp_t1 = accA[0:16, 512:1024]
        sp_t2 = accA[0:16, 1024:1536]
        dt_ht = accA[0:16, 1536:2048]
        ldt_t = accB[0:16, 0:512]
        dA_t = accB[0:16, 512:1024]
        cum = accB[0:16, 1024:1536]
        r1 = accB[0:16, 1536:2048]
        dd2 = sb2("dd2", [16, 1024])
        ddec = dd2[0:16, 0:512]
        dtdec = dd2[0:16, 512:1024]
        chi = sb2("chi", [16, TILE], BF16)
        cmid = sb2("cmid", [16, TILE], BF16)
        clo = sb2("clo", [16, TILE], BF16)
        ec_b = sb2("ec_b", [16, TILE], BF16)
        cdf = sb2("cdf", [16, 2])
        ldtT = sb2("ldtT", [128, 4 * 16])
        dtdecT = sb2("dtdecT", [128, 4 * 16])
        bjn = sb2("bjn", [128, 4 * 64], BF16)
        segL = sb2("segL", [6, 4096], BF16)
        segR = sb2("segR", [6, 4096], BF16)
        ecrow = sb2("ecrow", [1, 4096], BF16)
        cdrow = sb2("cdrow", [1, 16])
        cdrepS = sb2("cdrepS", [128, 16])
        cs2 = sb2("cs2", [128, 256], BF16)
        scoresM = sb2("scoresM", [128, 512], BF16)
        relb = [sb2(f"relb{i}", [128, 512], BF16) for i in range(3)]
        eb = [sb2(f"eb{i}", [128, 512], BF16) for i in range(3)]
        attb = [sb2(f"attb{i}", [128, 512], BF16) for i in range(3)]
        cecb = sb2("cecb", [128, 8 * 256], BF16)
        g2scr = [sb2(f"g2scr{i}", [128, 256], BF16) for i in range(2)]
        gbuf = xbcw[:, 0:H * TILE]
        rstd_rf = sb2("rstd_rf", [1, 512])
        rstd_rb = sb2("rstd_rb", [1, 512], BF16)
        rep_rstd = sb2("rep_rstd", [128, 512], BF16)
        ub2 = [ubp[:, 0:D], ubp[:, D:2 * D]]
        pstg = [sb2(f"pstg{i}", [128, D]) for i in range(2)]

        @block.vector
        def _(vector):
            w(vector, "ve", WOT_VE)
            nc.vector.memset(segL[:], 1.0)
            i = nc.vector.memset(segR[:], -1.0)
            inc(i, "ve")
        SEGI_VE = cnt["ve"]

        tile_res_ve = None
        tile_res_po = None
        pst_done = [None, None]
        for t in range(ntl):
            s_seq, cpair = t % 4, t // 4

            @block.sync
            def _(sync):
                if tile_res_ve is not None:
                    w(sync, "ve", tile_res_ve)
                xbv = xbc_s.rearrange("(s c p) l -> s c p l", s=SEQ, c=NCT)
                for c in range(NCT):
                    i = sync.dma_start(
                        out=xbcw[:, c * 515:(c + 1) * 515],
                        in_=xbv[s_seq, c, :, cpair * 512: cpair * 512 + 515])
                    inc(i, "sy", 16)
                i = sync.dma_start(
                    out=zld[:].rearrange("p (z n) -> p z n", z=NZT),
                    in_=siluz_s.rearrange("(t z p) n -> t p z n", t=NT, z=NZT)[t])
                inc(i, "sy", 16)
                sync.wait_ge(sems["sy"], cnt["sy"])
            LD_SY = cnt["sy"]

            conv_free = [None, None]
            diag_free = [None, None]
            for c in range(NCT):
                slot = c % 2
                pcv = (p_sc, p_seg)[slot]
                @block.vector
                def _(vector):
                    w(vector, "sy", LD_SY)
                    if c < 2:
                        w(vector, "ve")
                        w(vector, "pe")
                    if diag_free[slot] is not None:
                        w(vector, "pe", diag_free[slot])
                    for j in range(4):
                        i = nc.vector.tensor_scalar_mul(
                            diagb[:, slot * 512 + j * 128:
                                  slot * 512 + (j + 1) * 128],
                            identb[:], convw[:, c * 4 + j: c * 4 + j + 1])
                    inc(i, "ve")
                DG_VE = cnt["ve"]
                @block.tensor
                def _(tensor):
                    w(tensor, "ve", DG_VE)
                    if conv_free[slot] is not None:
                        w(tensor, "ac", conv_free[slot])
                    elif c < 2:
                        w(tensor, "ac")
                        w(tensor, "ve")
                    for j in range(4):
                        i = nc.tensor.matmul(
                            pcv[:, 0:512],
                            diagb[:, slot * 512 + j * 128:
                                  slot * 512 + (j + 1) * 128],
                            xbcw[:, c * 515 + j: c * 515 + j + 512],
                            start=(j == 0), stop=(j == 3))
                    inc(i, "pe")
                CV_PE = cnt["pe"]
                diag_free[slot] = CV_PE
                @block.scalar
                def _(scalar):
                    w(scalar, "pe", CV_PE)
                    if c == 0:
                        load_act_set(SET_SILU)
                    i = nc.scalar.activation(
                        xbcw[:, c * 515 + 3: c * 515 + 515], pcv[:, 0:512],
                        AF.Silu, bias=convb[:, c:c + 1])
                    inc(i, "ac")
                conv_free[slot] = cnt["ac"]
            @block.scalar
            def _(scalar):
                w(scalar, "sy", LD_SY)
                i = nc.scalar.activation(zld[:], zld[:], AF.Silu)
                inc(i, "ac")
            ZSIL_AC = cnt["ac"]
            CONV_AC = cnt["ac"]
            if p2stage <= 1:
                continue

            def xs(ct, tok0, nt_, rows=slice(0, 128)):
                return xbcw[rows, ct * 515 + 3 + tok0: ct * 515 + 3 + tok0 + nt_]

            @block.sync
            def _(sync):
                w(sync, "ve")
                w(sync, "ac")
                i = sync.dma_start(out=dtr, in_=dt_s[t * 16:(t + 1) * 16, :])
                inc(i, "sy", 16)
                sync.wait_ge(sems["sy"], cnt["sy"])
            DTL_SY = cnt["sy"]
            @block.scalar
            def _(scalar):
                w(scalar, "sy", DTL_SY)
                load_act_set(SET_NLE)
                nc.scalar.activation(sp_t1, dtr, AF.Abs, bias=dtbcol[:, 0:1])
                nc.scalar.drain()
                nc.scalar.activation(sp_t1, sp_t1, AF.Exp, scale=-1.0)
                nc.scalar.drain()
                nc.scalar.activation(sp_t2, sp_t1, AF.Ln, bias=1.0)
                nc.scalar.drain()
                i = nc.scalar.activation(sp_t1, dtr, AF.Relu, bias=dtbcol[:, 0:1])
                inc(i, "ac")
            @block.vector
            def _(vector):
                w(vector, "ac")
                nc.vector.tensor_add(dt_ht, sp_t1, sp_t2)
                nc.vector.drain()
                nc.vector.tensor_scalar_mul(dA_t, dt_ht, acol[:, 0:1])
                nc.vector.drain()
                for c in range(2):
                    sl = slice(c * CHUNK, (c + 1) * CHUNK)
                    nc.vector.tensor_tensor_scan(cum[:, sl], dA_t[:, sl],
                                                 dA_t[:, sl], 0.0, ALU.add,
                                                 ALU.bypass)
                nc.vector.drain()
                nc.vector.tensor_copy(chi[:], cum)
                nc.vector.drain()
                nc.vector.scalar_tensor_tensor(r1, cum, 1.0, chi[:],
                                               ALU.mult, ALU.subtract)
                nc.vector.drain()
                nc.vector.tensor_copy(cmid[:], r1)
                nc.vector.drain()
                nc.vector.scalar_tensor_tensor(r1, r1, 1.0, cmid[:],
                                               ALU.mult, ALU.subtract)
                nc.vector.drain()
                i = nc.vector.tensor_copy(clo[:], r1)
                inc(i, "ve")
            CUM_VE = cnt["ve"]
            @block.scalar
            def _(scalar):
                w(scalar, "ve", CUM_VE)
                for c in range(2):
                    sl = slice(c * CHUNK, (c + 1) * CHUNK)
                    last = slice(c * CHUNK + 255, c * CHUNK + 256)
                    nc.scalar.activation(ddec[:, c * 256:(c + 1) * 256],
                                         cum[:, sl], AF.Exp,
                                         scale=-1.0, bias=cum[:, last])
                    nc.scalar.activation(cdf[:, c:c + 1], cum[:, last], AF.Exp)
                i = nc.scalar.activation(ec_b[:], cum, AF.Exp)
                inc(i, "ac")
            DT_AC = cnt["ac"]
            @block.vector
            def _(vector):
                w(vector, "ac", DT_AC)
                i = nc.vector.tensor_mul(dtdec, dt_ht, ddec)
                inc(i, "ve")
            DTD_VE = cnt["ve"]

            blocks = [(dt_ht[0:16, tb * 128:(tb + 1) * 128],
                       ldtT[:, tb * 16:(tb + 1) * 16]) for tb in range(4)]
            emit_transpose(blocks, p_seg, p_seg, identf, rows_in=16,
                           pe_waits=[("ac", DT_AC)])
            blocks = [(dtdec[0:16, tb * 128:(tb + 1) * 128],
                       dtdecT[:, tb * 16:(tb + 1) * 16]) for tb in range(4)]
            emit_transpose(blocks, p_seg, p_seg, identf, rows_in=16,
                           pe_waits=[("ve", DTD_VE)])
            blocks = [(xs(zt, tb * 128, 128),
                       x_j[:, tb * DI + zt * 128: tb * DI + (zt + 1) * 128])
                      for tb in range(4) for zt in range(16)]
            emit_transpose(blocks, ptr2, ptr2, identb, rows_in=128,
                           pe_waits=[("ac", CONV_AC)])
            blocks = [(xs(16, tb * 128, 128, rows=slice(0, 64)),
                       bjn[:, tb * 64:(tb + 1) * 64]) for tb in range(4)]
            emit_transpose(blocks, ptr2, ptr2, identb, rows_in=64)
            TR_VE = cnt["ve"]

            @block.vector
            def _(vector):
                w(vector, "ve", TR_VE)
                for tb in range(4):
                    i = nc.vector.tensor_mul(
                        bdec[:, tb * 1024:(tb + 1) * 1024],
                        bjn[:, tb * 64:(tb + 1) * 64].unsqueeze(1)
                        .broadcast_to([128, 16, 64]),
                        dtdecT[:, tb * 16:(tb + 1) * 16].unsqueeze(2)
                        .broadcast_to([128, 16, 64]))
                inc(i, "ve")
            BDEC_VE = cnt["ve"]
            if p2stage <= 2:
                continue

            py_free = {0: None, 1: None}
            ebwar = {0: None, 1: None, 2: None}
            attbwar = {0: None, 1: None, 2: None}
            stbank_free = {0: None, 1: None}
            relseg_free = {0: None, 1: None}
            ssq_pe_prev = {0: None, 1: None}
            last_seg_pe = None
            last_cec_ve = None
            psc_free = None
            rstd_prev_ac = None
            for c in range(2):
                csl = slice(c * CHUNK, (c + 1) * CHUNK)
                @block.sync
                def _(sync):
                    w(sync, "ve", CUM_VE)
                    w(sync, "ac", DT_AC)
                    w(sync, "pe")
                    for row, srct in ((3, chi), (4, cmid), (5, clo)):
                        i = sync.dma_start(
                            out=segL[row:row + 1, :].rearrange(
                                "o (h i) -> o h i", h=16),
                            in_=srct[:, csl])
                        inc(i, "sy", 16)
                        i = sync.dma_start(
                            out=segR[row - 3:row - 2, :].rearrange(
                                "o (h i) -> o h i", h=16),
                            in_=srct[:, csl])
                        inc(i, "sy", 16)
                    i = sync.dma_start(
                        out=ecrow[:].rearrange("o (h i) -> o h i", h=16),
                        in_=ec_b[:, csl])
                    inc(i, "sy", 16)
                    i = sync.dma_start(out=cdrow[:].unsqueeze(2),
                                       in_=cdf[:, c:c + 1])
                    inc(i, "sy", 16)
                    i = sync.dma_start(out=cs2[0:64, :],
                                       in_=xs(16, c * CHUNK, CHUNK,
                                              rows=slice(64, 128)))
                    inc(i, "sy", 16)
                    i = sync.dma_start(out=cs2[64:128, :],
                                       in_=xs(16, c * CHUNK, CHUNK,
                                              rows=slice(64, 128)))
                    inc(i, "sy", 16)
                    sync.wait_ge(sems["sy"], cnt["sy"])
                STG_SY = cnt["sy"]

                @block.vector
                def _(vector):
                    w(vector, "ve")
                    i = nc.vector.tensor_copy(carry_b[:], carryA[s_seq][:])
                    inc(i, "ve")
                CB_VE = cnt["ve"]

                @block.tensor
                def _(tensor):
                    w(tensor, "ac", CONV_AC)
                    w(tensor, "sy", STG_SY)
                    if psc_free is not None:
                        w(tensor, "ve", psc_free)
                    for jt in range(2):
                        i = nc.tensor.matmul(
                            p_sc[:, jt * 256:(jt + 1) * 256],
                            xs(16, c * CHUNK + jt * 128, 128, slice(0, 64)),
                            cs2[0:64, :], start=True, stop=True)
                    inc(i, "pe")
                SCO_PE = cnt["pe"]
                @block.vector
                def _(vector):
                    w(vector, "pe", SCO_PE)
                    for jt in range(2):
                        i = nc.vector.tensor_mul(
                            scoresM[:, jt * 256:(jt + 1) * 256],
                            p_sc[:, jt * 256:(jt + 1) * 256], masks[jt][:])
                    inc(i, "ve")
                SCOM_VE = cnt["ve"]
                psc_free = SCOM_VE

                @block.tensor
                def _(tensor):
                    w(tensor, "sy", STG_SY)
                    w(tensor, "ve")
                    i = nc.tensor.matmul(p_rep[:, 0:16],
                                         ones_r128f[0:1, :], cdrow[:],
                                         start=True, stop=True)
                    inc(i, "pe")
                CDR_PE = cnt["pe"]
                @block.vector
                def _(vector):
                    w(vector, "pe", CDR_PE)
                    i = nc.vector.tensor_copy(cdrepS[:], p_rep[:, 0:16])
                    inc(i, "ve")
                CDREP_VE = cnt["ve"]

                for b4 in range(8):
                    rb = 0 if b4 < 4 else 64
                    @block.tensor
                    def _(tensor):
                        w(tensor, "ve")
                        for j in range(2):
                            hh = b4 * 2 + j
                            i = nc.tensor.matmul(
                                p_rep[rb:rb + 64, j * 256:(j + 1) * 256],
                                ones_r128b[0:1, 0:64],
                                ecrow[0:1, hh * 256:(hh + 1) * 256],
                                start=True, stop=True)
                        inc(i, "pe")
                    REP_PE = cnt["pe"]
                    @block.vector
                    def _(vector):
                        w(vector, "pe", REP_PE)
                        i = nc.vector.tensor_mul(
                            cecb[rb:rb + 64, (b4 % 4) * 512:(b4 % 4 + 1) * 512],
                            p_rep[rb:rb + 64, 0:512],
                            cs2[rb:rb + 64, :].unsqueeze(1)
                            .broadcast_to([64, 2, 256]))
                        inc(i, "ve")
                CEC_VE = cnt["ve"]
                if p2stage <= 3:
                    continue

                seg_pe = {}
                g2_ac = {}
                ssq_pe = {}

                def emit_seg(hh):
                    pcs_ = (p_seg, p_sc)[hh % 2]
                    @block.tensor
                    def _(tensor):
                        if relseg_free[hh % 2] is not None:
                            w(tensor, "ac", relseg_free[hh % 2])
                        elif hh % 2 == 0:
                            w(tensor, "ve", BDEC_VE)
                        else:
                            w(tensor, "ve", psc_free if psc_free else SCOM_VE)
                        w(tensor, "sy", STG_SY)
                        for jt in range(2):
                            i = nc.tensor.matmul(
                                pcs_[:, jt * 256:(jt + 1) * 256],
                                segL[0:6, hh * 256 + jt * 128:
                                     hh * 256 + (jt + 1) * 128],
                                segR[0:6, hh * 256:(hh + 1) * 256],
                                start=True, stop=True)
                        inc(i, "pe")
                    seg_pe[hh] = cnt["pe"]

                def emit_ssq(hh):
                    @block.tensor
                    def _(tensor):
                        w(tensor, "gp", g2_ac[hh])
                        if c == 1 and hh == 0:
                            w(tensor, "ac", rstd_prev_ac)
                        i = nc.tensor.matmul(
                            p_ssq[0:1, c * 256:(c + 1) * 256],
                            ones_c128b[:], g2scr[hh % 2][:],
                            start=(hh == 0), stop=(hh == H - 1))
                        inc(i, "pe")
                    ssq_pe[hh] = cnt["pe"]

                att_ve = {}

                def emit_actatt(hh):
                    pcs_ = (p_seg, p_sc)[hh % 2]
                    SEG_PE_ = seg_pe[hh]
                    EBW = ebwar[hh % 3]
                    @block.scalar
                    def _(scalar):
                        w(scalar, "pe", SEG_PE_)
                        if EBW is not None:
                            w(scalar, "ve", EBW)
                        i = nc.scalar.activation(
                            relb[hh % 3][:], pcs_[:], AF.Relu, scale=-1.0)
                        inc(i, "ac")
                        scalar.drain()
                        i = nc.scalar.activation(
                            eb[hh % 3][:], relb[hh % 3][:], AF.Exp, scale=-1.0)
                        inc(i, "ac")
                    E_AC_ = cnt["ac"]
                    relseg_free[hh % 2] = E_AC_ - 1
                    ATTBW = attbwar[hh % 3]
                    @block.vector
                    def _(vector):
                        w(vector, "ac", E_AC_)
                        w(vector, "ve", SCOM_VE)
                        if ATTBW is not None:
                            w(vector, "pe", ATTBW)
                        for jt in range(2):
                            tb = c * 2 + jt
                            i = nc.vector.scalar_tensor_tensor(
                                attb[hh % 3][:, jt * 256:(jt + 1) * 256],
                                eb[hh % 3][:, jt * 256:(jt + 1) * 256],
                                ldtT[:, tb * 16 + hh: tb * 16 + hh + 1],
                                scoresM[:, jt * 256:(jt + 1) * 256],
                                ALU.mult, ALU.mult)
                        inc(i, "ve")
                    ebwar[hh % 3] = cnt["ve"]
                    att_ve[hh] = cnt["ve"]

                emit_seg(0)
                emit_seg(1)
                emit_actatt(0)
                for h in range(H):
                    tb0, tb1 = c * 2, c * 2 + 1
                    hb, hc = cslc(h)
                    pyb = (p_y0, p_y1)[h % 2]
                    sbank = (p_st, p_rep)[h % 2]
                    ATT_VE = att_ve[h]

                    @block.tensor
                    def _(tensor):
                        w(tensor, "ve", ATT_VE)
                        w(tensor, "ve", CEC_VE)
                        w(tensor, "ve", CB_VE)
                        w(tensor, "ve", BDEC_VE)
                        if py_free[h % 2] is not None:
                            w(tensor, "ve", py_free[h % 2])
                        nc.tensor.matmul(
                            pyb[:, 0:256],
                            x_j[:, tb0 * DI + h * 128: tb0 * DI + (h + 1) * 128],
                            attb[h % 3][:, 0:256], start=True, stop=False)
                        nc.tensor.matmul(
                            pyb[:, 0:256],
                            x_j[:, tb1 * DI + h * 128: tb1 * DI + (h + 1) * 128],
                            attb[h % 3][:, 256:512], start=False, stop=False)
                        nc.tensor.matmul(
                            pyb[:, 0:128],
                            x_j[:, tb0 * DI + h * 128: tb0 * DI + (h + 1) * 128],
                            i_d[:, h * 128:(h + 1) * 128], start=False, stop=False)
                        nc.tensor.matmul(
                            pyb[:, 128:256],
                            x_j[:, tb1 * DI + h * 128: tb1 * DI + (h + 1) * 128],
                            i_d[:, h * 128:(h + 1) * 128], start=False, stop=False)
                        i = nc.tensor.matmul(
                            pyb[:, 0:256],
                            carry_b[hb:hb + 64, hc:hc + 128],
                            cecb[hb:hb + 64, (h % 8) * 256:(h % 8 + 1) * 256],
                            start=False, stop=True)
                        inc(i, "pe")
                        if stbank_free[h % 2] is not None:
                            w(tensor, "ve", stbank_free[h % 2])
                        nc.tensor.matmul(
                            p_st[hb:hb + 64, 0:128] if h % 2 == 0 else
                            p_rep[hb:hb + 64, 0:128],
                            bdec[:, tb0 * 1024 + h * 64: tb0 * 1024 + (h + 1) * 64],
                            x_j[:, tb0 * DI + h * 128: tb0 * DI + (h + 1) * 128],
                            start=True, stop=False)
                        i = nc.tensor.matmul(
                            p_st[hb:hb + 64, 0:128] if h % 2 == 0 else
                            p_rep[hb:hb + 64, 0:128],
                            bdec[:, tb1 * 1024 + h * 64: tb1 * 1024 + (h + 1) * 64],
                            x_j[:, tb1 * DI + h * 128: tb1 * DI + (h + 1) * 128],
                            start=False, stop=True)
                        inc(i, "pe")
                    ST_PE = cnt["pe"]
                    attbwar[h % 3] = ST_PE - 1

                    if h + 2 < H:
                        emit_seg(h + 2)
                    if h + 1 < H:
                        emit_actatt(h + 1)
                    if h >= 1:
                        emit_ssq(h - 1)

                    @block.vector
                    def _(vector):
                        w(vector, "pe", ST_PE)
                        w(vector, "ve", CDREP_VE)
                        nc.vector.scalar_tensor_tensor(
                            carryA[s_seq][hb:hb + 64, hc:hc + 128],
                            carryA[s_seq][hb:hb + 64, hc:hc + 128],
                            cdrepS[hb:hb + 64, h:h + 1],
                            sbank[hb:hb + 64, 0:128],
                            ALU.mult, ALU.add)
                        if h == 0 and c == 0:
                            w(vector, "ac", ZSIL_AC)
                        i = nc.vector.tensor_mul(
                            gbuf[:, h * TILE + c * 256: h * TILE + (c + 1) * 256],
                            pyb[:, 0:256],
                            zld[:, h * TILE + c * 256: h * TILE + (c + 1) * 256])
                        inc(i, "ve")
                    GE_VE = cnt["ve"]
                    stbank_free[h % 2] = GE_VE
                    py_free[h % 2] = GE_VE

                    @block.gpsimd
                    def _(gpsimd):
                        w(gpsimd, "ve", GE_VE)
                        if ssq_pe.get(h - 2) is not None:
                            w(gpsimd, "pe", ssq_pe[h - 2])
                        i = nc.gpsimd.tensor_mul(
                            g2scr[h % 2][:],
                            gbuf[:, h * TILE + c * 256: h * TILE + (c + 1) * 256],
                            gbuf[:, h * TILE + c * 256: h * TILE + (c + 1) * 256])
                        inc(i, "gp")
                    g2_ac[h] = cnt["gp"]
                emit_ssq(H - 1)
                SSQ_PE = cnt["pe"]
                if p2stage <= 4:
                    continue

                @block.scalar
                def _(scalar):
                    w(scalar, "pe", SSQ_PE)
                    nc.scalar.activation(rstd_rf[0:1, csl], p_ssq[0:1, csl],
                                         AF.Ln, scale=1.0 / DI, bias=epscol[0:1, 0:1])
                    nc.scalar.drain()
                    i = nc.scalar.activation(rstd_rf[0:1, csl], rstd_rf[0:1, csl],
                                             AF.Exp, scale=-0.5)
                    inc(i, "ac")
                RSTD_AC = cnt["ac"]
                rstd_prev_ac = RSTD_AC
                @block.vector
                def _(vector):
                    w(vector, "ac", RSTD_AC)
                    i = nc.vector.tensor_copy(rstd_rb[0:1, csl], rstd_rf[0:1, csl])
                    inc(i, "ve")
                RSTDB_VE = cnt["ve"]
                @block.tensor
                def _(tensor):
                    w(tensor, "ve", RSTDB_VE)
                    w(tensor, "ve", psc_free)
                    i = nc.tensor.matmul(p_sc[:, 0:256], ones_r128b[0:1, :],
                                         rstd_rb[0:1, csl], start=True, stop=True)
                    inc(i, "pe")
                RR_PE = cnt["pe"]
                @block.vector
                def _(vector):
                    w(vector, "pe", RR_PE)
                    i = nc.vector.tensor_copy(rep_rstd[:, csl], p_sc[:, 0:256])
                    inc(i, "ve")
                psc_free = cnt["ve"]

            if p2stage <= 5:
                continue
            @block.vector
            def _(vector):
                w(vector, "ve")
                for h in range(H):
                    i = nc.vector.scalar_tensor_tensor(
                        gbuf[:, h * TILE:(h + 1) * TILE],
                        gbuf[:, h * TILE:(h + 1) * TILE],
                        rmscol[:, h:h + 1], rep_rstd[:],
                        ALU.mult, ALU.mult)
                inc(i, "ve")
            GQ_VE = cnt["ve"]

            pop_free = {0: psc_free, 1: None}
            pop_sem = {0: "ve", 1: "ac"}
            for mt in range(4):
                @block.sync
                def _(sync):
                    if pst_done[mt % 2] is not None:
                        w(sync, "po", pst_done[mt % 2][1])
                    i = sync.dma_start(out=ub2[mt % 2],
                                       in_=u_view[trow(t) + mt])
                    inc(i, "sy", 16)
                    sync.wait_ge(sems["sy"], cnt["sy"])
                UB_SY = cnt["sy"]
                for ns in range(2):
                    pb = (p_sc, p_seg)[ns]
                    @block.tensor
                    def _(tensor):
                        w(tensor, "ve", GQ_VE)
                        if pop_free[ns] is not None:
                            if pop_sem[ns] == "ac":
                                w(tensor, "ac", pop_free[ns])
                            else:
                                w(tensor, "ve", pop_free[ns])
                        elif ns == 1:
                            w(tensor, "ac")
                        for h in range(H):
                            i = nc.tensor.matmul(
                                pb[:, 0:512],
                                gbuf[:, h * TILE + mt * 128:
                                     h * TILE + (mt + 1) * 128],
                                wot[:, h * 1024 + ns * 512:
                                    h * 1024 + (ns + 1) * 512],
                                start=(h == 0), stop=(h == H - 1))
                        inc(i, "pe")
                    OP_PE = cnt["pe"]
                    @block.vector
                    def _(vector):
                        w(vector, "pe", OP_PE)
                        w(vector, "sy", UB_SY)
                        if ns == 0 and pst_done[mt % 2] is not None:
                            w(vector, "po", pst_done[mt % 2][1])
                        i = nc.vector.tensor_add(
                            pstg[mt % 2][:, ns * 512:(ns + 1) * 512],
                            pb[:, 0:512],
                            ub2[mt % 2][:, ns * 512:(ns + 1) * 512])
                        inc(i, "ve")
                    pop_free[ns] = cnt["ve"]
                    pop_sem[ns] = "ve"
                PS_VE = cnt["ve"]
                @block.gpsimd
                def _(gpsimd):
                    w(gpsimd, "ve", PS_VE)
                    i = gpsimd.dma_start(
                        out=part_s.rearrange("(n p) d -> n p d", p=128)[trow(t) + mt],
                        in_=pstg[mt % 2][:])
                    inc(i, "po", 16)
                    gpsimd.wait_ge(sems["po"], cnt["po"])
                pst_done[mt % 2] = (PS_VE, cnt["po"])
            tile_res_ve = cnt["ve"]

        @block.gpsimd
        def _(gpsimd):
            w(gpsimd, "po")

    barrier()
    if maxphase <= 2:
        ctx.close()
        return nc

    with ExitStack() as stk, \
         nc.psum_tensor("p_g", [128, 512], F32) as p_g, \
         nc.psum_tensor("p_u", [128, 512], F32) as p_u, \
         nc.psum_tensor("ptrA3", [128, 512], BF16) as ptrA3, \
         nc.psum_tensor("ptrB3", [128, 512], BF16) as ptrB3:
        sb3 = lambda name, shape, dt=F32: stk.enter_context(nc.sbuf_tensor(name, shape, dt))
        wgt = sb3("wgt", [128, KD * FFI], BF16)
        wut = sb3("wut", [128, KD * FFI], BF16)
        pstk3 = ExitStack()
        sb3p = lambda name, shape, dt=F32: pstk3.enter_context(
            nc.sbuf_tensor(name, shape, dt))
        wld3 = [sb3p(f"wld3{i}", [128, D]) for i in range(2)]
        wbf3 = sb3p("wbf3", [128, D], BF16)
        qw3 = quant_scratch(pstk3, "qw3", 1024)

        @block.scalar
        def _(scalar):
            load_act_set(SET_SILU)
        emit_wquant_prep(wg_h, FFI, D, wld3, qw3, wbf3, ptrA3, ptrB3, wgt, 1024)
        emit_wquant_prep(wu_h, FFI, D, wld3, qw3, wbf3, ptrA3, ptrB3, wut, 1024)
        WQ_VE = cnt["ve"]
        pstk3.close()

        q3 = quant_scratch(stk, "q3", 1024, f16=True)
        xq = [sb3(f"xq{i}", [128, KD * TILE], BF16) for i in range(2)]
        gs = sb3("gs", [128, 512], BF16)
        hb = sb3("hb", [128, FFI])
        hqb = sb3("hqb", [128, FFI], BF16)
        hqTb = sb3("hqTb", [128, KFI * TILE], BF16)

        xq_done = [None, None]
        for t0 in range(2):
            @block.sync
            def _(sync):
                w(sync, "ve", WQ_VE)
                i = sync.dma_start(
                    out=xq[t0][:].rearrange("p (k n) -> p k n", k=KD),
                    in_=xnqT_s.rearrange("(t k p) n -> t p k n", t=NT, k=KD)[t0])
                xq_done[t0] = inc(i, "sy", 16)
                sync.wait_ge(sems["sy"], cnt["sy"])

        hqT_done = None
        pg_free = None
        pu_free = None
        for t in range(min(ntl, NT)):
            xqb = xq[t % 2]
            for mt in range(4):
                for ii, (i0, i1) in enumerate(ISLS):
                    nn = i1 - i0
                    @block.tensor
                    def _(tensor):
                        if t == 0 and mt == 0 and ii == 0:
                            w(tensor, "ve", WQ_VE)
                            w(tensor, "ac")
                        w(tensor, "sy", xq_done[t % 2])
                        if pg_free is not None:
                            w(tensor, "ac", pg_free)
                        for k in range(KD):
                            i = nc.tensor.matmul(
                                p_g[:, 0:nn],
                                xqb[:, k * TILE + mt * 128: k * TILE + (mt + 1) * 128],
                                wgt[:, k * FFI + i0: k * FFI + i1],
                                start=(k == 0), stop=(k == KD - 1))
                        inc(i, "pe")
                        if pu_free is not None:
                            w(tensor, "ve", pu_free)
                        for k in range(KD):
                            i = nc.tensor.matmul(
                                p_u[:, 0:nn],
                                xqb[:, k * TILE + mt * 128: k * TILE + (mt + 1) * 128],
                                wut[:, k * FFI + i0: k * FFI + i1],
                                start=(k == 0), stop=(k == KD - 1))
                        inc(i, "pe")
                    MM_PE = cnt["pe"]
                    @block.scalar
                    def _(scalar):
                        w(scalar, "pe", MM_PE - 1)
                        if pu_free is not None:
                            w(scalar, "ve", pu_free)
                        i = nc.scalar.activation(gs[:, 0:nn], p_g[:, 0:nn], AF.Silu)
                        inc(i, "ac")
                    GS_AC = cnt["ac"]
                    pg_free = GS_AC
                    @block.vector
                    def _(vector):
                        w(vector, "pe", MM_PE)
                        w(vector, "ac", GS_AC)
                        i = nc.vector.tensor_mul(hb[:, i0:i1], p_u[:, 0:nn],
                                                 gs[:, 0:nn])
                        inc(i, "ve")
                    HB_VE = cnt["ve"]
                    pu_free = HB_VE
                    if ii % 2 == 1:
                        lo = 1024 * (ii // 2)
                        emit_quant_full(hb[:, lo:i1], hqb[:, lo:i1], i1 - lo,
                                        q3, first_wait=("ve", HB_VE))
                HQ_VE = cnt["ve"]

                pe_extra = [("ve", HQ_VE)]
                if hqT_done is not None and mt == 0:
                    pe_extra.append(("po", hqT_done))
                blocks = [(hqb[:, k * 128:(k + 1) * 128],
                           hqTb[:, k * TILE + mt * 128: k * TILE + (mt + 1) * 128])
                          for k in range(KFI)]
                emit_transpose(blocks, ptrA3, ptrB3, identb, pe_waits=pe_extra)
            HQT_VE = cnt["ve"]

            if t + 2 < NT:
                @block.sync
                def _(sync):
                    w(sync, "pe")
                    i = sync.dma_start(
                        out=xq[t % 2][:].rearrange("p (k n) -> p k n", k=KD),
                        in_=xnqT_s.rearrange("(t k p) n -> t p k n",
                                             t=NT, k=KD)[t + 2])
                    xq_done[t % 2] = inc(i, "sy", 16)
                    sync.wait_ge(sems["sy"], cnt["sy"])

            @block.gpsimd
            def _(gpsimd):
                w(gpsimd, "ve", HQT_VE)
                i = gpsimd.dma_start(
                    out=hqT_s.rearrange("(t k p) n -> t p k n", t=NT, k=KFI)[t],
                    in_=hqTb[:].rearrange("p (k n) -> p k n", k=KFI))
                inc(i, "po", 16)
                gpsimd.wait_ge(sems["po"], cnt["po"])
            hqT_done = cnt["po"]

    barrier()
    if maxphase <= 3:
        ctx.close()
        return nc

    with ExitStack() as stk, \
         nc.psum_tensor("p_d0", [128, 512], F32) as p_d0, \
         nc.psum_tensor("p_d1", [128, 512], F32) as p_d1, \
         nc.psum_tensor("ptrA4", [128, 512], BF16) as ptrA4, \
         nc.psum_tensor("ptrB4", [128, 512], BF16) as ptrB4:
        sb4 = lambda name, shape, dt=F32: stk.enter_context(nc.sbuf_tensor(name, shape, dt))
        wdt = sb4("wdt", [128, KFI * D], BF16)
        wld4 = [sb4(f"wld4{i}", [128, FFI]) for i in range(2)]
        wbf4 = sb4("wbf4", [128, FFI], BF16)
        q4 = quant_scratch(stk, "q4", 1024)

        @block.scalar
        def _(scalar):
            load_act_set(SET_SILU)
        emit_wquant_prep(wd_h, D, FFI, wld4, q4, wbf4, ptrA4, ptrB4, wdt, 1024)
        WD_VE = cnt["ve"]

        hqt = [sb4(f"hqt{i}", [128, KFI * TILE], BF16) for i in range(2)]
        pld = sb4("pld", [128, 4 * D])
        ostg = sb4("ostg", [128, 4 * D])

        hq_done = [None, None]
        for t0 in range(2):
            @block.sync
            def _(sync):
                i = sync.dma_start(
                    out=hqt[t0][:].rearrange("p (k n) -> p k n", k=KFI),
                    in_=hqT_s.rearrange("(t k p) n -> t p k n", t=NT, k=KFI)[t0])
                hq_done[t0] = inc(i, "sy", 16)
                sync.wait_ge(sems["sy"], cnt["sy"])

        ost_po = None
        pd_free = [None, None]
        for t in range(min(ntl, NT)):
            hq_b = hqt[t % 2]
            @block.sync
            def _(sync):
                w(sync, "ve")
                i = sync.dma_start(
                    out=pld[:].rearrange("p (n d) -> p n d", n=4),
                    in_=part_s.rearrange("(n p) d -> n p d", p=128)
                    [trow(t):trow(t) + 4].rearrange("n p d -> p n d"))
                inc(i, "sy", 16)
                sync.wait_ge(sems["sy"], cnt["sy"])
            PLD_SY = cnt["sy"]

            for mt in range(4):
                for ns in range(2):
                    pb = (p_d0, p_d1)[ns]
                    @block.tensor
                    def _(tensor):
                        if t == 0 and mt == 0 and ns == 0:
                            w(tensor, "ve", WD_VE)
                        w(tensor, "sy", hq_done[t % 2])
                        if pd_free[ns] is not None:
                            w(tensor, "ve", pd_free[ns])
                        for k in range(KFI):
                            i = nc.tensor.matmul(
                                pb[:, 0:512],
                                hq_b[:, k * TILE + mt * 128:
                                     k * TILE + (mt + 1) * 128],
                                wdt[:, k * D + ns * 512: k * D + (ns + 1) * 512],
                                start=(k == 0), stop=(k == KFI - 1))
                        inc(i, "pe")
                    DN_PE = cnt["pe"]
                    @block.vector
                    def _(vector):
                        w(vector, "pe", DN_PE)
                        w(vector, "sy", PLD_SY)
                        if ost_po is not None and mt == 0 and ns == 0:
                            w(vector, "po", ost_po)
                        i = nc.vector.tensor_add(
                            ostg[:, mt * D + ns * 512: mt * D + (ns + 1) * 512],
                            pb[:, 0:512],
                            pld[:, mt * D + ns * 512: mt * D + (ns + 1) * 512])
                        inc(i, "ve")
                    pd_free[ns] = cnt["ve"]
            OST_VE = cnt["ve"]

            if t + 2 < NT:
                @block.sync
                def _(sync):
                    w(sync, "pe")
                    i = sync.dma_start(
                        out=hqt[t % 2][:].rearrange("p (k n) -> p k n", k=KFI),
                        in_=hqT_s.rearrange("(t k p) n -> t p k n",
                                            t=NT, k=KFI)[t + 2])
                    hq_done[t % 2] = inc(i, "sy", 16)
                    sync.wait_ge(sems["sy"], cnt["sy"])

            @block.gpsimd
            def _(gpsimd):
                w(gpsimd, "ve", OST_VE)
                i = gpsimd.dma_start(
                    out=out_h.rearrange("(n p) d -> n p d", p=128)
                    [trow(t):trow(t) + 4].rearrange("n p d -> p n d"),
                    in_=ostg[:].rearrange("p (n d) -> p n d", n=4))
                inc(i, "po", 16)
                gpsimd.wait_ge(sems["po"], cnt["po"])
            ost_po = cnt["po"]

        @block.gpsimd
        def _(gpsimd):
            w(gpsimd, "po")

    ctx.close()
    return nc


_HOST_CONSTS = None


def _host_consts():
    global _HOST_CONSTS
    if _HOST_CONSTS is None:
        identb = np.eye(128, dtype=BFD)
        identf = np.eye(128, dtype=np.float32)
        jj = np.arange(128)[:, None]
        ii = np.arange(256)[None, :]
        m0 = (ii >= jj).astype(BFD)
        m1 = (ii >= jj + 128).astype(BFD)
        _HOST_CONSTS = (identb, identf, m0, m1)
    return _HOST_CONSTS


def make_in_maps(inputs):
    identb, identf, m0, m1 = _host_consts()
    base = {
        "in_proj_w": np.ascontiguousarray(inputs["L0_attn_in_proj_w"]),
        "conv_w": np.ascontiguousarray(inputs["L0_attn_conv_w"]),
        "conv_b": np.ascontiguousarray(inputs["L0_attn_conv_b"]),
        "dt_bias": np.ascontiguousarray(inputs["L0_attn_dt_bias"]),
        "A_log": np.ascontiguousarray(inputs["L0_attn_A_log"]),
        "D_h": np.ascontiguousarray(inputs["L0_attn_D"]),
        "x_norm_w": np.ascontiguousarray(inputs["L0_attn_x_norm_w"]),
        "x_norm_b": np.ascontiguousarray(inputs["L0_attn_x_norm_b"]),
        "rms_w": np.ascontiguousarray(inputs["L0_attn_rms_w"]),
        "out_proj_w": np.ascontiguousarray(inputs["L0_attn_out_proj_w"]),
        "w_gate": np.ascontiguousarray(inputs["L0_ffn_w_gate"]),
        "w_up": np.ascontiguousarray(inputs["L0_ffn_w_up"]),
        "w_down": np.ascontiguousarray(inputs["L0_ffn_w_down"]),
        "ident_b": identb, "ident_f": identf,
        "mask_j0": m0, "mask_j1": m1,
    }
    in_maps = []
    u = inputs["L0_attn_u"]
    for c in range(8):
        m = dict(base)
        m["u"] = np.ascontiguousarray(u[c * SEQ:(c + 1) * SEQ].reshape(TOK, D))
        in_maps.append(m)
    return in_maps


def kernel(**inputs):
    nc = build_nc()
    in_maps = make_in_maps(inputs)
    res = run_bass_kernel_spmd(nc, in_maps, core_ids=list(range(8)))
    out = np.empty((B, L, D), dtype=np.float32)
    for c in range(8):
        out[c * SEQ:(c + 1) * SEQ] = res.results[c]["out"].reshape(SEQ, L, D)
    return out


# version 14 (final)
# speedup vs baseline: 1.0917x; 1.0141x over previous
import numpy as np
from contextlib import ExitStack

import concourse.bass as bass
import concourse.mybir as mybir
from concourse.bass_utils import run_bass_kernel_spmd

import ml_dtypes
BFD = ml_dtypes.bfloat16

F32 = mybir.dt.float32
BF16 = mybir.dt.bfloat16
F16 = mybir.dt.float16
AF = mybir.ActivationFunctionType
ALU = mybir.AluOpType
AX = mybir.AxisListType

B, L, D = 32, 2048, 1024
SEQ = 4
TOK = SEQ * L
DI = 2048
DS = 64
H = 16
CHD = 2176
F = 4240
CHUNK = 256
TILE = 512
NT = TOK // TILE
LN_EPS = 1e-5
RMS_EPS = 1e-5
NZT = DI // 128
NCT = CHD // 128
NFT = 34
KD = D // 128
FFI = 2816
KFI = FFI // 128
ISLS = [(0, 512), (512, 1024), (1024, 1536), (1536, 2048), (2048, 2560),
        (2560, 2816)]

QTHR = [0.25, 0.75, 1.25, 1.75, 2.5, 3.5, 5.0]
QWTS = [1.0, 1.0, 1.0, 1.0, 2.0, 2.0, 4.0]


def build_nc(dev_dump=(), maxphase=4, ntl=NT, p2stage=99):
    nc = bass.Bass("TRN2", target_bir_lowering=False)

    u_h = nc.dram_tensor("u", [TOK, D], F32, kind="ExternalInput").ap()
    win_h = nc.dram_tensor("in_proj_w", [F, D], F32, kind="ExternalInput").ap()
    convw_h = nc.dram_tensor("conv_w", [CHD, 4], F32, kind="ExternalInput").ap()
    convb_h = nc.dram_tensor("conv_b", [CHD], F32, kind="ExternalInput").ap()
    dtb_h = nc.dram_tensor("dt_bias", [H], F32, kind="ExternalInput").ap()
    alog_h = nc.dram_tensor("A_log", [H], F32, kind="ExternalInput").ap()
    dd_h = nc.dram_tensor("D_h", [H], F32, kind="ExternalInput").ap()
    xw_h = nc.dram_tensor("x_norm_w", [D], F32, kind="ExternalInput").ap()
    xb_h = nc.dram_tensor("x_norm_b", [D], F32, kind="ExternalInput").ap()
    rmsw_h = nc.dram_tensor("rms_w", [DI], F32, kind="ExternalInput").ap()
    wout_h = nc.dram_tensor("out_proj_w", [D, DI], F32, kind="ExternalInput").ap()
    wg_h = nc.dram_tensor("w_gate", [FFI, D], F32, kind="ExternalInput").ap()
    wu_h = nc.dram_tensor("w_up", [FFI, D], F32, kind="ExternalInput").ap()
    wd_h = nc.dram_tensor("w_down", [D, FFI], F32, kind="ExternalInput").ap()
    identb_h = nc.dram_tensor("ident_b", [128, 128], BF16, kind="ExternalInput").ap()
    identf_h = nc.dram_tensor("ident_f", [128, 128], F32, kind="ExternalInput").ap()
    mask0_h = nc.dram_tensor("mask_j0", [128, 256], BF16, kind="ExternalInput").ap()
    mask1_h = nc.dram_tensor("mask_j1", [128, 256], BF16, kind="ExternalInput").ap()

    out_h = nc.dram_tensor("out", [TOK, D], F32, kind="ExternalOutput").ap()

    def dscratch(name, shape, dt):
        kind = "ExternalOutput" if name in dev_dump else "Internal"
        return nc.dram_tensor(name, shape, dt, kind=kind).ap()

    xnqT_s = dscratch("s_xnqT", [NT * KD * 128, TILE], BF16)
    siluz_s = dscratch("s_siluz", [NT * NZT * 128, TILE], BF16)
    xbc_s = dscratch("s_xbc", [SEQ * NCT * 128, L + 3], BF16)
    dt_s = dscratch("s_dt", [NT * H, TILE], F32)
    part_s = dscratch("s_part", [TOK, D], F32)
    hqT_s = dscratch("s_hqT", [NT * KFI * 128, TILE], BF16)

    ctx = ExitStack()
    sbt = lambda name, shape, dt=F32: ctx.enter_context(nc.sbuf_tensor(name, shape, dt))

    SEMS = ["sy", "po", "pe", "ve", "ac", "gp"]
    sems = {s: ctx.enter_context(nc.semaphore(s)) for s in SEMS}
    cnt = {s: 0 for s in SEMS}

    def inc(instr, s, n=1):
        instr.then_inc(sems[s], n)
        cnt[s] += n
        return cnt[s]

    def w(eng, s, val=None):
        v = cnt[s] if val is None else val
        if v > 0:
            eng.wait_ge(sems[s], v)

    block = ctx.enter_context(nc.Block())

    from concourse.hw_specs import get_activation_tables
    _tabs = list(get_activation_tables(nc.m.arch).keys())
    SET_NLE = _tabs.index("natural_log_exp_and_others")
    SET_SILU = _tabs.index("silu_and_others")

    def load_act_set(sid):
        nc.scalar.add_instruction(mybir.InstLoadActFuncSet(
            name=nc.get_next_instruction_name(), act_func_set_id=sid,
            ins=[], outs=[]))

    def barrier():
        snap = dict(cnt)
        for name in ["sync", "gpsimd", "tensor", "vector", "scalar"]:
            dec = getattr(block, name)
            @dec
            def _(eng):
                for s in SEMS:
                    if snap[s]:
                        eng.wait_ge(sems[s], snap[s])

    identb = sbt("identb", [128, 128], BF16)
    identf = sbt("identf", [128, 128], F32)
    masks = [sbt("mask0", [128, 256], BF16), sbt("mask1", [128, 256], BF16)]
    ones_r128b = sbt("ones_r128b", [1, 128], BF16)
    ones_r128f = sbt("ones_r128f", [1, 128], F32)
    ones_r512b = sbt("ones_r512b", [1, TILE], BF16)
    ones_c128b = sbt("ones_c128b", [128, 1], BF16)
    convw = sbt("convw", [128, NCT * 4], F32)
    convb = sbt("convb", [128, NCT], F32)
    dtbcol = sbt("dtbcol", [16, 1], F32)
    acol = sbt("acol", [16, 1], F32)
    rmscol = sbt("rmscol", [128, H], F32)
    i_d = sbt("i_d", [128, H * 128], BF16)
    xwcol = sbt("xwcol", [128, KD], F32)
    xbcolT = sbt("xbcolT", [128, KD], F32)
    xbcolT_b = sbt("xbcolT_b", [128, KD], BF16)
    carryA = [sbt(f"carryA{s}", [128, 8 * 128], F32) for s in range(SEQ)]
    carry_b = sbt("carry_b", [128, 8 * 128], BF16)
    drow = sbt("drow", [1, H], F32)
    zero3 = sbt("zero3", [128, 4], BF16)
    epscol = sbt("epscol", [128, 1], F32)

    def cslc(h):
        hb = 0 if h < 8 else 64
        hc = (h % 8) * 128
        return hb, hc

    @block.sync
    def _(sync):
        for dst, src, slow in [
            (identb, identb_h, False), (identf, identf_h, False),
            (masks[0], mask0_h, False), (masks[1], mask1_h, False),
            (convw[:].rearrange("p (c k) -> p c k", k=4),
             convw_h.rearrange("(c p) k -> p c k", p=128), False),
            (convb, convb_h.rearrange("(c p) -> p c", p=128), True),
            (dtbcol, dtb_h.unsqueeze(1), False),
            (acol, alog_h.unsqueeze(1), False),
            (rmscol, rmsw_h.rearrange("(h p) -> p h", p=128), True),
            (xwcol, xw_h.rearrange("(k p) -> p k", p=128), True),
            (xbcolT, xb_h.rearrange("(k p) -> p k", p=128), True),
            (drow, dd_h.unsqueeze(0), False),
        ]:
            i = sync.dma_start(out=dst[:], in_=src, allow_slow_non_contiguous=slow)
            inc(i, "sy", 16)
        sync.wait_ge(sems["sy"], cnt["sy"])
    SMALL_LD = cnt["sy"]

    @block.vector
    def _(vector):
        nc.vector.memset(ones_r128b[:], 1.0)
        nc.vector.memset(ones_r128f[:], 1.0)
        nc.vector.memset(ones_r512b[:], 1.0)
        nc.vector.memset(ones_c128b[:], 1.0)
        nc.vector.memset(zero3[:], 0.0)
        nc.vector.memset(epscol[:], LN_EPS)
        for s in range(SEQ):
            nc.vector.memset(carryA[s][:], 0.0)
        nc.vector.memset(carry_b[:], 0.0)
        w(vector, "sy", SMALL_LD)
        i = nc.vector.tensor_copy(xbcolT_b[:], xbcolT[:])
        inc(i, "ve")
    CONSTS_VE = cnt["ve"]

    @block.scalar
    def _(scalar):
        w(scalar, "sy", SMALL_LD)
        load_act_set(SET_NLE)
        i = nc.scalar.activation(acol[:], acol[:], AF.Exp)
        inc(i, "ac")
    ACOL_AC = cnt["ac"]

    with nc.psum_tensor("dcols_p", [128, H], F32) as dcols_p:
        dcols = sbt("dcols", [128, H], F32)

        @block.tensor
        def _(tensor):
            w(tensor, "sy", SMALL_LD)
            w(tensor, "ve", CONSTS_VE)
            i = nc.tensor.matmul(dcols_p[:], ones_r128f[0:1, :], drow[:],
                                 start=True, stop=True)
            inc(i, "pe")

        @block.vector
        def _(vector):
            w(vector, "pe")
            w(vector, "ac", ACOL_AC)
            nc.vector.tensor_scalar_mul(acol[:], acol[:], -1.0)
            nc.vector.tensor_copy(dcols[:], dcols_p[:])
            nc.vector.drain()
            for h in range(H):
                i = nc.vector.tensor_scalar_mul(i_d[:, h * 128:(h + 1) * 128],
                                                identb[:], dcols[:, h:h + 1])
            inc(i, "ve")

    @block.gpsimd
    def _(gpsimd):
        w(gpsimd, "ve", CONSTS_VE)
        xbv = xbc_s.rearrange("(s c p) l -> s c p l", s=SEQ, c=NCT)
        for s in range(SEQ):
            for c in range(NCT):
                i = gpsimd.dma_start(out=xbv[s, c, :, 0:3], in_=zero3[:, 0:3])
                inc(i, "po", 16)
        gpsimd.wait_ge(sems["po"], cnt["po"])

    def quant_scratch(stk, tag, n, f16=False):
        sb = lambda nm, shape, dt=F32: stk.enter_context(nc.sbuf_tensor(nm, shape, dt))
        nb = n // 16
        ddt = F16 if f16 else F32
        d = {
            "amax": sb(f"{tag}_amax", [128, nb]), "scl": sb(f"{tag}_scl", [128, nb]),
            "absx": sb(f"{tag}_absx", [128, n], ddt),
            "sg": sb(f"{tag}_sg", [128, n], F16 if f16 else BF16),
            "aa": sb(f"{tag}_aa", [128, n], ddt),
            "s1": sb(f"{tag}_s1", [128, n], F16 if f16 else BF16),
            "s2": sb(f"{tag}_s2", [128, n], F16 if f16 else BF16),
            "cc": sb(f"{tag}_cc", [128, n], F16 if f16 else BF16),
            "cd": sb(f"{tag}_cd", [128, n], F16 if f16 else BF16),
            "c5": sb(f"{tag}_c5", [128, n], F16 if f16 else BF16),
            "c6": sb(f"{tag}_c6", [128, n], F16 if f16 else BF16),
            "ss": sb(f"{tag}_ss", [128, n], F16 if f16 else BF16),
            "rcp16": sb(f"{tag}_rcp16", [128, nb], F16 if f16 else F32),
            "sclh16": sb(f"{tag}_sclh16", [128, nb], F16 if f16 else F32),
        }
        return d

    def emit_quant_full(src, dst, n, q, first_wait=None):
        nb = n // 16
        assert n % 32 == 0
        hn, hb = n // 2, nb // 2
        CS = [slice(0, hn), slice(hn, n)]
        GS = [slice(0, hb), slice(hb, nb)]
        @block.scalar
        def _(scalar):
            if first_wait is not None:
                w(scalar, first_wait[0], first_wait[1])
            if q.get("prev") is not None:
                w(scalar, "ve", q["prev_absx"])
                w(scalar, "gp", q["prev_sg"])
            else:
                w(scalar, "ve")
                w(scalar, "gp")
            nc.scalar.activation(q["absx"][:, 0:n], src, AF.Abs)
            i = nc.scalar.activation(q["sg"][:, 0:n], src, AF.Sign)
            inc(i, "ac")
        AC = cnt["ac"]
        gp0 = cnt["gp"]
        gp_ss = [gp0 + 1, gp0 + 2]
        gp_ad = [gp0 + 5, gp0 + 6]
        mk = {}

        @block.vector
        def _(vector):
            def vw(v):
                if v:
                    nc.vector.wait_ge(sems["ve"], v)
            def vop(key, i, instr):
                inc(instr, "ve")
                mk[(key, i)] = cnt["ve"]
            nc.vector.wait_ge(sems["ac"], AC)
            if q.get("prev") is not None:
                vw(q["prev"])
                nc.vector.wait_ge(sems["gp"], q["prev_gp"])
            for i in (0, 1):
                vop("mx", i, nc.vector.reduce_max(
                    q["amax"][:, GS[i]],
                    src[:, CS[i]].rearrange("p (a b) -> p a b", b=16),
                    axis=AX.X, apply_absolute_value=True))
            for i in (0, 1):
                vw(mk[("mx", i)])
                vop("rc", i, nc.vector.reciprocal(q["scl"][:, GS[i]],
                                                  q["amax"][:, GS[i]]))
            for i in (0, 1):
                vw(mk[("rc", i)])
                vop("r6", i, nc.vector.tensor_scalar(
                    q["rcp16"][:, GS[i]], q["scl"][:, GS[i]], 6.0, 1e12,
                    ALU.mult, ALU.min))
                vop("sh", i, nc.vector.tensor_scalar(
                    q["sclh16"][:, GS[i]], q["amax"][:, GS[i]], 1.0 / 12.0,
                    5e-13, ALU.mult, ALU.max))
            for i in (0, 1):
                vw(mk[("r6", i)])
                vop("aa", i, nc.vector.tensor_mul(
                    q["aa"][:, CS[i]], q["absx"][:, CS[i]],
                    q["rcp16"][:, GS[i]].unsqueeze(2)
                    .broadcast_to([128, hb, 16])))
            for i in (0, 1):
                vw(mk[("aa", i)])
                vop("t0", i, nc.vector.tensor_scalar(
                    q["s1"][:, CS[i]], q["aa"][:, CS[i]], QTHR[0], 1.0,
                    ALU.is_gt, ALU.mult))
                vop("c0", i, nc.vector.tensor_scalar(
                    q["cc"][:, CS[i]], q["aa"][:, CS[i]], QTHR[1], 1.0,
                    ALU.is_gt, ALU.mult))
            for i in (0, 1):
                vw(mk[("t0", i)])
                vw(mk[("c0", i)])
                vop("t1", i, nc.vector.tensor_add(
                    q["s1"][:, CS[i]], q["s1"][:, CS[i]], q["cc"][:, CS[i]]))
                vop("c1", i, nc.vector.tensor_scalar(
                    q["cd"][:, CS[i]], q["aa"][:, CS[i]], QTHR[2], 1.0,
                    ALU.is_gt, ALU.mult))
            for i in (0, 1):
                vw(mk[("t1", i)])
                vw(mk[("c1", i)])
                vop("t2", i, nc.vector.tensor_add(
                    q["s1"][:, CS[i]], q["s1"][:, CS[i]], q["cd"][:, CS[i]]))
                vop("c2", i, nc.vector.tensor_scalar(
                    q["cc"][:, CS[i]], q["aa"][:, CS[i]], QTHR[3], 1.0,
                    ALU.is_gt, ALU.mult))
            for i in (0, 1):
                vw(mk[("t2", i)])
                vw(mk[("c2", i)])
                vop("t3", i, nc.vector.tensor_add(
                    q["s1"][:, CS[i]], q["s1"][:, CS[i]], q["cc"][:, CS[i]]))
                vop("u0", i, nc.vector.tensor_scalar(
                    q["s2"][:, CS[i]], q["aa"][:, CS[i]], QTHR[4], 2.0,
                    ALU.is_gt, ALU.mult))
                vop("u1", i, nc.vector.tensor_scalar(
                    q["c5"][:, CS[i]], q["aa"][:, CS[i]], QTHR[5], 2.0,
                    ALU.is_gt, ALU.mult))
                vop("u2", i, nc.vector.tensor_scalar(
                    q["c6"][:, CS[i]], q["aa"][:, CS[i]], QTHR[6], 4.0,
                    ALU.is_gt, ALU.mult))
            for i in (0, 1):
                vw(mk[("u1", i)])
                vop("v1", i, nc.vector.tensor_add(
                    q["s2"][:, CS[i]], q["s2"][:, CS[i]], q["c5"][:, CS[i]]))
            for i in (0, 1):
                vw(mk[("t3", i)])
                vw(mk[("v1", i)])
                vop("v2", i, nc.vector.tensor_add(
                    q["s1"][:, CS[i]], q["s1"][:, CS[i]], q["s2"][:, CS[i]]))
            for i in (0, 1):
                vw(mk[("u2", i)])
                vw(mk[("v2", i)])
                vop("fin", i, nc.vector.tensor_add(
                    q["s1"][:, CS[i]], q["s1"][:, CS[i]], q["c6"][:, CS[i]]))
            for i in (0, 1):
                vw(mk[("fin", i)])
                nc.vector.wait_ge(sems["gp"], gp_ss[i])
                vop("ds", i, nc.vector.tensor_mul(
                    dst[:, CS[i]], q["s1"][:, CS[i]], q["ss"][:, CS[i]]))

        @block.gpsimd
        def _(gpsimd):
            w(gpsimd, "ac", AC)
            if q.get("prev") is not None:
                w(gpsimd, "ve", q["prev"])
            for i in (0, 1):
                gpsimd.wait_ge(sems["ve"], mk[("sh", i)])
                ii = nc.gpsimd.tensor_mul(
                    q["ss"][:, CS[i]], q["sg"][:, CS[i]],
                    q["sclh16"][:, GS[i]].unsqueeze(2)
                    .broadcast_to([128, hb, 16]))
                inc(ii, "gp")
        assert cnt["gp"] == gp_ss[1]
        q["prev"] = cnt["ve"]
        q["prev_gp"] = cnt["gp"]
        q["prev_absx"] = mk[("aa", 1)]
        q["prev_sg"] = gp_ss[1]
        return cnt["ve"]

    def emit_transpose(blocks, bankA, bankB, ident_t, rows_in=128, pe_waits=()):
        same = bankA is bankB
        ngr = (len(blocks) + 3) // 4
        ev_done = [None, None]
        for g in range(ngr):
            bank = (bankA, bankB)[g % 2]
            cur = blocks[g * 4:(g + 1) * 4]
            @block.tensor
            def _(tensor):
                if g == 0:
                    for s_, v_ in pe_waits:
                        w(tensor, s_, v_)
                    w(tensor, "ve")
                guard = ev_done[(g - 1) % 2] if same else ev_done[g % 2]
                if guard is not None:
                    w(tensor, "ve", guard)
                for j, (srcap, dstap) in enumerate(cur):
                    i = nc.tensor.transpose(
                        bank[:, j * rows_in:(j + 1) * rows_in], srcap,
                        ident_t[0:rows_in, 0:rows_in])
                inc(i, "pe")
            PE_V = cnt["pe"]
            @block.vector
            def _(vector):
                w(vector, "pe", PE_V)
                for j, (srcap, dstap) in enumerate(cur):
                    i = nc.vector.tensor_copy(
                        dstap, bank[0:dstap.shape[0],
                                    j * rows_in: j * rows_in + dstap.shape[1]])
                inc(i, "ve")
            ev_done[g % 2] = cnt["ve"]
        return cnt["ve"]

    def emit_l2norm_prep(w_hbm, n_rows, ncols, wld, wsq, wssq, wrn, wbf,
                         bankA, bankB, dstT, scale_const):
        kk = ncols // 128
        n_tiles = (n_rows + 127) // 128
        buf_free = [None, None]
        for wt in range(n_tiles):
            rows = min(128, n_rows - wt * 128)
            buf = wld[wt % 2]
            @block.sync
            def _(sync):
                if wt < 2:
                    w(sync, "ve")
                    w(sync, "ac")
                if buf_free[wt % 2] is not None:
                    w(sync, "ve", buf_free[wt % 2])
                i = sync.dma_start(out=buf[0:rows, 0:ncols],
                                   in_=w_hbm[wt * 128: wt * 128 + rows, :])
                inc(i, "sy", 16)
                sync.wait_ge(sems["sy"], cnt["sy"])
            @block.scalar
            def _(scalar):
                w(scalar, "sy")
                w(scalar, "ve")
                nc.scalar.activation(wsq[0:rows, 0:ncols], buf[0:rows, 0:ncols],
                                     AF.Square, accum_out=wssq[0:rows, :])
                nc.scalar.drain()
                nc.scalar.activation(wrn[0:rows, :], wssq[0:rows, :], AF.Ln)
                nc.scalar.drain()
                i = nc.scalar.activation(wrn[0:rows, :], wrn[0:rows, :], AF.Exp,
                                         scale=-0.5)
                inc(i, "ac")
            @block.vector
            def _(vector):
                w(vector, "ac")
                w(vector, "pe")
                nc.vector.tensor_scalar(wrn[0:rows, :], wrn[0:rows, :], 1e6,
                                        scale_const, ALU.min, ALU.mult)
                nc.vector.drain()
                i = nc.vector.tensor_scalar_mul(wbf[0:rows, 0:ncols],
                                                buf[0:rows, 0:ncols],
                                                wrn[0:rows, 0:1])
                inc(i, "ve")
            buf_free[wt % 2] = cnt["ve"]
            blocks = [(wbf[0:rows, k * 128:(k + 1) * 128],
                       dstT[:, k * n_rows + wt * 128: k * n_rows + wt * 128 + rows])
                      for k in range(kk)]
            emit_transpose(blocks, bankA, bankB, identb, rows_in=rows,
                           pe_waits=[("ve", buf_free[wt % 2])])

    def emit_wquant_prep(w_hbm, n_rows, ncols, wld, qsc, wbf, bankA, bankB,
                         dstT, qn):
        kk = ncols // 128
        n_tiles = (n_rows + 127) // 128
        buf_free = [None, None]
        for wt in range(n_tiles):
            rows = min(128, n_rows - wt * 128)
            buf = wld[wt % 2]
            @block.sync
            def _(sync):
                if wt < 2:
                    w(sync, "ve")
                    w(sync, "ac")
                if buf_free[wt % 2] is not None:
                    w(sync, "ve", buf_free[wt % 2])
                i = sync.dma_start(out=buf[0:rows, 0:ncols],
                                   in_=w_hbm[wt * 128: wt * 128 + rows, :])
                inc(i, "sy", 16)
                sync.wait_ge(sems["sy"], cnt["sy"])
            ld_done = cnt["sy"]
            @block.vector
            def _(vector):
                w(vector, "pe")
                i = nc.vector.engine_nop()
                inc(i, "ve")
            for off in range(0, ncols, qn):
                nn = min(qn, ncols - off)
                emit_quant_full(buf[0:128, off:off + nn],
                                wbf[0:128, off:off + nn], nn, qsc,
                                first_wait=("sy", ld_done))
            buf_free[wt % 2] = cnt["ve"]
            blocks = [(wbf[0:rows, k * 128:(k + 1) * 128],
                       dstT[:, k * n_rows + wt * 128: k * n_rows + wt * 128 + rows])
                      for k in range(kk)]
            emit_transpose(blocks, bankA, bankB, identb, rows_in=rows,
                           pe_waits=[("ve", buf_free[wt % 2])])

    u_view = u_h.rearrange("(n p) d -> n p d", p=128)

    def trow(t):
        return (t % 4) * 16 + (t // 4) * 4

    with ExitStack() as stk, \
         nc.psum_tensor("ptrA1", [128, 512], BF16) as ptrA, \
         nc.psum_tensor("ptrB1", [128, 512], BF16) as ptrB, \
         nc.psum_tensor("pmm0", [128, TILE], F32) as pmm0, \
         nc.psum_tensor("pmm1", [128, TILE], F32) as pmm1, \
         nc.psum_tensor("pmm2", [128, TILE], F32) as pmm2, \
         nc.psum_tensor("pcr", [1, 512], F32) as pcr:
        sb1 = lambda name, shape, dt=F32: stk.enter_context(nc.sbuf_tensor(name, shape, dt))
        wnt = sb1("wnt", [128, KD * F], BF16)
        constrow = sb1("constrow", [1, F], F32)
        constrow_b = sb1("constrow_b", [1, F], BF16)
        pstk = ExitStack()
        sb1p = lambda name, shape, dt=F32: pstk.enter_context(
            nc.sbuf_tensor(name, shape, dt))
        wld = [sb1p(f"wld{i}", [128, D]) for i in range(2)]
        wsq = sb1p("wsq", [128, D])
        wssq = sb1p("wssq", [128, 1])
        wrn = sb1p("wrn", [128, 1])
        wbf = sb1p("wbf", [128, D], BF16)

        emit_l2norm_prep(win_h, F, D, wld, wsq, wssq, wrn, wbf, ptrA, ptrB,
                         wnt, D ** -0.5)

        for nsl in range(9):
            n0, n1 = nsl * 512, min(F, nsl * 512 + 512)
            @block.tensor
            def _(tensor):
                w(tensor, "ve")
                for k in range(KD):
                    i = nc.tensor.matmul(pcr[0:1, 0:n1 - n0], xbcolT_b[:, k:k + 1],
                                         wnt[:, k * F + n0: k * F + n1],
                                         start=(k == 0), stop=(k == KD - 1))
                inc(i, "pe")
            CR_PE = cnt["pe"]
            @block.vector
            def _(vector):
                w(vector, "pe", CR_PE)
                i = nc.vector.tensor_copy(constrow[0:1, n0:n1], pcr[0:1, 0:n1 - n0])
                inc(i, "ve")
        @block.vector
        def _(vector):
            w(vector, "ve")
            nc.vector.tensor_copy(constrow_b[:], constrow[:])
            for k in range(KD):
                i = nc.vector.tensor_scalar_mul(wnt[:, k * F:(k + 1) * F],
                                                wnt[:, k * F:(k + 1) * F],
                                                xwcol[:, k:k + 1])
            inc(i, "ve")
        WNT_VE = cnt["ve"]
        pstk.close()

        ubuf = sb1("ubuf", [128, 4 * D])
        xn = sb1("xn", [128, 4 * D])
        xh = sb1("xh", [128, 4 * D], BF16)
        xhT = sb1("xhT", [128, KD * TILE], BF16)
        qd = sb1("qd", [128, 4096], BF16)
        xnqT = sb1("xnqT", [128, KD * TILE], BF16)
        su = sb1("su", [128, 4])
        ssq4 = sb1("ssq4", [128, 4])
        mean4 = sb1("mean4", [128, 4])
        msq4 = sb1("msq4", [128, 4])
        var4 = sb1("var4", [128, 4])
        rstd4 = sb1("rstd4", [128, 4])
        bia4 = sb1("bia4", [128, 4])
        q1 = quant_scratch(stk, "q1", 1024, f16=True)
        stg_ev = [sb1(f"stg_ev{i}", [128, TILE], BF16) for i in range(2)]
        dtstg = sb1("dtstg", [16, TILE])

        u_done = [None] * (NT + 1)
        @block.sync
        def _(sync):
            w(sync, "ve", WNT_VE)
            i = sync.dma_start(
                out=ubuf[:].rearrange("p (n d) -> p n d", n=4),
                in_=u_view[trow(0):trow(0) + 4, :, :].rearrange("n p d -> p n d"))
            u_done[0] = inc(i, "sy", 16)
            sync.wait_ge(sems["sy"], cnt["sy"])

        def emit_ln_dbl(src_buf, xn_dst, xh_dst, waits):
            @block.scalar
            def _(scalar):
                for s_, v_ in waits:
                    w(scalar, s_, v_)
                w(scalar, "ac")
                w(scalar, "ve")
                for mt in range(4):
                    sl = src_buf[:, mt * D:(mt + 1) * D]
                    nc.scalar.activation(xn_dst[:, mt * D:(mt + 1) * D], sl,
                                         AF.Copy, accum_out=su[:, mt:mt + 1])
                    i = nc.scalar.activation(xh_dst[:, mt * D:(mt + 1) * D],
                                             sl, AF.Square,
                                             accum_out=ssq4[:, mt:mt + 1])
                inc(i, "ac")
            ACC_AC = cnt["ac"]
            @block.vector
            def _(vector):
                w(vector, "ac", ACC_AC)
                w(vector, "ve")
                nc.vector.tensor_scalar_mul(mean4[:], su[:], 1.0 / D)
                nc.vector.drain()
                nc.vector.tensor_mul(msq4[:], mean4[:], mean4[:])
                nc.vector.drain()
                i = nc.vector.scalar_tensor_tensor(var4[:], ssq4[:], 1.0 / D,
                                                   msq4[:], ALU.mult, ALU.subtract)
                inc(i, "ve")
            @block.scalar
            def _(scalar):
                w(scalar, "ve")
                nc.scalar.activation(rstd4[:], var4[:], AF.Ln, bias=epscol[:, 0:1])
                nc.scalar.drain()
                i = nc.scalar.activation(rstd4[:], rstd4[:], AF.Exp, scale=-0.5)
                inc(i, "ac")
            @block.vector
            def _(vector):
                w(vector, "ac")
                nc.vector.tensor_mul(msq4[:], var4[:], rstd4[:])
                nc.vector.drain()
                i = nc.vector.tensor_mul(msq4[:], msq4[:], rstd4[:])
                inc(i, "ve")
            @block.scalar
            def _(scalar):
                w(scalar, "ve")
                nc.scalar.activation(su[:], msq4[:], AF.Ln, bias=epscol[:, 0:1])
                nc.scalar.drain()
                i = nc.scalar.activation(su[:], su[:], AF.Exp, scale=-0.5)
                inc(i, "ac")
            @block.vector
            def _(vector):
                w(vector, "ac")
                w(vector, "ve")
                nc.vector.tensor_mul(var4[:], rstd4[:], su[:])
                nc.vector.scalar_tensor_tensor(bia4[:], mean4[:], -1.0,
                                               rstd4[:], ALU.mult, ALU.mult)
                nc.vector.drain()
                nc.vector.scalar_tensor_tensor(ssq4[:], mean4[:], -1.0,
                                               var4[:], ALU.mult, ALU.mult)
                nc.vector.drain()
                for mt in range(4):
                    nc.vector.tensor_scalar(xn_dst[:, mt * D:(mt + 1) * D],
                                            src_buf[:, mt * D:(mt + 1) * D],
                                            rstd4[:, mt:mt + 1],
                                            bia4[:, mt:mt + 1],
                                            ALU.mult, ALU.add)
                    i = nc.vector.tensor_scalar(xh_dst[:, mt * D:(mt + 1) * D],
                                                src_buf[:, mt * D:(mt + 1) * D],
                                                var4[:, mt:mt + 1],
                                                ssq4[:, mt:mt + 1],
                                                ALU.mult, ALU.add)
                inc(i, "ve")
            return ("ve", cnt["ve"])

        pm_free = [None, None, None]
        pm_free_sem = [None, None, None]
        stg_free = [None] * 2
        prev_qsg_ac = None
        prev_xht_pe = None
        prev_xnqt_po = None

        for t in range(ntl):
            s_seq, cpair = t % 4, t // 4

            lnw = [("sy", u_done[t]), ("ve", WNT_VE)]
            if t > 0:
                lnw += [("ac", prev_qsg_ac), ("pe", prev_xht_pe),
                        ("po", prev_xnqt_po)]
            XN_M = emit_ln_dbl(ubuf, xn, xh, lnw)
            XH_M = XN_M

            if t + 1 < NT:
                @block.sync
                def _(sync):
                    w(sync, XN_M[0], XN_M[1])
                    i = sync.dma_start(
                        out=ubuf[:].rearrange("p (n d) -> p n d", n=4),
                        in_=u_view[trow(t + 1):trow(t + 1) + 4, :, :]
                        .rearrange("n p d -> p n d"))
                    u_done[t + 1] = inc(i, "sy", 16)
                    sync.wait_ge(sems["sy"], cnt["sy"])

            blocks = [(xh[:, mt * D + k * 128: mt * D + (k + 1) * 128],
                       xhT[:, k * TILE + mt * 128: k * TILE + (mt + 1) * 128])
                      for mt in range(4) for k in range(KD)]
            XHT_VE = emit_transpose(blocks, ptrA, ptrB, identb,
                                    pe_waits=[XH_M])
            prev_xht_pe = cnt["pe"]
            qtrans = []
            for sl4 in range(4):
                QS_VE = emit_quant_full(xn[:, sl4 * 1024:(sl4 + 1) * 1024],
                                        qd[:, sl4 * 1024:(sl4 + 1) * 1024],
                                        1024, q1, first_wait=XN_M)
                if sl4 == 3:
                    prev_qsg_ac = cnt["ac"]
                mt = sl4
                blocks = [(qd[:, sl4 * 1024 + j * 128:
                              sl4 * 1024 + (j + 1) * 128],
                           xnqT[:, j * TILE + mt * 128:
                                j * TILE + (mt + 1) * 128])
                          for j in range(8)]
                qtrans.append((blocks, QS_VE))

            xbv = xbc_s.rearrange("(s c p) l -> s c p l", s=SEQ, c=NCT)
            szv = siluz_s.rearrange("(t z p) n -> t z p n", t=NT, z=NZT)
            for ft in range(NFT):
                rows = 128 if ft < 33 else 16
                pm = (pmm0, pmm1, pmm2)[ft % 3]
                @block.tensor
                def _(tensor):
                    if ft == 0:
                        w(tensor, "ve", XHT_VE)
                    if pm_free[ft % 3] is not None:
                        w(tensor, pm_free_sem[ft % 3], pm_free[ft % 3])
                    for k in range(KD):
                        nc.tensor.matmul(
                            pm[0:rows, :],
                            wnt[:, k * F + ft * 128: k * F + ft * 128 + rows],
                            xhT[:, k * TILE:(k + 1) * TILE],
                            start=(k == 0), stop=False)
                    i = nc.tensor.matmul(pm[0:rows, :],
                                         constrow_b[0:1, ft * 128: ft * 128 + rows],
                                         ones_r512b[:], start=False, stop=True)
                    inc(i, "pe")
                MM_PE = cnt["pe"]
                if ft < 33:
                    sbuf_ev = stg_ev[ft % 2]
                    @block.scalar
                    def _(scalar):
                        w(scalar, "pe", MM_PE)
                        if stg_free[ft % 2] is not None:
                            w(scalar, stg_free[ft % 2][0], stg_free[ft % 2][1])
                        i = nc.scalar.activation(sbuf_ev[:], pm[:], AF.Copy)
                        inc(i, "ac")
                    EV_AC = cnt["ac"]
                    pm_free[ft % 3], pm_free_sem[ft % 3] = EV_AC, "ac"
                    if ft < 16:
                        @block.gpsimd
                        def _(gpsimd):
                            w(gpsimd, "ac", EV_AC)
                            i = gpsimd.dma_start(out=szv[t, ft], in_=sbuf_ev[:])
                            inc(i, "po", 16)
                            gpsimd.wait_ge(sems["po"], cnt["po"])
                        stg_free[ft % 2] = ("po", cnt["po"])
                    else:
                        ct = ft - 16
                        @block.sync
                        def _(sync):
                            w(sync, "ac", EV_AC)
                            i = sync.dma_start(
                                out=xbv[s_seq, ct, :,
                                        3 + cpair * 512: 3 + (cpair + 1) * 512],
                                in_=sbuf_ev[:])
                            inc(i, "sy", 16)
                            sync.wait_ge(sems["sy"], cnt["sy"])
                        stg_free[ft % 2] = ("sy", cnt["sy"])
                else:
                    @block.vector
                    def _(vector):
                        w(vector, "pe", MM_PE)
                        w(vector, "po")
                        i = nc.vector.tensor_copy(dtstg[:], pm[0:16, :])
                        inc(i, "ve")
                    DT_VE = cnt["ve"]
                    pm_free[ft % 3], pm_free_sem[ft % 3] = DT_VE, "ve"
                    @block.gpsimd
                    def _(gpsimd):
                        w(gpsimd, "ve", DT_VE)
                        i = gpsimd.dma_start(out=dt_s[t * 16:(t + 1) * 16, :],
                                             in_=dtstg[:])
                        inc(i, "po", 16)
                        gpsimd.wait_ge(sems["po"], cnt["po"])

            for blocks, qsv in qtrans:
                emit_transpose(blocks, ptrA, ptrB, identb,
                               pe_waits=[("ve", qsv)])
            XNQT_VE = cnt["ve"]

            @block.gpsimd
            def _(gpsimd):
                w(gpsimd, "ve", XNQT_VE)
                i = gpsimd.dma_start(
                    out=xnqT_s.rearrange("(t k p) n -> t p k n", t=NT, k=KD)[t],
                    in_=xnqT[:].rearrange("p (k n) -> p k n", k=KD))
                inc(i, "po", 16)
                gpsimd.wait_ge(sems["po"], cnt["po"])
            prev_xnqt_po = cnt["po"]

        @block.gpsimd
        def _(gpsimd):
            w(gpsimd, "po")

    barrier()
    if maxphase <= 1:
        ctx.close()
        return nc

    with ExitStack() as stk, \
         nc.psum_tensor("p_sc", [128, 512], F32) as p_sc, \
         nc.psum_tensor("p_seg", [128, 512], F32) as p_seg, \
         nc.psum_tensor("p_y0", [128, 512], F32) as p_y0, \
         nc.psum_tensor("p_y1", [128, 512], F32) as p_y1, \
         nc.psum_tensor("p_st", [128, 512], F32) as p_st, \
         nc.psum_tensor("p_ssq", [128, 512], F32) as p_ssq, \
         nc.psum_tensor("p_rep", [128, 512], F32) as p_rep, \
         nc.psum_tensor("ptr2", [128, 512], BF16) as ptr2:
        sb2 = lambda name, shape, dt=F32: stk.enter_context(nc.sbuf_tensor(name, shape, dt))
        wot = sb2("wot", [128, 16 * 1024], BF16)
        accA = sb2("accA", [128, 4 * TILE])
        accB = sb2("accB", [128, 4 * TILE])
        diagb = sb2("diagb", [128, 2 * 512], BF16)
        xbcw = sb2("xbcw", [128, NCT * 515], BF16)
        x_j = sb2("x_j", [128, 4 * DI], BF16)
        zld = sb2("zld", [128, NZT * TILE], BF16)
        bdec = sb2("bdec", [128, 4 * 1024], BF16)
        wssq2 = sb2("wssq2", [128, 1])
        wrn2 = sb2("wrn2", [128, 1])

        ubp = sb2("ubp", [128, DI])
        emit_l2norm_prep(wout_h, D, DI, [accA, accB], ubp, wssq2, wrn2,
                         x_j, ptr2, ptr2, wot, 1.0)
        WOT_VE = cnt["ve"]

        dtr = accA[0:16, 0:512]
        sp_t1 = accA[0:16, 512:1024]
        sp_t2 = accA[0:16, 1024:1536]
        dt_ht = accA[0:16, 1536:2048]
        ldt_t = accB[0:16, 0:512]
        dA_t = accB[0:16, 512:1024]
        cum = accB[0:16, 1024:1536]
        r1 = accB[0:16, 1536:2048]
        dd2 = sb2("dd2", [16, 1024])
        ddec = dd2[0:16, 0:512]
        dtdec = dd2[0:16, 512:1024]
        chi = sb2("chi", [16, TILE], BF16)
        cmid = sb2("cmid", [16, TILE], BF16)
        clo = sb2("clo", [16, TILE], BF16)
        ec_b = sb2("ec_b", [16, TILE], BF16)
        cdf = sb2("cdf", [16, 2])
        ldtT = sb2("ldtT", [128, 4 * 16])
        dtdecT = sb2("dtdecT", [128, 4 * 16])
        bjn = sb2("bjn", [128, 4 * 64], BF16)
        segL = sb2("segL", [6, 4096], BF16)
        segR = sb2("segR", [6, 4096], BF16)
        ecrow = sb2("ecrow", [1, 4096], BF16)
        cdrow = sb2("cdrow", [1, 16])
        cdrepS = sb2("cdrepS", [128, 16])
        cs2 = sb2("cs2", [128, 256], BF16)
        scoresM = sb2("scoresM", [128, 512], BF16)
        relb = [sb2(f"relb{i}", [128, 512], BF16) for i in range(3)]
        eb = [sb2(f"eb{i}", [128, 512], BF16) for i in range(3)]
        attb = [sb2(f"attb{i}", [128, 512], BF16) for i in range(3)]
        cecb = sb2("cecb", [128, 8 * 256], BF16)
        g2scr = [sb2(f"g2scr{i}", [128, 256], BF16) for i in range(2)]
        gbuf = xbcw[:, 0:H * TILE]
        rstd_rf = sb2("rstd_rf", [1, 512])
        rstd_rb = sb2("rstd_rb", [1, 512], BF16)
        rep_rstd = sb2("rep_rstd", [128, 512], BF16)
        ub2 = [ubp[:, 0:D], ubp[:, D:2 * D]]
        pstg = [sb2(f"pstg{i}", [128, D]) for i in range(2)]

        @block.vector
        def _(vector):
            w(vector, "ve", WOT_VE)
            nc.vector.memset(segL[:], 1.0)
            i = nc.vector.memset(segR[:], -1.0)
            inc(i, "ve")
        SEGI_VE = cnt["ve"]

        tile_res_ve = None
        tile_res_po = None
        pst_done = [None, None]
        for t in range(ntl):
            s_seq, cpair = t % 4, t // 4

            @block.sync
            def _(sync):
                if tile_res_ve is not None:
                    w(sync, "ve", tile_res_ve)
                xbv = xbc_s.rearrange("(s c p) l -> s c p l", s=SEQ, c=NCT)
                for c in range(NCT):
                    i = sync.dma_start(
                        out=xbcw[:, c * 515:(c + 1) * 515],
                        in_=xbv[s_seq, c, :, cpair * 512: cpair * 512 + 515])
                    inc(i, "sy", 16)
                i = sync.dma_start(
                    out=zld[:].rearrange("p (z n) -> p z n", z=NZT),
                    in_=siluz_s.rearrange("(t z p) n -> t p z n", t=NT, z=NZT)[t])
                inc(i, "sy", 16)
                sync.wait_ge(sems["sy"], cnt["sy"])
            LD_SY = cnt["sy"]

            conv_free = [None, None]
            diag_free = [None, None]
            for c in range(NCT):
                slot = c % 2
                pcv = (p_sc, p_seg)[slot]
                @block.vector
                def _(vector):
                    w(vector, "sy", LD_SY)
                    if c < 2:
                        w(vector, "ve")
                        w(vector, "pe")
                    if diag_free[slot] is not None:
                        w(vector, "pe", diag_free[slot])
                    for j in range(4):
                        i = nc.vector.tensor_scalar_mul(
                            diagb[:, slot * 512 + j * 128:
                                  slot * 512 + (j + 1) * 128],
                            identb[:], convw[:, c * 4 + j: c * 4 + j + 1])
                    inc(i, "ve")
                DG_VE = cnt["ve"]
                @block.tensor
                def _(tensor):
                    w(tensor, "ve", DG_VE)
                    if conv_free[slot] is not None:
                        w(tensor, "ac", conv_free[slot])
                    elif c < 2:
                        w(tensor, "ac")
                        w(tensor, "ve")
                    for j in range(4):
                        i = nc.tensor.matmul(
                            pcv[:, 0:512],
                            diagb[:, slot * 512 + j * 128:
                                  slot * 512 + (j + 1) * 128],
                            xbcw[:, c * 515 + j: c * 515 + j + 512],
                            start=(j == 0), stop=(j == 3))
                    inc(i, "pe")
                CV_PE = cnt["pe"]
                diag_free[slot] = CV_PE
                @block.scalar
                def _(scalar):
                    w(scalar, "pe", CV_PE)
                    if c == 0:
                        load_act_set(SET_SILU)
                    i = nc.scalar.activation(
                        xbcw[:, c * 515 + 3: c * 515 + 515], pcv[:, 0:512],
                        AF.Silu, bias=convb[:, c:c + 1])
                    inc(i, "ac")
                conv_free[slot] = cnt["ac"]
            @block.scalar
            def _(scalar):
                w(scalar, "sy", LD_SY)
                i = nc.scalar.activation(zld[:], zld[:], AF.Silu)
                inc(i, "ac")
            ZSIL_AC = cnt["ac"]
            CONV_AC = cnt["ac"]
            if p2stage <= 1:
                continue

            def xs(ct, tok0, nt_, rows=slice(0, 128)):
                return xbcw[rows, ct * 515 + 3 + tok0: ct * 515 + 3 + tok0 + nt_]

            @block.sync
            def _(sync):
                w(sync, "ve")
                w(sync, "ac")
                i = sync.dma_start(out=dtr, in_=dt_s[t * 16:(t + 1) * 16, :])
                inc(i, "sy", 16)
                sync.wait_ge(sems["sy"], cnt["sy"])
            DTL_SY = cnt["sy"]
            @block.scalar
            def _(scalar):
                w(scalar, "sy", DTL_SY)
                load_act_set(SET_NLE)
                nc.scalar.activation(sp_t1, dtr, AF.Abs, bias=dtbcol[:, 0:1])
                nc.scalar.drain()
                nc.scalar.activation(sp_t1, sp_t1, AF.Exp, scale=-1.0)
                nc.scalar.drain()
                nc.scalar.activation(sp_t2, sp_t1, AF.Ln, bias=1.0)
                nc.scalar.drain()
                i = nc.scalar.activation(sp_t1, dtr, AF.Relu, bias=dtbcol[:, 0:1])
                inc(i, "ac")
            @block.vector
            def _(vector):
                w(vector, "ac")
                nc.vector.tensor_add(dt_ht, sp_t1, sp_t2)
                nc.vector.drain()
                nc.vector.tensor_scalar_mul(dA_t, dt_ht, acol[:, 0:1])
                nc.vector.drain()
                for c in range(2):
                    sl = slice(c * CHUNK, (c + 1) * CHUNK)
                    nc.vector.tensor_tensor_scan(cum[:, sl], dA_t[:, sl],
                                                 dA_t[:, sl], 0.0, ALU.add,
                                                 ALU.bypass)
                nc.vector.drain()
                nc.vector.tensor_copy(chi[:], cum)
                nc.vector.drain()
                nc.vector.scalar_tensor_tensor(r1, cum, 1.0, chi[:],
                                               ALU.mult, ALU.subtract)
                nc.vector.drain()
                nc.vector.tensor_copy(cmid[:], r1)
                nc.vector.drain()
                nc.vector.scalar_tensor_tensor(r1, r1, 1.0, cmid[:],
                                               ALU.mult, ALU.subtract)
                nc.vector.drain()
                i = nc.vector.tensor_copy(clo[:], r1)
                inc(i, "ve")
            CUM_VE = cnt["ve"]
            @block.scalar
            def _(scalar):
                w(scalar, "ve", CUM_VE)
                for c in range(2):
                    sl = slice(c * CHUNK, (c + 1) * CHUNK)
                    last = slice(c * CHUNK + 255, c * CHUNK + 256)
                    nc.scalar.activation(ddec[:, c * 256:(c + 1) * 256],
                                         cum[:, sl], AF.Exp,
                                         scale=-1.0, bias=cum[:, last])
                    nc.scalar.activation(cdf[:, c:c + 1], cum[:, last], AF.Exp)
                i = nc.scalar.activation(ec_b[:], cum, AF.Exp)
                inc(i, "ac")
            DT_AC = cnt["ac"]
            @block.vector
            def _(vector):
                w(vector, "ac", DT_AC)
                i = nc.vector.tensor_mul(dtdec, dt_ht, ddec)
                inc(i, "ve")
            DTD_VE = cnt["ve"]

            blocks = [(dt_ht[0:16, tb * 128:(tb + 1) * 128],
                       ldtT[:, tb * 16:(tb + 1) * 16]) for tb in range(4)]
            emit_transpose(blocks, p_seg, p_seg, identf, rows_in=16,
                           pe_waits=[("ac", DT_AC)])
            blocks = [(dtdec[0:16, tb * 128:(tb + 1) * 128],
                       dtdecT[:, tb * 16:(tb + 1) * 16]) for tb in range(4)]
            emit_transpose(blocks, p_seg, p_seg, identf, rows_in=16,
                           pe_waits=[("ve", DTD_VE)])
            blocks = [(xs(zt, tb * 128, 128),
                       x_j[:, tb * DI + zt * 128: tb * DI + (zt + 1) * 128])
                      for tb in range(4) for zt in range(16)]
            emit_transpose(blocks, ptr2, ptr2, identb, rows_in=128,
                           pe_waits=[("ac", CONV_AC)])
            blocks = [(xs(16, tb * 128, 128, rows=slice(0, 64)),
                       bjn[:, tb * 64:(tb + 1) * 64]) for tb in range(4)]
            emit_transpose(blocks, ptr2, ptr2, identb, rows_in=64)
            TR_VE = cnt["ve"]

            @block.vector
            def _(vector):
                w(vector, "ve", TR_VE)
                for tb in range(4):
                    i = nc.vector.tensor_mul(
                        bdec[:, tb * 1024:(tb + 1) * 1024],
                        bjn[:, tb * 64:(tb + 1) * 64].unsqueeze(1)
                        .broadcast_to([128, 16, 64]),
                        dtdecT[:, tb * 16:(tb + 1) * 16].unsqueeze(2)
                        .broadcast_to([128, 16, 64]))
                inc(i, "ve")
            BDEC_VE = cnt["ve"]
            if p2stage <= 2:
                continue

            py_free = {0: None, 1: None}
            ebwar = {0: None, 1: None, 2: None}
            attbwar = {0: None, 1: None, 2: None}
            stbank_free = {0: None, 1: None}
            relseg_free = {0: None, 1: None}
            ssq_pe_prev = {0: None, 1: None}
            last_seg_pe = None
            last_cec_ve = None
            psc_free = None
            rstd_prev_ac = None
            for c in range(2):
                csl = slice(c * CHUNK, (c + 1) * CHUNK)
                @block.sync
                def _(sync):
                    w(sync, "ve", CUM_VE)
                    w(sync, "ac", DT_AC)
                    w(sync, "pe")
                    for row, srct in ((3, chi), (4, cmid), (5, clo)):
                        i = sync.dma_start(
                            out=segL[row:row + 1, :].rearrange(
                                "o (h i) -> o h i", h=16),
                            in_=srct[:, csl])
                        inc(i, "sy", 16)
                        i = sync.dma_start(
                            out=segR[row - 3:row - 2, :].rearrange(
                                "o (h i) -> o h i", h=16),
                            in_=srct[:, csl])
                        inc(i, "sy", 16)
                    i = sync.dma_start(
                        out=ecrow[:].rearrange("o (h i) -> o h i", h=16),
                        in_=ec_b[:, csl])
                    inc(i, "sy", 16)
                    i = sync.dma_start(out=cdrow[:].unsqueeze(2),
                                       in_=cdf[:, c:c + 1])
                    inc(i, "sy", 16)
                    i = sync.dma_start(out=cs2[0:64, :],
                                       in_=xs(16, c * CHUNK, CHUNK,
                                              rows=slice(64, 128)))
                    inc(i, "sy", 16)
                    i = sync.dma_start(out=cs2[64:128, :],
                                       in_=xs(16, c * CHUNK, CHUNK,
                                              rows=slice(64, 128)))
                    inc(i, "sy", 16)
                    sync.wait_ge(sems["sy"], cnt["sy"])
                STG_SY = cnt["sy"]

                @block.vector
                def _(vector):
                    w(vector, "ve")
                    i = nc.vector.tensor_copy(carry_b[:], carryA[s_seq][:])
                    inc(i, "ve")
                CB_VE = cnt["ve"]

                @block.tensor
                def _(tensor):
                    w(tensor, "ac", CONV_AC)
                    w(tensor, "sy", STG_SY)
                    if psc_free is not None:
                        w(tensor, "ve", psc_free)
                    for jt in range(2):
                        i = nc.tensor.matmul(
                            p_sc[:, jt * 256:(jt + 1) * 256],
                            xs(16, c * CHUNK + jt * 128, 128, slice(0, 64)),
                            cs2[0:64, :], start=True, stop=True)
                    inc(i, "pe")
                SCO_PE = cnt["pe"]
                @block.vector
                def _(vector):
                    w(vector, "pe", SCO_PE)
                    for jt in range(2):
                        i = nc.vector.tensor_mul(
                            scoresM[:, jt * 256:(jt + 1) * 256],
                            p_sc[:, jt * 256:(jt + 1) * 256], masks[jt][:])
                    inc(i, "ve")
                SCOM_VE = cnt["ve"]
                psc_free = SCOM_VE

                @block.tensor
                def _(tensor):
                    w(tensor, "sy", STG_SY)
                    w(tensor, "ve")
                    i = nc.tensor.matmul(p_rep[:, 0:16],
                                         ones_r128f[0:1, :], cdrow[:],
                                         start=True, stop=True)
                    inc(i, "pe")
                CDR_PE = cnt["pe"]
                @block.vector
                def _(vector):
                    w(vector, "pe", CDR_PE)
                    i = nc.vector.tensor_copy(cdrepS[:], p_rep[:, 0:16])
                    inc(i, "ve")
                CDREP_VE = cnt["ve"]

                for b4 in range(8):
                    rb = 0 if b4 < 4 else 64
                    @block.tensor
                    def _(tensor):
                        w(tensor, "ve")
                        for j in range(2):
                            hh = b4 * 2 + j
                            i = nc.tensor.matmul(
                                p_rep[rb:rb + 64, j * 256:(j + 1) * 256],
                                ones_r128b[0:1, 0:64],
                                ecrow[0:1, hh * 256:(hh + 1) * 256],
                                start=True, stop=True)
                        inc(i, "pe")
                    REP_PE = cnt["pe"]
                    @block.vector
                    def _(vector):
                        w(vector, "pe", REP_PE)
                        i = nc.vector.tensor_mul(
                            cecb[rb:rb + 64, (b4 % 4) * 512:(b4 % 4 + 1) * 512],
                            p_rep[rb:rb + 64, 0:512],
                            cs2[rb:rb + 64, :].unsqueeze(1)
                            .broadcast_to([64, 2, 256]))
                        inc(i, "ve")
                CEC_VE = cnt["ve"]
                if p2stage <= 3:
                    continue

                seg_pe = {}
                g2_ac = {}
                ssq_pe = {}

                def emit_seg(hh):
                    pcs_ = (p_seg, p_sc)[hh % 2]
                    @block.tensor
                    def _(tensor):
                        if relseg_free[hh % 2] is not None:
                            w(tensor, "ac", relseg_free[hh % 2])
                        elif hh % 2 == 0:
                            w(tensor, "ve", BDEC_VE)
                        else:
                            w(tensor, "ve", psc_free if psc_free else SCOM_VE)
                        w(tensor, "sy", STG_SY)
                        for jt in range(2):
                            i = nc.tensor.matmul(
                                pcs_[:, jt * 256:(jt + 1) * 256],
                                segL[0:6, hh * 256 + jt * 128:
                                     hh * 256 + (jt + 1) * 128],
                                segR[0:6, hh * 256:(hh + 1) * 256],
                                start=True, stop=True)
                        inc(i, "pe")
                    seg_pe[hh] = cnt["pe"]

                def emit_ssq(hh):
                    @block.tensor
                    def _(tensor):
                        w(tensor, "gp", g2_ac[hh])
                        if c == 1 and hh == 0:
                            w(tensor, "ac", rstd_prev_ac)
                        i = nc.tensor.matmul(
                            p_ssq[0:1, c * 256:(c + 1) * 256],
                            ones_c128b[:], g2scr[hh % 2][:],
                            start=(hh == 0), stop=(hh == H - 1))
                        inc(i, "pe")
                    ssq_pe[hh] = cnt["pe"]

                att_ve = {}

                def emit_actatt(hh):
                    pcs_ = (p_seg, p_sc)[hh % 2]
                    SEG_PE_ = seg_pe[hh]
                    EBW = ebwar[hh % 3]
                    @block.scalar
                    def _(scalar):
                        w(scalar, "pe", SEG_PE_)
                        if EBW is not None:
                            w(scalar, "ve", EBW)
                        i = nc.scalar.activation(
                            relb[hh % 3][:], pcs_[:], AF.Relu, scale=-1.0)
                        inc(i, "ac")
                        scalar.drain()
                        i = nc.scalar.activation(
                            eb[hh % 3][:], relb[hh % 3][:], AF.Exp, scale=-1.0)
                        inc(i, "ac")
                    E_AC_ = cnt["ac"]
                    relseg_free[hh % 2] = E_AC_ - 1
                    ATTBW = attbwar[hh % 3]
                    @block.vector
                    def _(vector):
                        w(vector, "ac", E_AC_)
                        w(vector, "ve", SCOM_VE)
                        if ATTBW is not None:
                            w(vector, "pe", ATTBW)
                        for jt in range(2):
                            tb = c * 2 + jt
                            i = nc.vector.scalar_tensor_tensor(
                                attb[hh % 3][:, jt * 256:(jt + 1) * 256],
                                eb[hh % 3][:, jt * 256:(jt + 1) * 256],
                                ldtT[:, tb * 16 + hh: tb * 16 + hh + 1],
                                scoresM[:, jt * 256:(jt + 1) * 256],
                                ALU.mult, ALU.mult)
                        inc(i, "ve")
                    ebwar[hh % 3] = cnt["ve"]
                    att_ve[hh] = cnt["ve"]

                emit_seg(0)
                emit_seg(1)
                emit_actatt(0)
                for h in range(H):
                    tb0, tb1 = c * 2, c * 2 + 1
                    hb, hc = cslc(h)
                    pyb = (p_y0, p_y1)[h % 2]
                    sbank = (p_st, p_rep)[h % 2]
                    ATT_VE = att_ve[h]

                    @block.tensor
                    def _(tensor):
                        w(tensor, "ve", ATT_VE)
                        w(tensor, "ve", CEC_VE)
                        w(tensor, "ve", CB_VE)
                        w(tensor, "ve", BDEC_VE)
                        if py_free[h % 2] is not None:
                            w(tensor, "ve", py_free[h % 2])
                        nc.tensor.matmul(
                            pyb[:, 0:256],
                            x_j[:, tb0 * DI + h * 128: tb0 * DI + (h + 1) * 128],
                            attb[h % 3][:, 0:256], start=True, stop=False)
                        nc.tensor.matmul(
                            pyb[:, 0:256],
                            x_j[:, tb1 * DI + h * 128: tb1 * DI + (h + 1) * 128],
                            attb[h % 3][:, 256:512], start=False, stop=False)
                        nc.tensor.matmul(
                            pyb[:, 0:128],
                            x_j[:, tb0 * DI + h * 128: tb0 * DI + (h + 1) * 128],
                            i_d[:, h * 128:(h + 1) * 128], start=False, stop=False)
                        nc.tensor.matmul(
                            pyb[:, 128:256],
                            x_j[:, tb1 * DI + h * 128: tb1 * DI + (h + 1) * 128],
                            i_d[:, h * 128:(h + 1) * 128], start=False, stop=False)
                        i = nc.tensor.matmul(
                            pyb[:, 0:256],
                            carry_b[hb:hb + 64, hc:hc + 128],
                            cecb[hb:hb + 64, (h % 8) * 256:(h % 8 + 1) * 256],
                            start=False, stop=True)
                        inc(i, "pe")
                        if stbank_free[h % 2] is not None:
                            w(tensor, "ve", stbank_free[h % 2])
                        nc.tensor.matmul(
                            p_st[hb:hb + 64, 0:128] if h % 2 == 0 else
                            p_rep[hb:hb + 64, 0:128],
                            bdec[:, tb0 * 1024 + h * 64: tb0 * 1024 + (h + 1) * 64],
                            x_j[:, tb0 * DI + h * 128: tb0 * DI + (h + 1) * 128],
                            start=True, stop=False)
                        i = nc.tensor.matmul(
                            p_st[hb:hb + 64, 0:128] if h % 2 == 0 else
                            p_rep[hb:hb + 64, 0:128],
                            bdec[:, tb1 * 1024 + h * 64: tb1 * 1024 + (h + 1) * 64],
                            x_j[:, tb1 * DI + h * 128: tb1 * DI + (h + 1) * 128],
                            start=False, stop=True)
                        inc(i, "pe")
                    ST_PE = cnt["pe"]
                    attbwar[h % 3] = ST_PE - 1

                    if h + 2 < H:
                        emit_seg(h + 2)
                    if h + 1 < H:
                        emit_actatt(h + 1)
                    if h >= 1:
                        emit_ssq(h - 1)

                    @block.vector
                    def _(vector):
                        w(vector, "pe", ST_PE)
                        w(vector, "ve", CDREP_VE)
                        nc.vector.scalar_tensor_tensor(
                            carryA[s_seq][hb:hb + 64, hc:hc + 128],
                            carryA[s_seq][hb:hb + 64, hc:hc + 128],
                            cdrepS[hb:hb + 64, h:h + 1],
                            sbank[hb:hb + 64, 0:128],
                            ALU.mult, ALU.add)
                        if h == 0 and c == 0:
                            w(vector, "ac", ZSIL_AC)
                        i = nc.vector.tensor_mul(
                            gbuf[:, h * TILE + c * 256: h * TILE + (c + 1) * 256],
                            pyb[:, 0:256],
                            zld[:, h * TILE + c * 256: h * TILE + (c + 1) * 256])
                        inc(i, "ve")
                    GE_VE = cnt["ve"]
                    stbank_free[h % 2] = GE_VE
                    py_free[h % 2] = GE_VE

                    @block.gpsimd
                    def _(gpsimd):
                        w(gpsimd, "ve", GE_VE)
                        if ssq_pe.get(h - 2) is not None:
                            w(gpsimd, "pe", ssq_pe[h - 2])
                        i = nc.gpsimd.tensor_mul(
                            g2scr[h % 2][:],
                            gbuf[:, h * TILE + c * 256: h * TILE + (c + 1) * 256],
                            gbuf[:, h * TILE + c * 256: h * TILE + (c + 1) * 256])
                        inc(i, "gp")
                    g2_ac[h] = cnt["gp"]
                emit_ssq(H - 1)
                SSQ_PE = cnt["pe"]
                if p2stage <= 4:
                    continue

                @block.scalar
                def _(scalar):
                    w(scalar, "pe", SSQ_PE)
                    nc.scalar.activation(rstd_rf[0:1, csl], p_ssq[0:1, csl],
                                         AF.Ln, scale=1.0 / DI, bias=epscol[0:1, 0:1])
                    nc.scalar.drain()
                    i = nc.scalar.activation(rstd_rf[0:1, csl], rstd_rf[0:1, csl],
                                             AF.Exp, scale=-0.5)
                    inc(i, "ac")
                RSTD_AC = cnt["ac"]
                rstd_prev_ac = RSTD_AC
                @block.vector
                def _(vector):
                    w(vector, "ac", RSTD_AC)
                    i = nc.vector.tensor_copy(rstd_rb[0:1, csl], rstd_rf[0:1, csl])
                    inc(i, "ve")
                RSTDB_VE = cnt["ve"]
                @block.tensor
                def _(tensor):
                    w(tensor, "ve", RSTDB_VE)
                    w(tensor, "ve", psc_free)
                    i = nc.tensor.matmul(p_sc[:, 0:256], ones_r128b[0:1, :],
                                         rstd_rb[0:1, csl], start=True, stop=True)
                    inc(i, "pe")
                RR_PE = cnt["pe"]
                @block.vector
                def _(vector):
                    w(vector, "pe", RR_PE)
                    i = nc.vector.tensor_copy(rep_rstd[:, csl], p_sc[:, 0:256])
                    inc(i, "ve")
                psc_free = cnt["ve"]

            if p2stage <= 5:
                continue
            @block.vector
            def _(vector):
                w(vector, "ve")
                for h in range(H):
                    i = nc.vector.scalar_tensor_tensor(
                        gbuf[:, h * TILE:(h + 1) * TILE],
                        gbuf[:, h * TILE:(h + 1) * TILE],
                        rmscol[:, h:h + 1], rep_rstd[:],
                        ALU.mult, ALU.mult)
                inc(i, "ve")
            GQ_VE = cnt["ve"]

            pop_free = {0: psc_free, 1: None}
            pop_sem = {0: "ve", 1: "ac"}
            for mt in range(4):
                @block.sync
                def _(sync):
                    if pst_done[mt % 2] is not None:
                        w(sync, "po", pst_done[mt % 2][1])
                    i = sync.dma_start(out=ub2[mt % 2],
                                       in_=u_view[trow(t) + mt])
                    inc(i, "sy", 16)
                    sync.wait_ge(sems["sy"], cnt["sy"])
                UB_SY = cnt["sy"]
                for ns in range(2):
                    pb = (p_sc, p_seg)[ns]
                    @block.tensor
                    def _(tensor):
                        w(tensor, "ve", GQ_VE)
                        if pop_free[ns] is not None:
                            if pop_sem[ns] == "ac":
                                w(tensor, "ac", pop_free[ns])
                            else:
                                w(tensor, "ve", pop_free[ns])
                        elif ns == 1:
                            w(tensor, "ac")
                        for h in range(H):
                            i = nc.tensor.matmul(
                                pb[:, 0:512],
                                gbuf[:, h * TILE + mt * 128:
                                     h * TILE + (mt + 1) * 128],
                                wot[:, h * 1024 + ns * 512:
                                    h * 1024 + (ns + 1) * 512],
                                start=(h == 0), stop=(h == H - 1))
                        inc(i, "pe")
                    OP_PE = cnt["pe"]
                    @block.vector
                    def _(vector):
                        w(vector, "pe", OP_PE)
                        w(vector, "sy", UB_SY)
                        if ns == 0 and pst_done[mt % 2] is not None:
                            w(vector, "po", pst_done[mt % 2][1])
                        i = nc.vector.tensor_add(
                            pstg[mt % 2][:, ns * 512:(ns + 1) * 512],
                            pb[:, 0:512],
                            ub2[mt % 2][:, ns * 512:(ns + 1) * 512])
                        inc(i, "ve")
                    pop_free[ns] = cnt["ve"]
                    pop_sem[ns] = "ve"
                PS_VE = cnt["ve"]
                @block.gpsimd
                def _(gpsimd):
                    w(gpsimd, "ve", PS_VE)
                    i = gpsimd.dma_start(
                        out=part_s.rearrange("(n p) d -> n p d", p=128)[trow(t) + mt],
                        in_=pstg[mt % 2][:])
                    inc(i, "po", 16)
                    gpsimd.wait_ge(sems["po"], cnt["po"])
                pst_done[mt % 2] = (PS_VE, cnt["po"])
            tile_res_ve = cnt["ve"]

        @block.gpsimd
        def _(gpsimd):
            w(gpsimd, "po")

    barrier()
    if maxphase <= 2:
        ctx.close()
        return nc

    with ExitStack() as stk, \
         nc.psum_tensor("p_g", [128, 512], F32) as p_g, \
         nc.psum_tensor("p_u", [128, 512], F32) as p_u, \
         nc.psum_tensor("ptrA3", [128, 512], BF16) as ptrA3, \
         nc.psum_tensor("ptrB3", [128, 512], BF16) as ptrB3:
        sb3 = lambda name, shape, dt=F32: stk.enter_context(nc.sbuf_tensor(name, shape, dt))
        wgt = sb3("wgt", [128, KD * FFI], BF16)
        wut = sb3("wut", [128, KD * FFI], BF16)
        pstk3 = ExitStack()
        sb3p = lambda name, shape, dt=F32: pstk3.enter_context(
            nc.sbuf_tensor(name, shape, dt))
        wld3 = [sb3p(f"wld3{i}", [128, D]) for i in range(2)]
        wbf3 = sb3p("wbf3", [128, D], BF16)
        qw3 = quant_scratch(pstk3, "qw3", 1024)

        @block.scalar
        def _(scalar):
            load_act_set(SET_SILU)
        emit_wquant_prep(wg_h, FFI, D, wld3, qw3, wbf3, ptrA3, ptrB3, wgt, 1024)
        emit_wquant_prep(wu_h, FFI, D, wld3, qw3, wbf3, ptrA3, ptrB3, wut, 1024)
        WQ_VE = cnt["ve"]
        pstk3.close()

        q3 = quant_scratch(stk, "q3", 1024, f16=True)
        xq = [sb3(f"xq{i}", [128, KD * TILE], BF16) for i in range(2)]
        gs = sb3("gs", [128, 512], BF16)
        hb = sb3("hb", [128, FFI])
        hqb = sb3("hqb", [128, FFI], BF16)
        hqTb = sb3("hqTb", [128, KFI * TILE], BF16)

        xq_done = [None, None]
        for t0 in range(2):
            @block.sync
            def _(sync):
                w(sync, "ve", WQ_VE)
                i = sync.dma_start(
                    out=xq[t0][:].rearrange("p (k n) -> p k n", k=KD),
                    in_=xnqT_s.rearrange("(t k p) n -> t p k n", t=NT, k=KD)[t0])
                xq_done[t0] = inc(i, "sy", 16)
                sync.wait_ge(sems["sy"], cnt["sy"])

        hqT_done = None
        pg_free = None
        pu_free = None
        for t in range(min(ntl, NT)):
            xqb = xq[t % 2]
            for mt in range(4):
                for ii, (i0, i1) in enumerate(ISLS):
                    nn = i1 - i0
                    @block.tensor
                    def _(tensor):
                        if t == 0 and mt == 0 and ii == 0:
                            w(tensor, "ve", WQ_VE)
                            w(tensor, "ac")
                        w(tensor, "sy", xq_done[t % 2])
                        if pg_free is not None:
                            w(tensor, "ac", pg_free)
                        for k in range(KD):
                            i = nc.tensor.matmul(
                                p_g[:, 0:nn],
                                xqb[:, k * TILE + mt * 128: k * TILE + (mt + 1) * 128],
                                wgt[:, k * FFI + i0: k * FFI + i1],
                                start=(k == 0), stop=(k == KD - 1))
                        inc(i, "pe")
                        if pu_free is not None:
                            w(tensor, "ve", pu_free)
                        for k in range(KD):
                            i = nc.tensor.matmul(
                                p_u[:, 0:nn],
                                xqb[:, k * TILE + mt * 128: k * TILE + (mt + 1) * 128],
                                wut[:, k * FFI + i0: k * FFI + i1],
                                start=(k == 0), stop=(k == KD - 1))
                        inc(i, "pe")
                    MM_PE = cnt["pe"]
                    @block.scalar
                    def _(scalar):
                        w(scalar, "pe", MM_PE - 1)
                        if pu_free is not None:
                            w(scalar, "ve", pu_free)
                        i = nc.scalar.activation(gs[:, 0:nn], p_g[:, 0:nn], AF.Silu)
                        inc(i, "ac")
                    GS_AC = cnt["ac"]
                    pg_free = GS_AC
                    @block.vector
                    def _(vector):
                        w(vector, "pe", MM_PE)
                        w(vector, "ac", GS_AC)
                        i = nc.vector.tensor_mul(hb[:, i0:i1], p_u[:, 0:nn],
                                                 gs[:, 0:nn])
                        inc(i, "ve")
                    HB_VE = cnt["ve"]
                    pu_free = HB_VE
                    if ii % 2 == 1:
                        lo = 1024 * (ii // 2)
                        emit_quant_full(hb[:, lo:i1], hqb[:, lo:i1], i1 - lo,
                                        q3, first_wait=("ve", HB_VE))
                HQ_VE = cnt["ve"]

                pe_extra = [("ve", HQ_VE)]
                if hqT_done is not None and mt == 0:
                    pe_extra.append(("po", hqT_done))
                blocks = [(hqb[:, k * 128:(k + 1) * 128],
                           hqTb[:, k * TILE + mt * 128: k * TILE + (mt + 1) * 128])
                          for k in range(KFI)]
                emit_transpose(blocks, ptrA3, ptrB3, identb, pe_waits=pe_extra)
            HQT_VE = cnt["ve"]

            if t + 2 < NT:
                @block.sync
                def _(sync):
                    w(sync, "pe")
                    i = sync.dma_start(
                        out=xq[t % 2][:].rearrange("p (k n) -> p k n", k=KD),
                        in_=xnqT_s.rearrange("(t k p) n -> t p k n",
                                             t=NT, k=KD)[t + 2])
                    xq_done[t % 2] = inc(i, "sy", 16)
                    sync.wait_ge(sems["sy"], cnt["sy"])

            @block.gpsimd
            def _(gpsimd):
                w(gpsimd, "ve", HQT_VE)
                i = gpsimd.dma_start(
                    out=hqT_s.rearrange("(t k p) n -> t p k n", t=NT, k=KFI)[t],
                    in_=hqTb[:].rearrange("p (k n) -> p k n", k=KFI))
                inc(i, "po", 16)
                gpsimd.wait_ge(sems["po"], cnt["po"])
            hqT_done = cnt["po"]

    barrier()
    if maxphase <= 3:
        ctx.close()
        return nc

    with ExitStack() as stk, \
         nc.psum_tensor("p_d0", [128, 512], F32) as p_d0, \
         nc.psum_tensor("p_d1", [128, 512], F32) as p_d1, \
         nc.psum_tensor("ptrA4", [128, 512], BF16) as ptrA4, \
         nc.psum_tensor("ptrB4", [128, 512], BF16) as ptrB4:
        sb4 = lambda name, shape, dt=F32: stk.enter_context(nc.sbuf_tensor(name, shape, dt))
        wdt = sb4("wdt", [128, KFI * D], BF16)
        wld4 = [sb4(f"wld4{i}", [128, FFI]) for i in range(2)]
        wbf4 = sb4("wbf4", [128, FFI], BF16)
        q4 = quant_scratch(stk, "q4", 1024)

        @block.scalar
        def _(scalar):
            load_act_set(SET_SILU)
        emit_wquant_prep(wd_h, D, FFI, wld4, q4, wbf4, ptrA4, ptrB4, wdt, 1024)
        WD_VE = cnt["ve"]

        hqt = [sb4(f"hqt{i}", [128, KFI * TILE], BF16) for i in range(2)]
        pld = sb4("pld", [128, 4 * D])
        ostg = sb4("ostg", [128, 4 * D])

        hq_done = [None, None]
        for t0 in range(2):
            @block.sync
            def _(sync):
                i = sync.dma_start(
                    out=hqt[t0][:].rearrange("p (k n) -> p k n", k=KFI),
                    in_=hqT_s.rearrange("(t k p) n -> t p k n", t=NT, k=KFI)[t0])
                hq_done[t0] = inc(i, "sy", 16)
                sync.wait_ge(sems["sy"], cnt["sy"])

        ost_po = None
        pd_free = [None, None]
        for t in range(min(ntl, NT)):
            hq_b = hqt[t % 2]
            @block.sync
            def _(sync):
                w(sync, "ve")
                i = sync.dma_start(
                    out=pld[:].rearrange("p (n d) -> p n d", n=4),
                    in_=part_s.rearrange("(n p) d -> n p d", p=128)
                    [trow(t):trow(t) + 4].rearrange("n p d -> p n d"))
                inc(i, "sy", 16)
                sync.wait_ge(sems["sy"], cnt["sy"])
            PLD_SY = cnt["sy"]

            for mt in range(4):
                for ns in range(2):
                    pb = (p_d0, p_d1)[ns]
                    @block.tensor
                    def _(tensor):
                        if t == 0 and mt == 0 and ns == 0:
                            w(tensor, "ve", WD_VE)
                        w(tensor, "sy", hq_done[t % 2])
                        if pd_free[ns] is not None:
                            w(tensor, "ve", pd_free[ns])
                        for k in range(KFI):
                            i = nc.tensor.matmul(
                                pb[:, 0:512],
                                hq_b[:, k * TILE + mt * 128:
                                     k * TILE + (mt + 1) * 128],
                                wdt[:, k * D + ns * 512: k * D + (ns + 1) * 512],
                                start=(k == 0), stop=(k == KFI - 1))
                        inc(i, "pe")
                    DN_PE = cnt["pe"]
                    @block.vector
                    def _(vector):
                        w(vector, "pe", DN_PE)
                        w(vector, "sy", PLD_SY)
                        if ost_po is not None and mt == 0 and ns == 0:
                            w(vector, "po", ost_po)
                        i = nc.vector.tensor_add(
                            ostg[:, mt * D + ns * 512: mt * D + (ns + 1) * 512],
                            pb[:, 0:512],
                            pld[:, mt * D + ns * 512: mt * D + (ns + 1) * 512])
                        inc(i, "ve")
                    pd_free[ns] = cnt["ve"]
            OST_VE = cnt["ve"]

            if t + 2 < NT:
                @block.sync
                def _(sync):
                    w(sync, "pe")
                    i = sync.dma_start(
                        out=hqt[t % 2][:].rearrange("p (k n) -> p k n", k=KFI),
                        in_=hqT_s.rearrange("(t k p) n -> t p k n",
                                            t=NT, k=KFI)[t + 2])
                    hq_done[t % 2] = inc(i, "sy", 16)
                    sync.wait_ge(sems["sy"], cnt["sy"])

            @block.gpsimd
            def _(gpsimd):
                w(gpsimd, "ve", OST_VE)
                i = gpsimd.dma_start(
                    out=out_h.rearrange("(n p) d -> n p d", p=128)
                    [trow(t):trow(t) + 4].rearrange("n p d -> p n d"),
                    in_=ostg[:].rearrange("p (n d) -> p n d", n=4))
                inc(i, "po", 16)
                gpsimd.wait_ge(sems["po"], cnt["po"])
            ost_po = cnt["po"]

        @block.gpsimd
        def _(gpsimd):
            w(gpsimd, "po")

    ctx.close()
    return nc


_HOST_CONSTS = None


def _host_consts():
    global _HOST_CONSTS
    if _HOST_CONSTS is None:
        identb = np.eye(128, dtype=BFD)
        identf = np.eye(128, dtype=np.float32)
        jj = np.arange(128)[:, None]
        ii = np.arange(256)[None, :]
        m0 = (ii >= jj).astype(BFD)
        m1 = (ii >= jj + 128).astype(BFD)
        _HOST_CONSTS = (identb, identf, m0, m1)
    return _HOST_CONSTS


def make_in_maps(inputs):
    identb, identf, m0, m1 = _host_consts()
    base = {
        "in_proj_w": np.ascontiguousarray(inputs["L0_attn_in_proj_w"]),
        "conv_w": np.ascontiguousarray(inputs["L0_attn_conv_w"]),
        "conv_b": np.ascontiguousarray(inputs["L0_attn_conv_b"]),
        "dt_bias": np.ascontiguousarray(inputs["L0_attn_dt_bias"]),
        "A_log": np.ascontiguousarray(inputs["L0_attn_A_log"]),
        "D_h": np.ascontiguousarray(inputs["L0_attn_D"]),
        "x_norm_w": np.ascontiguousarray(inputs["L0_attn_x_norm_w"]),
        "x_norm_b": np.ascontiguousarray(inputs["L0_attn_x_norm_b"]),
        "rms_w": np.ascontiguousarray(inputs["L0_attn_rms_w"]),
        "out_proj_w": np.ascontiguousarray(inputs["L0_attn_out_proj_w"]),
        "w_gate": np.ascontiguousarray(inputs["L0_ffn_w_gate"]),
        "w_up": np.ascontiguousarray(inputs["L0_ffn_w_up"]),
        "w_down": np.ascontiguousarray(inputs["L0_ffn_w_down"]),
        "ident_b": identb, "ident_f": identf,
        "mask_j0": m0, "mask_j1": m1,
    }
    in_maps = []
    u = inputs["L0_attn_u"]
    for c in range(8):
        m = dict(base)
        m["u"] = np.ascontiguousarray(u[c * SEQ:(c + 1) * SEQ].reshape(TOK, D))
        in_maps.append(m)
    return in_maps


def kernel(**inputs):
    nc = build_nc()
    in_maps = make_in_maps(inputs)
    res = run_bass_kernel_spmd(nc, in_maps, core_ids=list(range(8)))
    out = np.empty((B, L, D), dtype=np.float32)
    for c in range(8):
        out[c * SEQ:(c + 1) * SEQ] = res.results[c]["out"].reshape(SEQ, L, D)
    return out


# version 15
# speedup vs baseline: 1.1308x; 1.0359x over previous
import numpy as np
from contextlib import ExitStack

import concourse.bass as bass
import concourse.mybir as mybir
from concourse.bass_utils import run_bass_kernel_spmd

import ml_dtypes
BFD = ml_dtypes.bfloat16

F32 = mybir.dt.float32
BF16 = mybir.dt.bfloat16
F16 = mybir.dt.float16
AF = mybir.ActivationFunctionType
ALU = mybir.AluOpType
AX = mybir.AxisListType

B, L, D = 32, 2048, 1024
SEQ = 4
TOK = SEQ * L
DI = 2048
DS = 64
H = 16
CHD = 2176
F = 4240
CHUNK = 256
TILE = 512
NT = TOK // TILE
LN_EPS = 1e-5
RMS_EPS = 1e-5
NZT = DI // 128
NCT = CHD // 128
NFT = 34
KD = D // 128
FFI = 2816
KFI = FFI // 128
ISLS = [(0, 512), (512, 1024), (1024, 1536), (1536, 2048), (2048, 2560),
        (2560, 2816)]

QTHR = [0.25, 0.75, 1.25, 1.75, 2.5, 3.5, 5.0]
QWTS = [1.0, 1.0, 1.0, 1.0, 2.0, 2.0, 4.0]


def build_nc(dev_dump=(), maxphase=4, ntl=NT, p2stage=99):
    nc = bass.Bass("TRN2", target_bir_lowering=False)

    u_h = nc.dram_tensor("u", [TOK, D], F32, kind="ExternalInput").ap()
    win_h = nc.dram_tensor("in_proj_w", [F, D], F32, kind="ExternalInput").ap()
    convw_h = nc.dram_tensor("conv_w", [CHD, 4], F32, kind="ExternalInput").ap()
    convb_h = nc.dram_tensor("conv_b", [CHD], F32, kind="ExternalInput").ap()
    dtb_h = nc.dram_tensor("dt_bias", [H], F32, kind="ExternalInput").ap()
    alog_h = nc.dram_tensor("A_log", [H], F32, kind="ExternalInput").ap()
    dd_h = nc.dram_tensor("D_h", [H], F32, kind="ExternalInput").ap()
    xw_h = nc.dram_tensor("x_norm_w", [D], F32, kind="ExternalInput").ap()
    xb_h = nc.dram_tensor("x_norm_b", [D], F32, kind="ExternalInput").ap()
    rmsw_h = nc.dram_tensor("rms_w", [DI], F32, kind="ExternalInput").ap()
    wout_h = nc.dram_tensor("out_proj_w", [D, DI], F32, kind="ExternalInput").ap()
    wg_h = nc.dram_tensor("w_gate", [FFI, D], F32, kind="ExternalInput").ap()
    wu_h = nc.dram_tensor("w_up", [FFI, D], F32, kind="ExternalInput").ap()
    wd_h = nc.dram_tensor("w_down", [D, FFI], F32, kind="ExternalInput").ap()
    identb_h = nc.dram_tensor("ident_b", [128, 128], BF16, kind="ExternalInput").ap()
    identf_h = nc.dram_tensor("ident_f", [128, 128], F32, kind="ExternalInput").ap()
    mask0_h = nc.dram_tensor("mask_j0", [128, 256], BF16, kind="ExternalInput").ap()
    mask1_h = nc.dram_tensor("mask_j1", [128, 256], BF16, kind="ExternalInput").ap()

    out_h = nc.dram_tensor("out", [TOK, D], F32, kind="ExternalOutput").ap()

    def dscratch(name, shape, dt):
        kind = "ExternalOutput" if name in dev_dump else "Internal"
        return nc.dram_tensor(name, shape, dt, kind=kind).ap()

    xnqT_s = dscratch("s_xnqT", [NT * KD * 128, TILE], BF16)
    siluz_s = dscratch("s_siluz", [NT * NZT * 128, TILE], BF16)
    xbc_s = dscratch("s_xbc", [SEQ * NCT * 128, L + 3], BF16)
    dt_s = dscratch("s_dt", [NT * H, TILE], F32)
    part_s = dscratch("s_part", [TOK, D], F32)
    hqT_s = dscratch("s_hqT", [NT * KFI * 128, TILE], BF16)

    ctx = ExitStack()
    sbt = lambda name, shape, dt=F32: ctx.enter_context(nc.sbuf_tensor(name, shape, dt))

    SEMS = ["sy", "po", "pe", "ve", "ac", "gp"]
    sems = {s: ctx.enter_context(nc.semaphore(s)) for s in SEMS}
    cnt = {s: 0 for s in SEMS}

    def inc(instr, s, n=1):
        instr.then_inc(sems[s], n)
        cnt[s] += n
        return cnt[s]

    def w(eng, s, val=None):
        v = cnt[s] if val is None else val
        if v > 0:
            eng.wait_ge(sems[s], v)

    block = ctx.enter_context(nc.Block())

    from concourse.hw_specs import get_activation_tables
    _tabs = list(get_activation_tables(nc.m.arch).keys())
    SET_NLE = _tabs.index("natural_log_exp_and_others")
    SET_SILU = _tabs.index("silu_and_others")

    def load_act_set(sid):
        nc.scalar.add_instruction(mybir.InstLoadActFuncSet(
            name=nc.get_next_instruction_name(), act_func_set_id=sid,
            ins=[], outs=[]))

    def barrier():
        snap = dict(cnt)
        for name in ["sync", "gpsimd", "tensor", "vector", "scalar"]:
            dec = getattr(block, name)
            @dec
            def _(eng):
                for s in SEMS:
                    if snap[s]:
                        eng.wait_ge(sems[s], snap[s])

    identb = sbt("identb", [128, 128], BF16)
    identf = sbt("identf", [128, 128], F32)
    masks = [sbt("mask0", [128, 256], BF16), sbt("mask1", [128, 256], BF16)]
    ones_r128b = sbt("ones_r128b", [1, 128], BF16)
    ones_r128f = sbt("ones_r128f", [1, 128], F32)
    ones_r512b = sbt("ones_r512b", [1, TILE], BF16)
    ones_c128b = sbt("ones_c128b", [128, 1], BF16)
    convw = sbt("convw", [128, NCT * 4], F32)
    convb = sbt("convb", [128, NCT], F32)
    dtbcol = sbt("dtbcol", [16, 1], F32)
    acol = sbt("acol", [16, 1], F32)
    rmscol = sbt("rmscol", [128, H], F32)
    i_d = sbt("i_d", [128, H * 128], BF16)
    xwcol = sbt("xwcol", [128, KD], F32)
    xbcolT = sbt("xbcolT", [128, KD], F32)
    xbcolT_b = sbt("xbcolT_b", [128, KD], BF16)
    carryA = [sbt(f"carryA{s}", [128, 8 * 128], F32) for s in range(SEQ)]
    carry_b = sbt("carry_b", [128, 8 * 128], BF16)
    drow = sbt("drow", [1, H], F32)
    zero3 = sbt("zero3", [128, 4], BF16)
    epscol = sbt("epscol", [128, 1], F32)

    def cslc(h):
        hb = 0 if h < 8 else 64
        hc = (h % 8) * 128
        return hb, hc

    @block.sync
    def _(sync):
        for dst, src, slow in [
            (identb, identb_h, False), (identf, identf_h, False),
            (masks[0], mask0_h, False), (masks[1], mask1_h, False),
            (convw[:].rearrange("p (c k) -> p c k", k=4),
             convw_h.rearrange("(c p) k -> p c k", p=128), False),
            (convb, convb_h.rearrange("(c p) -> p c", p=128), True),
            (dtbcol, dtb_h.unsqueeze(1), False),
            (acol, alog_h.unsqueeze(1), False),
            (rmscol, rmsw_h.rearrange("(h p) -> p h", p=128), True),
            (xwcol, xw_h.rearrange("(k p) -> p k", p=128), True),
            (xbcolT, xb_h.rearrange("(k p) -> p k", p=128), True),
            (drow, dd_h.unsqueeze(0), False),
        ]:
            i = sync.dma_start(out=dst[:], in_=src, allow_slow_non_contiguous=slow)
            inc(i, "sy", 16)
        sync.wait_ge(sems["sy"], cnt["sy"])
    SMALL_LD = cnt["sy"]

    @block.vector
    def _(vector):
        nc.vector.memset(ones_r128b[:], 1.0)
        nc.vector.memset(ones_r128f[:], 1.0)
        nc.vector.memset(ones_r512b[:], 1.0)
        nc.vector.memset(ones_c128b[:], 1.0)
        nc.vector.memset(zero3[:], 0.0)
        nc.vector.memset(epscol[:], LN_EPS)
        for s in range(SEQ):
            nc.vector.memset(carryA[s][:], 0.0)
        nc.vector.memset(carry_b[:], 0.0)
        w(vector, "sy", SMALL_LD)
        i = nc.vector.tensor_copy(xbcolT_b[:], xbcolT[:])
        inc(i, "ve")
    CONSTS_VE = cnt["ve"]

    @block.scalar
    def _(scalar):
        w(scalar, "sy", SMALL_LD)
        load_act_set(SET_NLE)
        i = nc.scalar.activation(acol[:], acol[:], AF.Exp)
        inc(i, "ac")
    ACOL_AC = cnt["ac"]

    with nc.psum_tensor("dcols_p", [128, H], F32) as dcols_p:
        dcols = sbt("dcols", [128, H], F32)

        @block.tensor
        def _(tensor):
            w(tensor, "sy", SMALL_LD)
            w(tensor, "ve", CONSTS_VE)
            i = nc.tensor.matmul(dcols_p[:], ones_r128f[0:1, :], drow[:],
                                 start=True, stop=True)
            inc(i, "pe")

        @block.vector
        def _(vector):
            w(vector, "pe")
            w(vector, "ac", ACOL_AC)
            nc.vector.tensor_scalar_mul(acol[:], acol[:], -1.0)
            nc.vector.tensor_copy(dcols[:], dcols_p[:])
            nc.vector.drain()
            for h in range(H):
                i = nc.vector.tensor_scalar_mul(i_d[:, h * 128:(h + 1) * 128],
                                                identb[:], dcols[:, h:h + 1])
            inc(i, "ve")

    @block.gpsimd
    def _(gpsimd):
        w(gpsimd, "ve", CONSTS_VE)
        xbv = xbc_s.rearrange("(s c p) l -> s c p l", s=SEQ, c=NCT)
        for s in range(SEQ):
            for c in range(NCT):
                i = gpsimd.dma_start(out=xbv[s, c, :, 0:3], in_=zero3[:, 0:3])
                inc(i, "po", 16)
        gpsimd.wait_ge(sems["po"], cnt["po"])

    def quant_scratch(stk, tag, n, f16=False):
        sb = lambda nm, shape, dt=F32: stk.enter_context(nc.sbuf_tensor(nm, shape, dt))
        nb = n // 16
        ddt = F16 if f16 else F32
        d = {
            "amax": sb(f"{tag}_amax", [128, nb]), "scl": sb(f"{tag}_scl", [128, nb]),
            "absx": sb(f"{tag}_absx", [128, n], ddt),
            "sg": sb(f"{tag}_sg", [128, n], F16 if f16 else BF16),
            "aa": sb(f"{tag}_aa", [128, n], ddt),
            "s1": sb(f"{tag}_s1", [128, n], F16 if f16 else BF16),
            "s2": sb(f"{tag}_s2", [128, n], F16 if f16 else BF16),
            "cc": sb(f"{tag}_cc", [128, n], F16 if f16 else BF16),
            "cd": sb(f"{tag}_cd", [128, n], F16 if f16 else BF16),
            "c5": sb(f"{tag}_c5", [128, n], F16 if f16 else BF16),
            "c6": sb(f"{tag}_c6", [128, n], F16 if f16 else BF16),
            "ss": sb(f"{tag}_ss", [128, n], F16 if f16 else BF16),
            "rcp16": sb(f"{tag}_rcp16", [128, nb], F16 if f16 else F32),
            "sclh16": sb(f"{tag}_sclh16", [128, nb], F16 if f16 else F32),
        }
        return d

    def emit_quant_full(src, dst, n, q, first_wait=None):
        nb = n // 16
        assert n % 32 == 0
        hn, hb = n // 2, nb // 2
        CS = [slice(0, hn), slice(hn, n)]
        GS = [slice(0, hb), slice(hb, nb)]
        @block.scalar
        def _(scalar):
            if first_wait is not None:
                w(scalar, first_wait[0], first_wait[1])
            if q.get("prev") is not None:
                w(scalar, "ve", q["prev_absx"])
                w(scalar, "gp", q["prev_sg"])
            else:
                w(scalar, "ve")
                w(scalar, "gp")
            nc.scalar.activation(q["absx"][:, 0:n], src, AF.Abs)
            i = nc.scalar.activation(q["sg"][:, 0:n], src, AF.Sign)
            inc(i, "ac")
        AC = cnt["ac"]
        gp0 = cnt["gp"]
        gp_ss = [gp0 + 1, gp0 + 2]
        gp_ad = [gp0 + 5, gp0 + 6]
        mk = {}

        @block.vector
        def _(vector):
            def vw(v):
                if v:
                    nc.vector.wait_ge(sems["ve"], v)
            def vop(key, i, instr):
                inc(instr, "ve")
                mk[(key, i)] = cnt["ve"]
            nc.vector.wait_ge(sems["ac"], AC)
            if q.get("prev") is not None:
                vw(q["prev"])
                nc.vector.wait_ge(sems["gp"], q["prev_gp"])
            for i in (0, 1):
                vop("mx", i, nc.vector.reduce_max(
                    q["amax"][:, GS[i]],
                    src[:, CS[i]].rearrange("p (a b) -> p a b", b=16),
                    axis=AX.X, apply_absolute_value=True))
            for i in (0, 1):
                vw(mk[("mx", i)])
                vop("rc", i, nc.vector.reciprocal(q["scl"][:, GS[i]],
                                                  q["amax"][:, GS[i]]))
            for i in (0, 1):
                vw(mk[("rc", i)])
                vop("r6", i, nc.vector.tensor_scalar(
                    q["rcp16"][:, GS[i]], q["scl"][:, GS[i]], 6.0, 1e12,
                    ALU.mult, ALU.min))
                vop("sh", i, nc.vector.tensor_scalar(
                    q["sclh16"][:, GS[i]], q["amax"][:, GS[i]], 1.0 / 12.0,
                    5e-13, ALU.mult, ALU.max))
            for i in (0, 1):
                vw(mk[("r6", i)])
                vop("aa", i, nc.vector.tensor_mul(
                    q["aa"][:, CS[i]], q["absx"][:, CS[i]],
                    q["rcp16"][:, GS[i]].unsqueeze(2)
                    .broadcast_to([128, hb, 16])))
            for i in (0, 1):
                vw(mk[("aa", i)])
                vop("t0", i, nc.vector.tensor_scalar(
                    q["s1"][:, CS[i]], q["aa"][:, CS[i]], QTHR[0], 1.0,
                    ALU.is_gt, ALU.mult))
                vop("c0", i, nc.vector.tensor_scalar(
                    q["cc"][:, CS[i]], q["aa"][:, CS[i]], QTHR[1], 1.0,
                    ALU.is_gt, ALU.mult))
            for i in (0, 1):
                vw(mk[("t0", i)])
                vw(mk[("c0", i)])
                vop("t1", i, nc.vector.tensor_add(
                    q["s1"][:, CS[i]], q["s1"][:, CS[i]], q["cc"][:, CS[i]]))
                vop("c1", i, nc.vector.tensor_scalar(
                    q["cd"][:, CS[i]], q["aa"][:, CS[i]], QTHR[2], 1.0,
                    ALU.is_gt, ALU.mult))
            for i in (0, 1):
                vw(mk[("t1", i)])
                vw(mk[("c1", i)])
                vop("t2", i, nc.vector.tensor_add(
                    q["s1"][:, CS[i]], q["s1"][:, CS[i]], q["cd"][:, CS[i]]))
                vop("c2", i, nc.vector.tensor_scalar(
                    q["cc"][:, CS[i]], q["aa"][:, CS[i]], QTHR[3], 1.0,
                    ALU.is_gt, ALU.mult))
            for i in (0, 1):
                vw(mk[("t2", i)])
                vw(mk[("c2", i)])
                vop("t3", i, nc.vector.tensor_add(
                    q["s1"][:, CS[i]], q["s1"][:, CS[i]], q["cc"][:, CS[i]]))
                vop("u0", i, nc.vector.tensor_scalar(
                    q["s2"][:, CS[i]], q["aa"][:, CS[i]], QTHR[4], 2.0,
                    ALU.is_gt, ALU.mult))
                vop("u1", i, nc.vector.tensor_scalar(
                    q["c5"][:, CS[i]], q["aa"][:, CS[i]], QTHR[5], 2.0,
                    ALU.is_gt, ALU.mult))
                vop("u2", i, nc.vector.tensor_scalar(
                    q["c6"][:, CS[i]], q["aa"][:, CS[i]], QTHR[6], 4.0,
                    ALU.is_gt, ALU.mult))
            for i in (0, 1):
                vw(mk[("u1", i)])
                vop("v1", i, nc.vector.tensor_add(
                    q["s2"][:, CS[i]], q["s2"][:, CS[i]], q["c5"][:, CS[i]]))
            for i in (0, 1):
                vw(mk[("t3", i)])
                vw(mk[("v1", i)])
                vop("v2", i, nc.vector.tensor_add(
                    q["s1"][:, CS[i]], q["s1"][:, CS[i]], q["s2"][:, CS[i]]))
            for i in (0, 1):
                vw(mk[("u2", i)])
                vw(mk[("v2", i)])
                vop("fin", i, nc.vector.tensor_add(
                    q["s1"][:, CS[i]], q["s1"][:, CS[i]], q["c6"][:, CS[i]]))
            for i in (0, 1):
                vw(mk[("fin", i)])
                nc.vector.wait_ge(sems["gp"], gp_ss[i])
                vop("ds", i, nc.vector.tensor_mul(
                    dst[:, CS[i]], q["s1"][:, CS[i]], q["ss"][:, CS[i]]))

        @block.gpsimd
        def _(gpsimd):
            w(gpsimd, "ac", AC)
            if q.get("prev") is not None:
                w(gpsimd, "ve", q["prev"])
            for i in (0, 1):
                gpsimd.wait_ge(sems["ve"], mk[("sh", i)])
                ii = nc.gpsimd.tensor_mul(
                    q["ss"][:, CS[i]], q["sg"][:, CS[i]],
                    q["sclh16"][:, GS[i]].unsqueeze(2)
                    .broadcast_to([128, hb, 16]))
                inc(ii, "gp")
        assert cnt["gp"] == gp_ss[1]
        q["prev"] = cnt["ve"]
        q["prev_gp"] = cnt["gp"]
        q["prev_absx"] = mk[("aa", 1)]
        q["prev_sg"] = gp_ss[1]
        return cnt["ve"]

    def emit_transpose(blocks, bankA, bankB, ident_t, rows_in=128, pe_waits=()):
        same = bankA is bankB
        ngr = (len(blocks) + 3) // 4
        ev_done = [None, None]
        for g in range(ngr):
            bank = (bankA, bankB)[g % 2]
            cur = blocks[g * 4:(g + 1) * 4]
            @block.tensor
            def _(tensor):
                if g == 0:
                    for s_, v_ in pe_waits:
                        w(tensor, s_, v_)
                    w(tensor, "ve")
                guard = ev_done[(g - 1) % 2] if same else ev_done[g % 2]
                if guard is not None:
                    w(tensor, "ve", guard)
                for j, (srcap, dstap) in enumerate(cur):
                    i = nc.tensor.transpose(
                        bank[:, j * rows_in:(j + 1) * rows_in], srcap,
                        ident_t[0:rows_in, 0:rows_in])
                inc(i, "pe")
            PE_V = cnt["pe"]
            @block.vector
            def _(vector):
                w(vector, "pe", PE_V)
                for j, (srcap, dstap) in enumerate(cur):
                    i = nc.vector.tensor_copy(
                        dstap, bank[0:dstap.shape[0],
                                    j * rows_in: j * rows_in + dstap.shape[1]])
                inc(i, "ve")
            ev_done[g % 2] = cnt["ve"]
        return cnt["ve"]

    def emit_l2norm_prep(w_hbm, n_rows, ncols, wld, wsq, wssq, wrn, wbf,
                         bankA, bankB, dstT, scale_const):
        kk = ncols // 128
        n_tiles = (n_rows + 127) // 128
        buf_free = [None, None]
        for wt in range(n_tiles):
            rows = min(128, n_rows - wt * 128)
            buf = wld[wt % 2]
            @block.sync
            def _(sync):
                if wt < 2:
                    w(sync, "ve")
                    w(sync, "ac")
                if buf_free[wt % 2] is not None:
                    w(sync, "ve", buf_free[wt % 2])
                i = sync.dma_start(out=buf[0:rows, 0:ncols],
                                   in_=w_hbm[wt * 128: wt * 128 + rows, :])
                inc(i, "sy", 16)
                sync.wait_ge(sems["sy"], cnt["sy"])
            @block.scalar
            def _(scalar):
                w(scalar, "sy")
                w(scalar, "ve")
                nc.scalar.activation(wsq[0:rows, 0:ncols], buf[0:rows, 0:ncols],
                                     AF.Square, accum_out=wssq[0:rows, :])
                nc.scalar.drain()
                nc.scalar.activation(wrn[0:rows, :], wssq[0:rows, :], AF.Ln)
                nc.scalar.drain()
                i = nc.scalar.activation(wrn[0:rows, :], wrn[0:rows, :], AF.Exp,
                                         scale=-0.5)
                inc(i, "ac")
            @block.vector
            def _(vector):
                w(vector, "ac")
                w(vector, "pe")
                nc.vector.tensor_scalar(wrn[0:rows, :], wrn[0:rows, :], 1e6,
                                        scale_const, ALU.min, ALU.mult)
                nc.vector.drain()
                i = nc.vector.tensor_scalar_mul(wbf[0:rows, 0:ncols],
                                                buf[0:rows, 0:ncols],
                                                wrn[0:rows, 0:1])
                inc(i, "ve")
            buf_free[wt % 2] = cnt["ve"]
            blocks = [(wbf[0:rows, k * 128:(k + 1) * 128],
                       dstT[:, k * n_rows + wt * 128: k * n_rows + wt * 128 + rows])
                      for k in range(kk)]
            emit_transpose(blocks, bankA, bankB, identb, rows_in=rows,
                           pe_waits=[("ve", buf_free[wt % 2])])

    def emit_wquant_prep(w_hbm, n_rows, ncols, wld, qsc, wbf, bankA, bankB,
                         dstT, qn):
        kk = ncols // 128
        n_tiles = (n_rows + 127) // 128
        buf_free = [None, None]
        for wt in range(n_tiles):
            rows = min(128, n_rows - wt * 128)
            buf = wld[wt % 2]
            @block.sync
            def _(sync):
                if wt < 2:
                    w(sync, "ve")
                    w(sync, "ac")
                if buf_free[wt % 2] is not None:
                    w(sync, "ve", buf_free[wt % 2])
                i = sync.dma_start(out=buf[0:rows, 0:ncols],
                                   in_=w_hbm[wt * 128: wt * 128 + rows, :])
                inc(i, "sy", 16)
                sync.wait_ge(sems["sy"], cnt["sy"])
            ld_done = cnt["sy"]
            @block.vector
            def _(vector):
                w(vector, "pe")
                i = nc.vector.engine_nop()
                inc(i, "ve")
            for off in range(0, ncols, qn):
                nn = min(qn, ncols - off)
                emit_quant_full(buf[0:128, off:off + nn],
                                wbf[0:128, off:off + nn], nn, qsc,
                                first_wait=("sy", ld_done))
            buf_free[wt % 2] = cnt["ve"]
            blocks = [(wbf[0:rows, k * 128:(k + 1) * 128],
                       dstT[:, k * n_rows + wt * 128: k * n_rows + wt * 128 + rows])
                      for k in range(kk)]
            emit_transpose(blocks, bankA, bankB, identb, rows_in=rows,
                           pe_waits=[("ve", buf_free[wt % 2])])

    u_view = u_h.rearrange("(n p) d -> n p d", p=128)

    def trow(t):
        return (t % 4) * 16 + (t // 4) * 4

    with ExitStack() as stk, \
         nc.psum_tensor("ptrA1", [128, 512], BF16) as ptrA, \
         nc.psum_tensor("ptrB1", [128, 512], BF16) as ptrB, \
         nc.psum_tensor("pmm0", [128, TILE], F32) as pmm0, \
         nc.psum_tensor("pmm1", [128, TILE], F32) as pmm1, \
         nc.psum_tensor("pmm2", [128, TILE], F32) as pmm2, \
         nc.psum_tensor("pcr", [1, 512], F32) as pcr:
        sb1 = lambda name, shape, dt=F32: stk.enter_context(nc.sbuf_tensor(name, shape, dt))
        wnt = sb1("wnt", [128, KD * F], BF16)
        constrow = sb1("constrow", [1, F], F32)
        constrow_b = sb1("constrow_b", [1, F], BF16)
        pstk = ExitStack()
        sb1p = lambda name, shape, dt=F32: pstk.enter_context(
            nc.sbuf_tensor(name, shape, dt))
        wld = [sb1p(f"wld{i}", [128, D]) for i in range(2)]
        wsq = sb1p("wsq", [128, D])
        wssq = sb1p("wssq", [128, 1])
        wrn = sb1p("wrn", [128, 1])
        wbf = sb1p("wbf", [128, D], BF16)

        emit_l2norm_prep(win_h, F, D, wld, wsq, wssq, wrn, wbf, ptrA, ptrB,
                         wnt, D ** -0.5)

        for nsl in range(9):
            n0, n1 = nsl * 512, min(F, nsl * 512 + 512)
            @block.tensor
            def _(tensor):
                w(tensor, "ve")
                for k in range(KD):
                    i = nc.tensor.matmul(pcr[0:1, 0:n1 - n0], xbcolT_b[:, k:k + 1],
                                         wnt[:, k * F + n0: k * F + n1],
                                         start=(k == 0), stop=(k == KD - 1))
                inc(i, "pe")
            CR_PE = cnt["pe"]
            @block.vector
            def _(vector):
                w(vector, "pe", CR_PE)
                i = nc.vector.tensor_copy(constrow[0:1, n0:n1], pcr[0:1, 0:n1 - n0])
                inc(i, "ve")
        @block.vector
        def _(vector):
            w(vector, "ve")
            nc.vector.tensor_copy(constrow_b[:], constrow[:])
            for k in range(KD):
                i = nc.vector.tensor_scalar_mul(wnt[:, k * F:(k + 1) * F],
                                                wnt[:, k * F:(k + 1) * F],
                                                xwcol[:, k:k + 1])
            inc(i, "ve")
        WNT_VE = cnt["ve"]
        pstk.close()

        ubuf = sb1("ubuf", [128, 4 * D])
        xn = sb1("xn", [128, 4 * D])
        xh = sb1("xh", [128, 4 * D], BF16)
        xhT = sb1("xhT", [128, KD * TILE], BF16)
        qd = sb1("qd", [128, 4096], BF16)
        xnqT = sb1("xnqT", [128, KD * TILE], BF16)
        su = sb1("su", [128, 4])
        ssq4 = sb1("ssq4", [128, 4])
        mean4 = sb1("mean4", [128, 4])
        msq4 = sb1("msq4", [128, 4])
        var4 = sb1("var4", [128, 4])
        rstd4 = sb1("rstd4", [128, 4])
        bia4 = sb1("bia4", [128, 4])
        q1 = quant_scratch(stk, "q1", 1024, f16=True)
        stg_ev = [sb1(f"stg_ev{i}", [128, TILE], BF16) for i in range(2)]
        dtstg = sb1("dtstg", [16, TILE])

        u_done = [None] * (NT + 1)
        @block.sync
        def _(sync):
            w(sync, "ve", WNT_VE)
            i = sync.dma_start(
                out=ubuf[:].rearrange("p (n d) -> p n d", n=4),
                in_=u_view[trow(0):trow(0) + 4, :, :].rearrange("n p d -> p n d"))
            u_done[0] = inc(i, "sy", 16)
            sync.wait_ge(sems["sy"], cnt["sy"])

        def emit_ln_dbl(src_buf, xn_dst, xh_dst, waits):
            @block.scalar
            def _(scalar):
                for s_, v_ in waits:
                    w(scalar, s_, v_)
                w(scalar, "ac")
                w(scalar, "ve")
                for mt in range(4):
                    sl = src_buf[:, mt * D:(mt + 1) * D]
                    nc.scalar.activation(xn_dst[:, mt * D:(mt + 1) * D], sl,
                                         AF.Copy, accum_out=su[:, mt:mt + 1])
                    i = nc.scalar.activation(xh_dst[:, mt * D:(mt + 1) * D],
                                             sl, AF.Square,
                                             accum_out=ssq4[:, mt:mt + 1])
                inc(i, "ac")
            ACC_AC = cnt["ac"]
            @block.vector
            def _(vector):
                w(vector, "ac", ACC_AC)
                w(vector, "ve")
                nc.vector.tensor_scalar_mul(mean4[:], su[:], 1.0 / D)
                nc.vector.drain()
                nc.vector.tensor_mul(msq4[:], mean4[:], mean4[:])
                nc.vector.drain()
                i = nc.vector.scalar_tensor_tensor(var4[:], ssq4[:], 1.0 / D,
                                                   msq4[:], ALU.mult, ALU.subtract)
                inc(i, "ve")
            @block.scalar
            def _(scalar):
                w(scalar, "ve")
                nc.scalar.activation(rstd4[:], var4[:], AF.Ln, bias=epscol[:, 0:1])
                nc.scalar.drain()
                i = nc.scalar.activation(rstd4[:], rstd4[:], AF.Exp, scale=-0.5)
                inc(i, "ac")
            @block.vector
            def _(vector):
                w(vector, "ac")
                nc.vector.tensor_mul(msq4[:], var4[:], rstd4[:])
                nc.vector.drain()
                i = nc.vector.tensor_mul(msq4[:], msq4[:], rstd4[:])
                inc(i, "ve")
            @block.scalar
            def _(scalar):
                w(scalar, "ve")
                nc.scalar.activation(su[:], msq4[:], AF.Ln, bias=epscol[:, 0:1])
                nc.scalar.drain()
                i = nc.scalar.activation(su[:], su[:], AF.Exp, scale=-0.5)
                inc(i, "ac")
            @block.vector
            def _(vector):
                w(vector, "ac")
                w(vector, "ve")
                nc.vector.tensor_mul(var4[:], rstd4[:], su[:])
                nc.vector.scalar_tensor_tensor(bia4[:], mean4[:], -1.0,
                                               rstd4[:], ALU.mult, ALU.mult)
                nc.vector.drain()
                nc.vector.scalar_tensor_tensor(ssq4[:], mean4[:], -1.0,
                                               var4[:], ALU.mult, ALU.mult)
                nc.vector.drain()
                for mt in range(4):
                    nc.vector.tensor_scalar(xn_dst[:, mt * D:(mt + 1) * D],
                                            src_buf[:, mt * D:(mt + 1) * D],
                                            rstd4[:, mt:mt + 1],
                                            bia4[:, mt:mt + 1],
                                            ALU.mult, ALU.add)
                    i = nc.vector.tensor_scalar(xh_dst[:, mt * D:(mt + 1) * D],
                                                src_buf[:, mt * D:(mt + 1) * D],
                                                var4[:, mt:mt + 1],
                                                ssq4[:, mt:mt + 1],
                                                ALU.mult, ALU.add)
                inc(i, "ve")
            return ("ve", cnt["ve"])

        pm_free = [None, None, None]
        pm_free_sem = [None, None, None]
        stg_free = [None] * 2
        prev_qsg_ac = None
        prev_xht_pe = None
        prev_xnqt_po = None

        for t in range(ntl):
            s_seq, cpair = t % 4, t // 4

            lnw = [("sy", u_done[t]), ("ve", WNT_VE)]
            if t > 0:
                lnw += [("ac", prev_qsg_ac), ("pe", prev_xht_pe),
                        ("po", prev_xnqt_po)]
            XN_M = emit_ln_dbl(ubuf, xn, xh, lnw)
            XH_M = XN_M

            if t + 1 < NT:
                @block.sync
                def _(sync):
                    w(sync, XN_M[0], XN_M[1])
                    i = sync.dma_start(
                        out=ubuf[:].rearrange("p (n d) -> p n d", n=4),
                        in_=u_view[trow(t + 1):trow(t + 1) + 4, :, :]
                        .rearrange("n p d -> p n d"))
                    u_done[t + 1] = inc(i, "sy", 16)
                    sync.wait_ge(sems["sy"], cnt["sy"])

            blocks = [(xh[:, mt * D + k * 128: mt * D + (k + 1) * 128],
                       xhT[:, k * TILE + mt * 128: k * TILE + (mt + 1) * 128])
                      for mt in range(4) for k in range(KD)]
            XHT_VE = emit_transpose(blocks, ptrA, ptrB, identb,
                                    pe_waits=[XH_M])
            prev_xht_pe = cnt["pe"]
            qtrans = []

            def emit_q1(sl4):
                QS_VE = emit_quant_full(xn[:, sl4 * 1024:(sl4 + 1) * 1024],
                                        qd[:, sl4 * 1024:(sl4 + 1) * 1024],
                                        1024, q1, first_wait=XN_M)
                blocks = [(qd[:, sl4 * 1024 + j * 128:
                              sl4 * 1024 + (j + 1) * 128],
                           xnqT[:, j * TILE + sl4 * 128:
                                j * TILE + (sl4 + 1) * 128])
                          for j in range(8)]
                qtrans.append((blocks, QS_VE))
                return cnt["ac"]

            xbv = xbc_s.rearrange("(s c p) l -> s c p l", s=SEQ, c=NCT)
            szv = siluz_s.rearrange("(t z p) n -> t z p n", t=NT, z=NZT)
            for ft in range(NFT):
                if ft % 8 == 1 and ft < 32:
                    qac = emit_q1(ft // 8)
                    if ft // 8 == 3:
                        prev_qsg_ac = qac
                rows = 128 if ft < 33 else 16
                pm = (pmm0, pmm1, pmm2)[ft % 3]
                @block.tensor
                def _(tensor):
                    if ft == 0:
                        w(tensor, "ve", XHT_VE)
                    if pm_free[ft % 3] is not None:
                        w(tensor, pm_free_sem[ft % 3], pm_free[ft % 3])
                    for k in range(KD):
                        nc.tensor.matmul(
                            pm[0:rows, :],
                            wnt[:, k * F + ft * 128: k * F + ft * 128 + rows],
                            xhT[:, k * TILE:(k + 1) * TILE],
                            start=(k == 0), stop=False)
                    i = nc.tensor.matmul(pm[0:rows, :],
                                         constrow_b[0:1, ft * 128: ft * 128 + rows],
                                         ones_r512b[:], start=False, stop=True)
                    inc(i, "pe")
                MM_PE = cnt["pe"]
                if ft < 33:
                    sbuf_ev = stg_ev[ft % 2]
                    @block.scalar
                    def _(scalar):
                        w(scalar, "pe", MM_PE)
                        if stg_free[ft % 2] is not None:
                            w(scalar, stg_free[ft % 2][0], stg_free[ft % 2][1])
                        i = nc.scalar.activation(sbuf_ev[:], pm[:], AF.Copy)
                        inc(i, "ac")
                    EV_AC = cnt["ac"]
                    pm_free[ft % 3], pm_free_sem[ft % 3] = EV_AC, "ac"
                    if ft < 16:
                        @block.gpsimd
                        def _(gpsimd):
                            w(gpsimd, "ac", EV_AC)
                            i = gpsimd.dma_start(out=szv[t, ft], in_=sbuf_ev[:])
                            inc(i, "po", 16)
                            gpsimd.wait_ge(sems["po"], cnt["po"])
                        stg_free[ft % 2] = ("po", cnt["po"])
                    else:
                        ct = ft - 16
                        @block.sync
                        def _(sync):
                            w(sync, "ac", EV_AC)
                            i = sync.dma_start(
                                out=xbv[s_seq, ct, :,
                                        3 + cpair * 512: 3 + (cpair + 1) * 512],
                                in_=sbuf_ev[:])
                            inc(i, "sy", 16)
                            sync.wait_ge(sems["sy"], cnt["sy"])
                        stg_free[ft % 2] = ("sy", cnt["sy"])
                else:
                    @block.vector
                    def _(vector):
                        w(vector, "pe", MM_PE)
                        w(vector, "po")
                        i = nc.vector.tensor_copy(dtstg[:], pm[0:16, :])
                        inc(i, "ve")
                    DT_VE = cnt["ve"]
                    pm_free[ft % 3], pm_free_sem[ft % 3] = DT_VE, "ve"
                    @block.gpsimd
                    def _(gpsimd):
                        w(gpsimd, "ve", DT_VE)
                        i = gpsimd.dma_start(out=dt_s[t * 16:(t + 1) * 16, :],
                                             in_=dtstg[:])
                        inc(i, "po", 16)
                        gpsimd.wait_ge(sems["po"], cnt["po"])

            for blocks, qsv in qtrans:
                emit_transpose(blocks, ptrA, ptrB, identb,
                               pe_waits=[("ve", qsv)])
            XNQT_VE = cnt["ve"]

            @block.gpsimd
            def _(gpsimd):
                w(gpsimd, "ve", XNQT_VE)
                i = gpsimd.dma_start(
                    out=xnqT_s.rearrange("(t k p) n -> t p k n", t=NT, k=KD)[t],
                    in_=xnqT[:].rearrange("p (k n) -> p k n", k=KD))
                inc(i, "po", 16)
                gpsimd.wait_ge(sems["po"], cnt["po"])
            prev_xnqt_po = cnt["po"]

        @block.gpsimd
        def _(gpsimd):
            w(gpsimd, "po")

    barrier()
    if maxphase <= 1:
        ctx.close()
        return nc

    with ExitStack() as stk, \
         nc.psum_tensor("p_sc", [128, 512], F32) as p_sc, \
         nc.psum_tensor("p_seg", [128, 512], F32) as p_seg, \
         nc.psum_tensor("p_y0", [128, 512], F32) as p_y0, \
         nc.psum_tensor("p_y1", [128, 512], F32) as p_y1, \
         nc.psum_tensor("p_st", [128, 512], F32) as p_st, \
         nc.psum_tensor("p_ssq", [128, 512], F32) as p_ssq, \
         nc.psum_tensor("p_rep", [128, 512], F32) as p_rep, \
         nc.psum_tensor("ptr2", [128, 512], BF16) as ptr2:
        sb2 = lambda name, shape, dt=F32: stk.enter_context(nc.sbuf_tensor(name, shape, dt))
        wot = sb2("wot", [128, 16 * 1024], BF16)
        accA = sb2("accA", [128, 4 * TILE])
        accB = sb2("accB", [128, 4 * TILE])
        diagb = sb2("diagb", [128, 2 * 512], BF16)
        xbcw = sb2("xbcw", [128, NCT * 515], BF16)
        x_j = sb2("x_j", [128, 4 * DI], BF16)
        zld = sb2("zld", [128, NZT * TILE], BF16)
        bdec = sb2("bdec", [128, 4 * 1024], BF16)
        wssq2 = sb2("wssq2", [128, 1])
        wrn2 = sb2("wrn2", [128, 1])

        ubp = sb2("ubp", [128, DI])
        emit_l2norm_prep(wout_h, D, DI, [accA, accB], ubp, wssq2, wrn2,
                         x_j, ptr2, ptr2, wot, 1.0)
        WOT_VE = cnt["ve"]

        dtr = accA[0:16, 0:512]
        sp_t1 = accA[0:16, 512:1024]
        sp_t2 = accA[0:16, 1024:1536]
        dt_ht = accA[0:16, 1536:2048]
        ldt_t = accB[0:16, 0:512]
        dA_t = accB[0:16, 512:1024]
        cum = accB[0:16, 1024:1536]
        r1 = accB[0:16, 1536:2048]
        dd2 = sb2("dd2", [16, 1024])
        ddec = dd2[0:16, 0:512]
        dtdec = dd2[0:16, 512:1024]
        chi = sb2("chi", [16, TILE], BF16)
        cmid = sb2("cmid", [16, TILE], BF16)
        clo = sb2("clo", [16, TILE], BF16)
        ec_b = sb2("ec_b", [16, TILE], BF16)
        cdf = sb2("cdf", [16, 2])
        ldtT = sb2("ldtT", [128, 4 * 16])
        dtdecT = sb2("dtdecT", [128, 4 * 16])
        bjn = sb2("bjn", [128, 4 * 64], BF16)
        segL = sb2("segL", [6, 4096], BF16)
        segR = sb2("segR", [6, 4096], BF16)
        ecrow = sb2("ecrow", [1, 4096], BF16)
        cdrow = sb2("cdrow", [1, 16])
        cdrepS = sb2("cdrepS", [128, 16])
        cs2 = sb2("cs2", [128, 256], BF16)
        scoresM = sb2("scoresM", [128, 512], BF16)
        relb = [sb2(f"relb{i}", [128, 512], BF16) for i in range(3)]
        eb = [sb2(f"eb{i}", [128, 512], BF16) for i in range(3)]
        attb = [sb2(f"attb{i}", [128, 512], BF16) for i in range(3)]
        cecb = sb2("cecb", [128, 8 * 256], BF16)
        g2scr = [sb2(f"g2scr{i}", [128, 256], BF16) for i in range(2)]
        gbuf = xbcw[:, 0:H * TILE]
        rstd_rf = sb2("rstd_rf", [1, 512])
        rstd_rb = sb2("rstd_rb", [1, 512], BF16)
        rep_rstd = sb2("rep_rstd", [128, 512], BF16)
        ub2 = [ubp[:, 0:D], ubp[:, D:2 * D]]
        pstg = [sb2(f"pstg{i}", [128, D]) for i in range(2)]

        @block.vector
        def _(vector):
            w(vector, "ve", WOT_VE)
            nc.vector.memset(segL[:], 1.0)
            i = nc.vector.memset(segR[:], -1.0)
            inc(i, "ve")
        SEGI_VE = cnt["ve"]

        tile_res_ve = None
        tile_res_po = None
        pst_done = [None, None]
        for t in range(ntl):
            s_seq, cpair = t % 4, t // 4

            @block.sync
            def _(sync):
                if tile_res_ve is not None:
                    w(sync, "ve", tile_res_ve)
                xbv = xbc_s.rearrange("(s c p) l -> s c p l", s=SEQ, c=NCT)
                for c in range(NCT):
                    i = sync.dma_start(
                        out=xbcw[:, c * 515:(c + 1) * 515],
                        in_=xbv[s_seq, c, :, cpair * 512: cpair * 512 + 515])
                    inc(i, "sy", 16)
                i = sync.dma_start(
                    out=zld[:].rearrange("p (z n) -> p z n", z=NZT),
                    in_=siluz_s.rearrange("(t z p) n -> t p z n", t=NT, z=NZT)[t])
                inc(i, "sy", 16)
                sync.wait_ge(sems["sy"], cnt["sy"])
            LD_SY = cnt["sy"]

            conv_free = [None, None]
            diag_free = [None, None]
            for c in range(NCT):
                slot = c % 2
                pcv = (p_sc, p_seg)[slot]
                @block.vector
                def _(vector):
                    w(vector, "sy", LD_SY)
                    if c < 2:
                        w(vector, "ve")
                        w(vector, "pe")
                    if diag_free[slot] is not None:
                        w(vector, "pe", diag_free[slot])
                    for j in range(4):
                        i = nc.vector.tensor_scalar_mul(
                            diagb[:, slot * 512 + j * 128:
                                  slot * 512 + (j + 1) * 128],
                            identb[:], convw[:, c * 4 + j: c * 4 + j + 1])
                    inc(i, "ve")
                DG_VE = cnt["ve"]
                @block.tensor
                def _(tensor):
                    w(tensor, "ve", DG_VE)
                    if conv_free[slot] is not None:
                        w(tensor, "ac", conv_free[slot])
                    elif c < 2:
                        w(tensor, "ac")
                        w(tensor, "ve")
                    for j in range(4):
                        i = nc.tensor.matmul(
                            pcv[:, 0:512],
                            diagb[:, slot * 512 + j * 128:
                                  slot * 512 + (j + 1) * 128],
                            xbcw[:, c * 515 + j: c * 515 + j + 512],
                            start=(j == 0), stop=(j == 3))
                    inc(i, "pe")
                CV_PE = cnt["pe"]
                diag_free[slot] = CV_PE
                @block.scalar
                def _(scalar):
                    w(scalar, "pe", CV_PE)
                    if c == 0:
                        load_act_set(SET_SILU)
                    i = nc.scalar.activation(
                        xbcw[:, c * 515 + 3: c * 515 + 515], pcv[:, 0:512],
                        AF.Silu, bias=convb[:, c:c + 1])
                    inc(i, "ac")
                conv_free[slot] = cnt["ac"]
            @block.scalar
            def _(scalar):
                w(scalar, "sy", LD_SY)
                i = nc.scalar.activation(zld[:], zld[:], AF.Silu)
                inc(i, "ac")
            ZSIL_AC = cnt["ac"]
            CONV_AC = cnt["ac"]
            if p2stage <= 1:
                continue

            def xs(ct, tok0, nt_, rows=slice(0, 128)):
                return xbcw[rows, ct * 515 + 3 + tok0: ct * 515 + 3 + tok0 + nt_]

            @block.sync
            def _(sync):
                w(sync, "ve")
                w(sync, "ac")
                i = sync.dma_start(out=dtr, in_=dt_s[t * 16:(t + 1) * 16, :])
                inc(i, "sy", 16)
                sync.wait_ge(sems["sy"], cnt["sy"])
            DTL_SY = cnt["sy"]
            @block.scalar
            def _(scalar):
                w(scalar, "sy", DTL_SY)
                load_act_set(SET_NLE)
                nc.scalar.activation(sp_t1, dtr, AF.Abs, bias=dtbcol[:, 0:1])
                nc.scalar.drain()
                nc.scalar.activation(sp_t1, sp_t1, AF.Exp, scale=-1.0)
                nc.scalar.drain()
                nc.scalar.activation(sp_t2, sp_t1, AF.Ln, bias=1.0)
                nc.scalar.drain()
                i = nc.scalar.activation(sp_t1, dtr, AF.Relu, bias=dtbcol[:, 0:1])
                inc(i, "ac")
            @block.vector
            def _(vector):
                w(vector, "ac")
                nc.vector.tensor_add(dt_ht, sp_t1, sp_t2)
                nc.vector.drain()
                nc.vector.tensor_scalar_mul(dA_t, dt_ht, acol[:, 0:1])
                nc.vector.drain()
                for c in range(2):
                    sl = slice(c * CHUNK, (c + 1) * CHUNK)
                    nc.vector.tensor_tensor_scan(cum[:, sl], dA_t[:, sl],
                                                 dA_t[:, sl], 0.0, ALU.add,
                                                 ALU.bypass)
                nc.vector.drain()
                nc.vector.tensor_copy(chi[:], cum)
                nc.vector.drain()
                nc.vector.scalar_tensor_tensor(r1, cum, 1.0, chi[:],
                                               ALU.mult, ALU.subtract)
                nc.vector.drain()
                nc.vector.tensor_copy(cmid[:], r1)
                nc.vector.drain()
                nc.vector.scalar_tensor_tensor(r1, r1, 1.0, cmid[:],
                                               ALU.mult, ALU.subtract)
                nc.vector.drain()
                i = nc.vector.tensor_copy(clo[:], r1)
                inc(i, "ve")
            CUM_VE = cnt["ve"]
            @block.scalar
            def _(scalar):
                w(scalar, "ve", CUM_VE)
                for c in range(2):
                    sl = slice(c * CHUNK, (c + 1) * CHUNK)
                    last = slice(c * CHUNK + 255, c * CHUNK + 256)
                    nc.scalar.activation(ddec[:, c * 256:(c + 1) * 256],
                                         cum[:, sl], AF.Exp,
                                         scale=-1.0, bias=cum[:, last])
                    nc.scalar.activation(cdf[:, c:c + 1], cum[:, last], AF.Exp)
                i = nc.scalar.activation(ec_b[:], cum, AF.Exp)
                inc(i, "ac")
            DT_AC = cnt["ac"]
            @block.vector
            def _(vector):
                w(vector, "ac", DT_AC)
                i = nc.vector.tensor_mul(dtdec, dt_ht, ddec)
                inc(i, "ve")
            DTD_VE = cnt["ve"]

            blocks = [(dt_ht[0:16, tb * 128:(tb + 1) * 128],
                       ldtT[:, tb * 16:(tb + 1) * 16]) for tb in range(4)]
            emit_transpose(blocks, p_seg, p_seg, identf, rows_in=16,
                           pe_waits=[("ac", DT_AC)])
            blocks = [(dtdec[0:16, tb * 128:(tb + 1) * 128],
                       dtdecT[:, tb * 16:(tb + 1) * 16]) for tb in range(4)]
            emit_transpose(blocks, p_seg, p_seg, identf, rows_in=16,
                           pe_waits=[("ve", DTD_VE)])
            blocks = [(xs(zt, tb * 128, 128),
                       x_j[:, tb * DI + zt * 128: tb * DI + (zt + 1) * 128])
                      for tb in range(4) for zt in range(16)]
            emit_transpose(blocks, ptr2, ptr2, identb, rows_in=128,
                           pe_waits=[("ac", CONV_AC)])
            blocks = [(xs(16, tb * 128, 128, rows=slice(0, 64)),
                       bjn[:, tb * 64:(tb + 1) * 64]) for tb in range(4)]
            emit_transpose(blocks, ptr2, ptr2, identb, rows_in=64)
            TR_VE = cnt["ve"]

            @block.vector
            def _(vector):
                w(vector, "ve", TR_VE)
                for tb in range(4):
                    i = nc.vector.tensor_mul(
                        bdec[:, tb * 1024:(tb + 1) * 1024],
                        bjn[:, tb * 64:(tb + 1) * 64].unsqueeze(1)
                        .broadcast_to([128, 16, 64]),
                        dtdecT[:, tb * 16:(tb + 1) * 16].unsqueeze(2)
                        .broadcast_to([128, 16, 64]))
                inc(i, "ve")
            BDEC_VE = cnt["ve"]
            if p2stage <= 2:
                continue

            py_free = {0: None, 1: None}
            ebwar = {0: None, 1: None, 2: None}
            attbwar = {0: None, 1: None, 2: None}
            stbank_free = {0: None, 1: None}
            relseg_free = {0: None, 1: None}
            ssq_pe_prev = {0: None, 1: None}
            last_seg_pe = None
            last_cec_ve = None
            psc_free = None
            rstd_prev_ac = None
            for c in range(2):
                csl = slice(c * CHUNK, (c + 1) * CHUNK)
                @block.sync
                def _(sync):
                    w(sync, "ve", CUM_VE)
                    w(sync, "ac", DT_AC)
                    w(sync, "pe")
                    for row, srct in ((3, chi), (4, cmid), (5, clo)):
                        i = sync.dma_start(
                            out=segL[row:row + 1, :].rearrange(
                                "o (h i) -> o h i", h=16),
                            in_=srct[:, csl])
                        inc(i, "sy", 16)
                        i = sync.dma_start(
                            out=segR[row - 3:row - 2, :].rearrange(
                                "o (h i) -> o h i", h=16),
                            in_=srct[:, csl])
                        inc(i, "sy", 16)
                    i = sync.dma_start(
                        out=ecrow[:].rearrange("o (h i) -> o h i", h=16),
                        in_=ec_b[:, csl])
                    inc(i, "sy", 16)
                    i = sync.dma_start(out=cdrow[:].unsqueeze(2),
                                       in_=cdf[:, c:c + 1])
                    inc(i, "sy", 16)
                    i = sync.dma_start(out=cs2[0:64, :],
                                       in_=xs(16, c * CHUNK, CHUNK,
                                              rows=slice(64, 128)))
                    inc(i, "sy", 16)
                    i = sync.dma_start(out=cs2[64:128, :],
                                       in_=xs(16, c * CHUNK, CHUNK,
                                              rows=slice(64, 128)))
                    inc(i, "sy", 16)
                    sync.wait_ge(sems["sy"], cnt["sy"])
                STG_SY = cnt["sy"]

                @block.vector
                def _(vector):
                    w(vector, "ve")
                    i = nc.vector.tensor_copy(carry_b[:], carryA[s_seq][:])
                    inc(i, "ve")
                CB_VE = cnt["ve"]

                @block.tensor
                def _(tensor):
                    w(tensor, "ac", CONV_AC)
                    w(tensor, "sy", STG_SY)
                    if psc_free is not None:
                        w(tensor, "ve", psc_free)
                    for jt in range(2):
                        i = nc.tensor.matmul(
                            p_sc[:, jt * 256:(jt + 1) * 256],
                            xs(16, c * CHUNK + jt * 128, 128, slice(0, 64)),
                            cs2[0:64, :], start=True, stop=True)
                    inc(i, "pe")
                SCO_PE = cnt["pe"]
                @block.vector
                def _(vector):
                    w(vector, "pe", SCO_PE)
                    for jt in range(2):
                        i = nc.vector.tensor_mul(
                            scoresM[:, jt * 256:(jt + 1) * 256],
                            p_sc[:, jt * 256:(jt + 1) * 256], masks[jt][:])
                    inc(i, "ve")
                SCOM_VE = cnt["ve"]
                psc_free = SCOM_VE

                @block.tensor
                def _(tensor):
                    w(tensor, "sy", STG_SY)
                    w(tensor, "ve")
                    i = nc.tensor.matmul(p_rep[:, 0:16],
                                         ones_r128f[0:1, :], cdrow[:],
                                         start=True, stop=True)
                    inc(i, "pe")
                CDR_PE = cnt["pe"]
                @block.vector
                def _(vector):
                    w(vector, "pe", CDR_PE)
                    i = nc.vector.tensor_copy(cdrepS[:], p_rep[:, 0:16])
                    inc(i, "ve")
                CDREP_VE = cnt["ve"]

                for b4 in range(8):
                    rb = 0 if b4 < 4 else 64
                    @block.tensor
                    def _(tensor):
                        w(tensor, "ve")
                        for j in range(2):
                            hh = b4 * 2 + j
                            i = nc.tensor.matmul(
                                p_rep[rb:rb + 64, j * 256:(j + 1) * 256],
                                ones_r128b[0:1, 0:64],
                                ecrow[0:1, hh * 256:(hh + 1) * 256],
                                start=True, stop=True)
                        inc(i, "pe")
                    REP_PE = cnt["pe"]
                    @block.vector
                    def _(vector):
                        w(vector, "pe", REP_PE)
                        i = nc.vector.tensor_mul(
                            cecb[rb:rb + 64, (b4 % 4) * 512:(b4 % 4 + 1) * 512],
                            p_rep[rb:rb + 64, 0:512],
                            cs2[rb:rb + 64, :].unsqueeze(1)
                            .broadcast_to([64, 2, 256]))
                        inc(i, "ve")
                CEC_VE = cnt["ve"]
                if p2stage <= 3:
                    continue

                seg_pe = {}
                g2_ac = {}
                ssq_pe = {}

                def emit_seg(hh):
                    pcs_ = (p_seg, p_sc)[hh % 2]
                    @block.tensor
                    def _(tensor):
                        if relseg_free[hh % 2] is not None:
                            w(tensor, "ac", relseg_free[hh % 2])
                        elif hh % 2 == 0:
                            w(tensor, "ve", BDEC_VE)
                        else:
                            w(tensor, "ve", psc_free if psc_free else SCOM_VE)
                        w(tensor, "sy", STG_SY)
                        for jt in range(2):
                            i = nc.tensor.matmul(
                                pcs_[:, jt * 256:(jt + 1) * 256],
                                segL[0:6, hh * 256 + jt * 128:
                                     hh * 256 + (jt + 1) * 128],
                                segR[0:6, hh * 256:(hh + 1) * 256],
                                start=True, stop=True)
                        inc(i, "pe")
                    seg_pe[hh] = cnt["pe"]

                def emit_ssq(hh):
                    @block.tensor
                    def _(tensor):
                        w(tensor, "gp", g2_ac[hh])
                        if c == 1 and hh == 0:
                            w(tensor, "ac", rstd_prev_ac)
                        i = nc.tensor.matmul(
                            p_ssq[0:1, c * 256:(c + 1) * 256],
                            ones_c128b[:], g2scr[hh % 2][:],
                            start=(hh == 0), stop=(hh == H - 1))
                        inc(i, "pe")
                    ssq_pe[hh] = cnt["pe"]

                att_ve = {}

                def emit_actatt(hh):
                    pcs_ = (p_seg, p_sc)[hh % 2]
                    SEG_PE_ = seg_pe[hh]
                    EBW = ebwar[hh % 3]
                    @block.scalar
                    def _(scalar):
                        w(scalar, "pe", SEG_PE_)
                        if EBW is not None:
                            w(scalar, "ve", EBW)
                        i = nc.scalar.activation(
                            relb[hh % 3][:], pcs_[:], AF.Relu, scale=-1.0)
                        inc(i, "ac")
                        scalar.drain()
                        i = nc.scalar.activation(
                            eb[hh % 3][:], relb[hh % 3][:], AF.Exp, scale=-1.0)
                        inc(i, "ac")
                    E_AC_ = cnt["ac"]
                    relseg_free[hh % 2] = E_AC_ - 1
                    ATTBW = attbwar[hh % 3]
                    @block.vector
                    def _(vector):
                        w(vector, "ac", E_AC_)
                        w(vector, "ve", SCOM_VE)
                        if ATTBW is not None:
                            w(vector, "pe", ATTBW)
                        for jt in range(2):
                            tb = c * 2 + jt
                            i = nc.vector.scalar_tensor_tensor(
                                attb[hh % 3][:, jt * 256:(jt + 1) * 256],
                                eb[hh % 3][:, jt * 256:(jt + 1) * 256],
                                ldtT[:, tb * 16 + hh: tb * 16 + hh + 1],
                                scoresM[:, jt * 256:(jt + 1) * 256],
                                ALU.mult, ALU.mult)
                        inc(i, "ve")
                    ebwar[hh % 3] = cnt["ve"]
                    att_ve[hh] = cnt["ve"]

                emit_seg(0)
                emit_seg(1)
                emit_actatt(0)
                for h in range(H):
                    tb0, tb1 = c * 2, c * 2 + 1
                    hb, hc = cslc(h)
                    pyb = (p_y0, p_y1)[h % 2]
                    sbank = (p_st, p_rep)[h % 2]
                    ATT_VE = att_ve[h]

                    @block.tensor
                    def _(tensor):
                        w(tensor, "ve", ATT_VE)
                        w(tensor, "ve", CEC_VE)
                        w(tensor, "ve", CB_VE)
                        w(tensor, "ve", BDEC_VE)
                        if py_free[h % 2] is not None:
                            w(tensor, "ve", py_free[h % 2])
                        nc.tensor.matmul(
                            pyb[:, 0:256],
                            x_j[:, tb0 * DI + h * 128: tb0 * DI + (h + 1) * 128],
                            attb[h % 3][:, 0:256], start=True, stop=False)
                        nc.tensor.matmul(
                            pyb[:, 0:256],
                            x_j[:, tb1 * DI + h * 128: tb1 * DI + (h + 1) * 128],
                            attb[h % 3][:, 256:512], start=False, stop=False)
                        nc.tensor.matmul(
                            pyb[:, 0:128],
                            x_j[:, tb0 * DI + h * 128: tb0 * DI + (h + 1) * 128],
                            i_d[:, h * 128:(h + 1) * 128], start=False, stop=False)
                        nc.tensor.matmul(
                            pyb[:, 128:256],
                            x_j[:, tb1 * DI + h * 128: tb1 * DI + (h + 1) * 128],
                            i_d[:, h * 128:(h + 1) * 128], start=False, stop=False)
                        i = nc.tensor.matmul(
                            pyb[:, 0:256],
                            carry_b[hb:hb + 64, hc:hc + 128],
                            cecb[hb:hb + 64, (h % 8) * 256:(h % 8 + 1) * 256],
                            start=False, stop=True)
                        inc(i, "pe")
                        if stbank_free[h % 2] is not None:
                            w(tensor, "ve", stbank_free[h % 2])
                        nc.tensor.matmul(
                            p_st[hb:hb + 64, 0:128] if h % 2 == 0 else
                            p_rep[hb:hb + 64, 0:128],
                            bdec[:, tb0 * 1024 + h * 64: tb0 * 1024 + (h + 1) * 64],
                            x_j[:, tb0 * DI + h * 128: tb0 * DI + (h + 1) * 128],
                            start=True, stop=False)
                        i = nc.tensor.matmul(
                            p_st[hb:hb + 64, 0:128] if h % 2 == 0 else
                            p_rep[hb:hb + 64, 0:128],
                            bdec[:, tb1 * 1024 + h * 64: tb1 * 1024 + (h + 1) * 64],
                            x_j[:, tb1 * DI + h * 128: tb1 * DI + (h + 1) * 128],
                            start=False, stop=True)
                        inc(i, "pe")
                    ST_PE = cnt["pe"]
                    attbwar[h % 3] = ST_PE - 1

                    if h + 2 < H:
                        emit_seg(h + 2)
                    if h + 1 < H:
                        emit_actatt(h + 1)
                    if h >= 1:
                        emit_ssq(h - 1)

                    @block.vector
                    def _(vector):
                        w(vector, "pe", ST_PE)
                        w(vector, "ve", CDREP_VE)
                        nc.vector.scalar_tensor_tensor(
                            carryA[s_seq][hb:hb + 64, hc:hc + 128],
                            carryA[s_seq][hb:hb + 64, hc:hc + 128],
                            cdrepS[hb:hb + 64, h:h + 1],
                            sbank[hb:hb + 64, 0:128],
                            ALU.mult, ALU.add)
                        if h == 0 and c == 0:
                            w(vector, "ac", ZSIL_AC)
                        i = nc.vector.tensor_mul(
                            gbuf[:, h * TILE + c * 256: h * TILE + (c + 1) * 256],
                            pyb[:, 0:256],
                            zld[:, h * TILE + c * 256: h * TILE + (c + 1) * 256])
                        inc(i, "ve")
                    GE_VE = cnt["ve"]
                    stbank_free[h % 2] = GE_VE
                    py_free[h % 2] = GE_VE

                    @block.gpsimd
                    def _(gpsimd):
                        w(gpsimd, "ve", GE_VE)
                        if ssq_pe.get(h - 2) is not None:
                            w(gpsimd, "pe", ssq_pe[h - 2])
                        i = nc.gpsimd.tensor_mul(
                            g2scr[h % 2][:],
                            gbuf[:, h * TILE + c * 256: h * TILE + (c + 1) * 256],
                            gbuf[:, h * TILE + c * 256: h * TILE + (c + 1) * 256])
                        inc(i, "gp")
                    g2_ac[h] = cnt["gp"]
                emit_ssq(H - 1)
                SSQ_PE = cnt["pe"]
                if p2stage <= 4:
                    continue

                @block.scalar
                def _(scalar):
                    w(scalar, "pe", SSQ_PE)
                    nc.scalar.activation(rstd_rf[0:1, csl], p_ssq[0:1, csl],
                                         AF.Ln, scale=1.0 / DI, bias=epscol[0:1, 0:1])
                    nc.scalar.drain()
                    i = nc.scalar.activation(rstd_rf[0:1, csl], rstd_rf[0:1, csl],
                                             AF.Exp, scale=-0.5)
                    inc(i, "ac")
                RSTD_AC = cnt["ac"]
                rstd_prev_ac = RSTD_AC
                @block.vector
                def _(vector):
                    w(vector, "ac", RSTD_AC)
                    i = nc.vector.tensor_copy(rstd_rb[0:1, csl], rstd_rf[0:1, csl])
                    inc(i, "ve")
                RSTDB_VE = cnt["ve"]
                @block.tensor
                def _(tensor):
                    w(tensor, "ve", RSTDB_VE)
                    w(tensor, "ve", psc_free)
                    i = nc.tensor.matmul(p_sc[:, 0:256], ones_r128b[0:1, :],
                                         rstd_rb[0:1, csl], start=True, stop=True)
                    inc(i, "pe")
                RR_PE = cnt["pe"]
                @block.vector
                def _(vector):
                    w(vector, "pe", RR_PE)
                    i = nc.vector.tensor_copy(rep_rstd[:, csl], p_sc[:, 0:256])
                    inc(i, "ve")
                psc_free = cnt["ve"]

            if p2stage <= 5:
                continue
            @block.vector
            def _(vector):
                w(vector, "ve")
                for h in range(H):
                    i = nc.vector.scalar_tensor_tensor(
                        gbuf[:, h * TILE:(h + 1) * TILE],
                        gbuf[:, h * TILE:(h + 1) * TILE],
                        rmscol[:, h:h + 1], rep_rstd[:],
                        ALU.mult, ALU.mult)
                inc(i, "ve")
            GQ_VE = cnt["ve"]

            pop_free = {0: psc_free, 1: None}
            pop_sem = {0: "ve", 1: "ac"}
            for mt in range(4):
                @block.sync
                def _(sync):
                    if pst_done[mt % 2] is not None:
                        w(sync, "po", pst_done[mt % 2][1])
                    i = sync.dma_start(out=ub2[mt % 2],
                                       in_=u_view[trow(t) + mt])
                    inc(i, "sy", 16)
                    sync.wait_ge(sems["sy"], cnt["sy"])
                UB_SY = cnt["sy"]
                for ns in range(2):
                    pb = (p_sc, p_seg)[ns]
                    @block.tensor
                    def _(tensor):
                        w(tensor, "ve", GQ_VE)
                        if pop_free[ns] is not None:
                            if pop_sem[ns] == "ac":
                                w(tensor, "ac", pop_free[ns])
                            else:
                                w(tensor, "ve", pop_free[ns])
                        elif ns == 1:
                            w(tensor, "ac")
                        for h in range(H):
                            i = nc.tensor.matmul(
                                pb[:, 0:512],
                                gbuf[:, h * TILE + mt * 128:
                                     h * TILE + (mt + 1) * 128],
                                wot[:, h * 1024 + ns * 512:
                                    h * 1024 + (ns + 1) * 512],
                                start=(h == 0), stop=(h == H - 1))
                        inc(i, "pe")
                    OP_PE = cnt["pe"]
                    @block.vector
                    def _(vector):
                        w(vector, "pe", OP_PE)
                        w(vector, "sy", UB_SY)
                        if ns == 0 and pst_done[mt % 2] is not None:
                            w(vector, "po", pst_done[mt % 2][1])
                        i = nc.vector.tensor_add(
                            pstg[mt % 2][:, ns * 512:(ns + 1) * 512],
                            pb[:, 0:512],
                            ub2[mt % 2][:, ns * 512:(ns + 1) * 512])
                        inc(i, "ve")
                    pop_free[ns] = cnt["ve"]
                    pop_sem[ns] = "ve"
                PS_VE = cnt["ve"]
                @block.gpsimd
                def _(gpsimd):
                    w(gpsimd, "ve", PS_VE)
                    i = gpsimd.dma_start(
                        out=part_s.rearrange("(n p) d -> n p d", p=128)[trow(t) + mt],
                        in_=pstg[mt % 2][:])
                    inc(i, "po", 16)
                    gpsimd.wait_ge(sems["po"], cnt["po"])
                pst_done[mt % 2] = (PS_VE, cnt["po"])
            tile_res_ve = cnt["ve"]

        @block.gpsimd
        def _(gpsimd):
            w(gpsimd, "po")

    barrier()
    if maxphase <= 2:
        ctx.close()
        return nc

    with ExitStack() as stk, \
         nc.psum_tensor("p_g", [128, 512], F32) as p_g, \
         nc.psum_tensor("p_u", [128, 512], F32) as p_u, \
         nc.psum_tensor("ptrA3", [128, 512], BF16) as ptrA3, \
         nc.psum_tensor("ptrB3", [128, 512], BF16) as ptrB3:
        sb3 = lambda name, shape, dt=F32: stk.enter_context(nc.sbuf_tensor(name, shape, dt))
        wgt = sb3("wgt", [128, KD * FFI], BF16)
        wut = sb3("wut", [128, KD * FFI], BF16)
        pstk3 = ExitStack()
        sb3p = lambda name, shape, dt=F32: pstk3.enter_context(
            nc.sbuf_tensor(name, shape, dt))
        wld3 = [sb3p(f"wld3{i}", [128, D]) for i in range(2)]
        wbf3 = sb3p("wbf3", [128, D], BF16)
        qw3 = quant_scratch(pstk3, "qw3", 1024)

        @block.scalar
        def _(scalar):
            load_act_set(SET_SILU)
        emit_wquant_prep(wg_h, FFI, D, wld3, qw3, wbf3, ptrA3, ptrB3, wgt, 1024)
        emit_wquant_prep(wu_h, FFI, D, wld3, qw3, wbf3, ptrA3, ptrB3, wut, 1024)
        WQ_VE = cnt["ve"]
        pstk3.close()

        q3 = quant_scratch(stk, "q3", 1024, f16=True)
        xq = [sb3(f"xq{i}", [128, KD * TILE], BF16) for i in range(2)]
        gs = sb3("gs", [128, 512], BF16)
        hb = sb3("hb", [128, FFI])
        hqb = sb3("hqb", [128, FFI], BF16)
        hqTb = sb3("hqTb", [128, KFI * TILE], BF16)

        xq_done = [None, None]
        for t0 in range(2):
            @block.sync
            def _(sync):
                w(sync, "ve", WQ_VE)
                i = sync.dma_start(
                    out=xq[t0][:].rearrange("p (k n) -> p k n", k=KD),
                    in_=xnqT_s.rearrange("(t k p) n -> t p k n", t=NT, k=KD)[t0])
                xq_done[t0] = inc(i, "sy", 16)
                sync.wait_ge(sems["sy"], cnt["sy"])

        hqT_done = None
        pg_free = None
        pu_free = None
        for t in range(min(ntl, NT)):
            xqb = xq[t % 2]
            for mt in range(4):
                for ii, (i0, i1) in enumerate(ISLS):
                    nn = i1 - i0
                    @block.tensor
                    def _(tensor):
                        if t == 0 and mt == 0 and ii == 0:
                            w(tensor, "ve", WQ_VE)
                            w(tensor, "ac")
                        w(tensor, "sy", xq_done[t % 2])
                        if pg_free is not None:
                            w(tensor, "ac", pg_free)
                        for k in range(KD):
                            i = nc.tensor.matmul(
                                p_g[:, 0:nn],
                                xqb[:, k * TILE + mt * 128: k * TILE + (mt + 1) * 128],
                                wgt[:, k * FFI + i0: k * FFI + i1],
                                start=(k == 0), stop=(k == KD - 1))
                        inc(i, "pe")
                        if pu_free is not None:
                            w(tensor, "ve", pu_free)
                        for k in range(KD):
                            i = nc.tensor.matmul(
                                p_u[:, 0:nn],
                                xqb[:, k * TILE + mt * 128: k * TILE + (mt + 1) * 128],
                                wut[:, k * FFI + i0: k * FFI + i1],
                                start=(k == 0), stop=(k == KD - 1))
                        inc(i, "pe")
                    MM_PE = cnt["pe"]
                    @block.scalar
                    def _(scalar):
                        w(scalar, "pe", MM_PE - 1)
                        if pu_free is not None:
                            w(scalar, "ve", pu_free)
                        i = nc.scalar.activation(gs[:, 0:nn], p_g[:, 0:nn], AF.Silu)
                        inc(i, "ac")
                    GS_AC = cnt["ac"]
                    pg_free = GS_AC
                    @block.vector
                    def _(vector):
                        w(vector, "pe", MM_PE)
                        w(vector, "ac", GS_AC)
                        i = nc.vector.tensor_mul(hb[:, i0:i1], p_u[:, 0:nn],
                                                 gs[:, 0:nn])
                        inc(i, "ve")
                    HB_VE = cnt["ve"]
                    pu_free = HB_VE
                    if ii % 2 == 1:
                        lo = 1024 * (ii // 2)
                        emit_quant_full(hb[:, lo:i1], hqb[:, lo:i1], i1 - lo,
                                        q3, first_wait=("ve", HB_VE))
                HQ_VE = cnt["ve"]

                pe_extra = [("ve", HQ_VE)]
                if hqT_done is not None and mt == 0:
                    pe_extra.append(("po", hqT_done))
                blocks = [(hqb[:, k * 128:(k + 1) * 128],
                           hqTb[:, k * TILE + mt * 128: k * TILE + (mt + 1) * 128])
                          for k in range(KFI)]
                emit_transpose(blocks, ptrA3, ptrB3, identb, pe_waits=pe_extra)
            HQT_VE = cnt["ve"]

            if t + 2 < NT:
                @block.sync
                def _(sync):
                    w(sync, "pe")
                    i = sync.dma_start(
                        out=xq[t % 2][:].rearrange("p (k n) -> p k n", k=KD),
                        in_=xnqT_s.rearrange("(t k p) n -> t p k n",
                                             t=NT, k=KD)[t + 2])
                    xq_done[t % 2] = inc(i, "sy", 16)
                    sync.wait_ge(sems["sy"], cnt["sy"])

            @block.gpsimd
            def _(gpsimd):
                w(gpsimd, "ve", HQT_VE)
                i = gpsimd.dma_start(
                    out=hqT_s.rearrange("(t k p) n -> t p k n", t=NT, k=KFI)[t],
                    in_=hqTb[:].rearrange("p (k n) -> p k n", k=KFI))
                inc(i, "po", 16)
                gpsimd.wait_ge(sems["po"], cnt["po"])
            hqT_done = cnt["po"]

    barrier()
    if maxphase <= 3:
        ctx.close()
        return nc

    with ExitStack() as stk, \
         nc.psum_tensor("p_d0", [128, 512], F32) as p_d0, \
         nc.psum_tensor("p_d1", [128, 512], F32) as p_d1, \
         nc.psum_tensor("ptrA4", [128, 512], BF16) as ptrA4, \
         nc.psum_tensor("ptrB4", [128, 512], BF16) as ptrB4:
        sb4 = lambda name, shape, dt=F32: stk.enter_context(nc.sbuf_tensor(name, shape, dt))
        wdt = sb4("wdt", [128, KFI * D], BF16)
        wld4 = [sb4(f"wld4{i}", [128, FFI]) for i in range(2)]
        wbf4 = sb4("wbf4", [128, FFI], BF16)
        q4 = quant_scratch(stk, "q4", 1024)

        @block.scalar
        def _(scalar):
            load_act_set(SET_SILU)
        emit_wquant_prep(wd_h, D, FFI, wld4, q4, wbf4, ptrA4, ptrB4, wdt, 1024)
        WD_VE = cnt["ve"]

        hqt = [sb4(f"hqt{i}", [128, KFI * TILE], BF16) for i in range(2)]
        pld = sb4("pld", [128, 4 * D])
        ostg = sb4("ostg", [128, 4 * D])

        hq_done = [None, None]
        for t0 in range(2):
            @block.sync
            def _(sync):
                i = sync.dma_start(
                    out=hqt[t0][:].rearrange("p (k n) -> p k n", k=KFI),
                    in_=hqT_s.rearrange("(t k p) n -> t p k n", t=NT, k=KFI)[t0])
                hq_done[t0] = inc(i, "sy", 16)
                sync.wait_ge(sems["sy"], cnt["sy"])

        ost_po = None
        pd_free = [None, None]
        for t in range(min(ntl, NT)):
            hq_b = hqt[t % 2]
            @block.sync
            def _(sync):
                w(sync, "ve")
                i = sync.dma_start(
                    out=pld[:].rearrange("p (n d) -> p n d", n=4),
                    in_=part_s.rearrange("(n p) d -> n p d", p=128)
                    [trow(t):trow(t) + 4].rearrange("n p d -> p n d"))
                inc(i, "sy", 16)
                sync.wait_ge(sems["sy"], cnt["sy"])
            PLD_SY = cnt["sy"]

            for mt in range(4):
                for ns in range(2):
                    pb = (p_d0, p_d1)[ns]
                    @block.tensor
                    def _(tensor):
                        if t == 0 and mt == 0 and ns == 0:
                            w(tensor, "ve", WD_VE)
                        w(tensor, "sy", hq_done[t % 2])
                        if pd_free[ns] is not None:
                            w(tensor, "ve", pd_free[ns])
                        for k in range(KFI):
                            i = nc.tensor.matmul(
                                pb[:, 0:512],
                                hq_b[:, k * TILE + mt * 128:
                                     k * TILE + (mt + 1) * 128],
                                wdt[:, k * D + ns * 512: k * D + (ns + 1) * 512],
                                start=(k == 0), stop=(k == KFI - 1))
                        inc(i, "pe")
                    DN_PE = cnt["pe"]
                    @block.vector
                    def _(vector):
                        w(vector, "pe", DN_PE)
                        w(vector, "sy", PLD_SY)
                        if ost_po is not None and mt == 0 and ns == 0:
                            w(vector, "po", ost_po)
                        i = nc.vector.tensor_add(
                            ostg[:, mt * D + ns * 512: mt * D + (ns + 1) * 512],
                            pb[:, 0:512],
                            pld[:, mt * D + ns * 512: mt * D + (ns + 1) * 512])
                        inc(i, "ve")
                    pd_free[ns] = cnt["ve"]
            OST_VE = cnt["ve"]

            if t + 2 < NT:
                @block.sync
                def _(sync):
                    w(sync, "pe")
                    i = sync.dma_start(
                        out=hqt[t % 2][:].rearrange("p (k n) -> p k n", k=KFI),
                        in_=hqT_s.rearrange("(t k p) n -> t p k n",
                                            t=NT, k=KFI)[t + 2])
                    hq_done[t % 2] = inc(i, "sy", 16)
                    sync.wait_ge(sems["sy"], cnt["sy"])

            @block.gpsimd
            def _(gpsimd):
                w(gpsimd, "ve", OST_VE)
                i = gpsimd.dma_start(
                    out=out_h.rearrange("(n p) d -> n p d", p=128)
                    [trow(t):trow(t) + 4].rearrange("n p d -> p n d"),
                    in_=ostg[:].rearrange("p (n d) -> p n d", n=4))
                inc(i, "po", 16)
                gpsimd.wait_ge(sems["po"], cnt["po"])
            ost_po = cnt["po"]

        @block.gpsimd
        def _(gpsimd):
            w(gpsimd, "po")

    ctx.close()
    return nc


_HOST_CONSTS = None


def _host_consts():
    global _HOST_CONSTS
    if _HOST_CONSTS is None:
        identb = np.eye(128, dtype=BFD)
        identf = np.eye(128, dtype=np.float32)
        jj = np.arange(128)[:, None]
        ii = np.arange(256)[None, :]
        m0 = (ii >= jj).astype(BFD)
        m1 = (ii >= jj + 128).astype(BFD)
        _HOST_CONSTS = (identb, identf, m0, m1)
    return _HOST_CONSTS


def make_in_maps(inputs):
    identb, identf, m0, m1 = _host_consts()
    base = {
        "in_proj_w": np.ascontiguousarray(inputs["L0_attn_in_proj_w"]),
        "conv_w": np.ascontiguousarray(inputs["L0_attn_conv_w"]),
        "conv_b": np.ascontiguousarray(inputs["L0_attn_conv_b"]),
        "dt_bias": np.ascontiguousarray(inputs["L0_attn_dt_bias"]),
        "A_log": np.ascontiguousarray(inputs["L0_attn_A_log"]),
        "D_h": np.ascontiguousarray(inputs["L0_attn_D"]),
        "x_norm_w": np.ascontiguousarray(inputs["L0_attn_x_norm_w"]),
        "x_norm_b": np.ascontiguousarray(inputs["L0_attn_x_norm_b"]),
        "rms_w": np.ascontiguousarray(inputs["L0_attn_rms_w"]),
        "out_proj_w": np.ascontiguousarray(inputs["L0_attn_out_proj_w"]),
        "w_gate": np.ascontiguousarray(inputs["L0_ffn_w_gate"]),
        "w_up": np.ascontiguousarray(inputs["L0_ffn_w_up"]),
        "w_down": np.ascontiguousarray(inputs["L0_ffn_w_down"]),
        "ident_b": identb, "ident_f": identf,
        "mask_j0": m0, "mask_j1": m1,
    }
    in_maps = []
    u = inputs["L0_attn_u"]
    for c in range(8):
        m = dict(base)
        m["u"] = np.ascontiguousarray(u[c * SEQ:(c + 1) * SEQ].reshape(TOK, D))
        in_maps.append(m)
    return in_maps


def kernel(**inputs):
    nc = build_nc()
    in_maps = make_in_maps(inputs)
    res = run_bass_kernel_spmd(nc, in_maps, core_ids=list(range(8)))
    out = np.empty((B, L, D), dtype=np.float32)
    for c in range(8):
        out[c * SEQ:(c + 1) * SEQ] = res.results[c]["out"].reshape(SEQ, L, D)
    return out
